# version 01
# baseline (speedup 1.0000x reference)
import numpy as np
import ml_dtypes
import concourse.bass as bass
import concourse.mybir as mybir
from concourse.bass_utils import run_bass_kernel_spmd
from contextlib import ExitStack
from collections import defaultdict

F32 = mybir.dt.float32
BF16 = mybir.dt.bfloat16
AF = mybir.ActivationFunctionType
ALU = mybir.AluOpType
AX = mybir.AxisListType
bf16 = ml_dtypes.bfloat16

B, N, D = 4, 8192, 1024
HALF = N // 2
NSLAB = 8
SLAB = 1024
QB = 4
NB3 = 8
BLK3 = 512
C_ATTN = 24.0


class Buf:
    __slots__ = ("writer", "readers", "name")

    def __init__(self, name=""):
        self.writer = None
        self.readers = []
        self.name = name


class Ctx:
    ENGS = ("tensor", "vector", "scalar", "gpsimd", "sync")
    QUEUES = ("qS", "qA", "qG")

    def __init__(self, nc, es):
        self.nc = nc
        self.sem = {}
        for n in self.ENGS + self.QUEUES:
            self.sem[n] = es.enter_context(nc.semaphore("sem_" + n))
        self.cnt = defaultdict(int)
        self.waited = defaultdict(dict)

    def _wait(self, eng, src, val):
        if src == eng:
            return
        assert val <= self.cnt[src], ("wait-ahead deadlock", eng, src, val, self.cnt[src])
        w = self.waited[eng]
        if val > w.get(src, 0):
            getattr(self.nc, eng).wait_ge(self.sem[src], val)
            w[src] = val

    def _deps(self, eng, reads, writes):
        need = {}
        for b in reads:
            if b.writer:
                s, v = b.writer
                if need.get(s, 0) < v:
                    need[s] = v
        for b in writes:
            if b.writer:
                s, v = b.writer
                if need.get(s, 0) < v:
                    need[s] = v
            for s, v in b.readers:
                if need.get(s, 0) < v:
                    need[s] = v
        for s, v in need.items():
            self._wait(eng, s, v)

    def _finish(self, tag, c, reads, writes):
        for b in reads:
            b.readers.append((tag, c))
        for b in writes:
            b.writer = (tag, c)
            b.readers = []

    def _maybe_drain(self, eng, reads):
        if eng in ("vector", "scalar", "gpsimd"):
            c = self.cnt[eng]
            for b in reads:
                if b.writer and b.writer[0] == eng and b.writer[1] >= c - 1:
                    getattr(self.nc, eng).drain()
                    return

    def op(self, eng, fn, reads=(), writes=()):
        self._deps(eng, reads, writes)
        self._maybe_drain(eng, reads)
        ins = fn(getattr(self.nc, eng))
        self.cnt[eng] += 1
        ins.then_inc(self.sem[eng], 1)
        self._finish(eng, self.cnt[eng], reads, writes)

    def ops(self, eng, fns, reads=(), writes=()):
        self._deps(eng, reads, writes)
        self._maybe_drain(eng, reads)
        e = getattr(self.nc, eng)
        ins = None
        for fn in fns:
            ins = fn(e)
        self.cnt[eng] += 1
        ins.then_inc(self.sem[eng], 1)
        self._finish(eng, self.cnt[eng], reads, writes)

    def dma(self, eng, queue, out, in_, reads=(), writes=(), transpose=False):
        self._deps(eng, reads, writes)
        getattr(self.nc, eng).dma_start(out=out, in_=in_, transpose=transpose) \
            .then_inc(self.sem[queue], 16)
        self.cnt[queue] += 16
        self._finish(queue, self.cnt[queue], reads, writes)

    def barrier_queue(self, eng, queue):
        self._wait(eng, queue, self.cnt[queue])

    def barrier_all(self):
        snap = {n: self.cnt[n] for n in self.ENGS + self.QUEUES}
        for e in self.ENGS:
            for n, v in snap.items():
                if n != e and v > 0:
                    self._wait(e, n, v)


def _host_consts():
    ident = np.eye(128, dtype=np.float32).astype(bf16)
    onesb = np.ones((128, 136), dtype=np.float32).astype(bf16)
    cbf = np.concatenate([ident, onesb], axis=1)
    selm = np.zeros((6, 600), dtype=np.float32)
    for e in range(6):
        selm[e, e * 100:(e + 1) * 100] = 1.0
    eind = np.zeros((100, 36), dtype=np.float32)
    for e in range(6):
        eind[:, e * 6 + e] = 1.0
    return cbf, selm.astype(np.float32), eind.astype(bf16)


def emit_ln(ctx, bf_, xt, xbuf, sbufs, tag):
    st6, st2, tmp4, tmp1, mean2, mcol, rcol, cf32 = sbufs
    V, S = "vector", "scalar"
    bST6, bSM, bSCV, bT4, bSM2 = bf_(tag + "st6"), bf_(tag + "sm"), bf_(tag + "scv"), bf_(tag + "t4"), bf_(tag + "sm2")
    bMEAN, bM2, bVAR, bRC, bMC = bf_(tag + "mean"), bf_(tag + "m2"), bf_(tag + "var"), bf_(tag + "rc"), bf_(tag + "mc")
    ctx.op(V, lambda e: e.bn_stats(st6[:, 0:6], xt[:, 0:512]), reads=[xbuf], writes=[bST6])
    ctx.op(V, lambda e: e.bn_stats(st6[:, 6:12], xt[:, 512:1024]), reads=[xbuf], writes=[bST6])
    ctx.op(V, lambda e: e.reduce_sum(st2[:, 0:1], st6[:, 1:12:3], axis=AX.X), reads=[bST6], writes=[bSM])
    ctx.op(V, lambda e: e.reduce_sum(st2[:, 1:2], st6[:, 2:12:3], axis=AX.X), reads=[bST6], writes=[bSCV])
    ctx.op(V, lambda e: e.tensor_tensor(tmp4[:], st6[:, 1:12:3], st6[:, 1:12:3], ALU.mult),
           reads=[bST6], writes=[bT4])
    ctx.op(V, lambda e: e.reduce_sum(tmp1[:], tmp4[:], axis=AX.X), reads=[bT4], writes=[bSM2])
    ctx.op(V, lambda e: e.tensor_scalar(st2[:, 0:1], st2[:, 0:1], 0.25, None, ALU.mult),
           reads=[bSM], writes=[bMEAN])
    ctx.op(V, lambda e: e.tensor_tensor(mean2[:], st2[:, 0:1], st2[:, 0:1], ALU.mult),
           reads=[bMEAN], writes=[bM2])
    ctx.op(V, lambda e: e.tensor_scalar(st2[:, 1:2], st2[:, 1:2], 1.0 / 1024.0, None, ALU.mult),
           reads=[bSCV], writes=[bVAR])
    ctx.op(V, lambda e: e.scalar_tensor_tensor(st2[:, 1:2], tmp1[:], 0.25, st2[:, 1:2], ALU.mult, ALU.add),
           reads=[bSM2, bVAR], writes=[bVAR])
    ctx.op(V, lambda e: e.tensor_tensor(st2[:, 1:2], st2[:, 1:2], mean2[:], ALU.subtract),
           reads=[bVAR, bM2], writes=[bVAR])
    ctx.op(S, lambda e: e.activation(rcol[:], st2[:, 1:2], AF.Ln, bias=cf32[:, 1:2], scale=1.0),
           reads=[bVAR, bf_("sb_cf32_s")], writes=[bRC])
    ctx.op(S, lambda e: e.activation(rcol[:], rcol[:], AF.Exp, bias=cf32[:, 2:3], scale=-0.5),
           reads=[bRC, bf_("sb_cf32_s")], writes=[bRC])
    ctx.op(V, lambda e: e.tensor_scalar(mcol[:], st2[:, 0:1], rcol[:], -1.0, ALU.mult, op1=ALU.mult),
           reads=[bMEAN, bRC], writes=[bMC])
    return bRC, bMC


def split_pa(ap, p=128):
    return ap.rearrange("(a p) d -> a p d", p=p)


def build(upto='p3', taps=()):
    nc = bass.Bass("TRN2", target_bir_lowering=False)
    dt = {}

    def din(name, shape, dtype):
        dt[name] = nc.dram_tensor(name, list(shape), dtype, kind="ExternalInput").ap()
        return dt[name]

    def dout(name, shape, dtype):
        dt[name] = nc.dram_tensor(name, list(shape), dtype, kind="ExternalOutput").ap()
        return dt[name]

    def dint(name, shape, dtype):
        dt[name] = nc.dram_tensor(name, list(shape), dtype, kind="Internal").ap()
        return dt[name]

    x_seq = din("x_seq", (N, D), F32)
    w_qkv = din("w_qkv", (3072, D), F32)
    bcol_d = din("bcol", (128, 16), F32)
    a_in = din("a_in", (6, 100, 16), F32)
    b_in = din("b_in", (6, 16, D), F32)
    wgate = din("wgate", (D, 6), F32)
    wt_w = din("wt_w", (D, D), F32)
    wd_w = din("wd_w", (D, D), F32)
    rows_d = din("rows_bf", (1, 3072), BF16)
    cbf_d = din("cbf", (128, 264), BF16)
    selm_d = din("selm", (6, 600), F32)
    eind_d = din("eind", (100, 36), BF16)
    cf32_d = din("cf32", (128, 132), F32)
    out_d = dout("out", (HALF, D), F32)

    t_sp = dint("t_sp", (N, D), BF16)
    t2_sp = dint("t2_sp", (HALF, D), BF16)
    x1_sp = dint("x1_sp", (HALF, D), F32)
    k_sp = dint("k_sp", (NSLAB, 8, 128, SLAB), BF16)
    v_sp = dint("v_sp", (NSLAB, 8, 128, SLAB), BF16)
    q_sp = dint("q_sp", (QB, 8, 128, SLAB), BF16)
    wq_bf = dint("wq_bf", (3072, D), BF16)
    wt_bf = dint("wt_bf", (D, D), BF16)
    wd_bf = dint("wd_bf", (D, D), BF16)
    tok_sp = dint("tok_sp", (608, D), BF16)

    tap_d = {}
    for name in taps:
        t = dt[name].tensor
        tap_d[name] = dout("tap_" + name, list(t.shape), t.dtype)

    bufs = {}

    def bf_(name):
        if name not in bufs:
            bufs[name] = Buf(name)
        return bufs[name]

    es_prog = ExitStack()
    with es_prog:
        ctx = Ctx(nc, es_prog)
        blk = es_prog.enter_context(nc.Block())
        pers = ExitStack()
        es_prog.enter_context(pers)

        def sb(name, shape, dtype, stack=None):
            return (stack or pers).enter_context(nc.sbuf_tensor(name, list(shape), dtype))

        cbf = sb("cbf_s", (128, 264), BF16)
        rows = sb("rows_s", (1, 3072), BF16)
        bcol = sb("bcol_s", (128, 16), F32)
        cf32 = sb("cf32_s", (128, 132), F32)
        selm = sb("selm_s", (6, 600), F32)
        eind = sb("eind_s", (100, 36), BF16)
        wg_fm = sb("wg_fm", (128, 48), BF16)
        tokfm = sb("tokfm", (128, 8 * 608), BF16)
        val_s = [sb(f"val{e}", (128, D), BF16) for e in range(6)]

        IDENT_F32 = cf32[:, 4:132]
        ONES_ROW_128 = cbf[0:1, 128:256]
        ONES_ROW_100 = cbf[0:1, 128:228]
        BIAS_NC = cf32[:, 0:1]
        BIAS_EPS = cf32[:, 1:2]
        BIAS_Z = cf32[:, 2:3]
        SCALE_COL = cf32[:, 3:4]

        psum = es_prog.enter_context(nc.psum_tensor("psA", [128, 4096], F32))
        pbank = [bf_(f"pbank{i}") for i in range(8)]

        def pslice(c0, c1):
            return psum[:, c0:c1], [pbank[i] for i in range(c0 // 512, (c1 + 511) // 512)]

        for dstt, src in ((cbf, cbf_d), (rows, rows_d), (bcol, bcol_d),
                          (cf32, cf32_d), (selm, selm_d), (eind, eind_d)):
            ctx.dma("sync", "qS", dstt[:], src, writes=[bf_("sb_" + dstt.name)])
        CB, RW, BC, CF, SM_, EI = [bufs["sb_" + t.name] for t in (cbf, rows, bcol, cf32, selm, eind)]

        ctx.dma("gpsimd", "qG", dt["wq_bf"][:], w_qkv, writes=[bf_("dr_wq_bf")])
        ctx.dma("gpsimd", "qG", dt["wt_bf"][:], wt_w, writes=[bf_("dr_wt_bf")])
        ctx.dma("gpsimd", "qG", dt["wd_bf"][:], wd_w, writes=[bf_("dr_wd_bf")])

        es0 = ExitStack()
        wg_f32 = sb("wg_f32", (128, 48), F32, es0)
        for cc in range(8):
            ctx.dma("sync", "qS", wg_f32[:, cc * 6:(cc + 1) * 6],
                    wgate[cc * 128:(cc + 1) * 128, :], writes=[bf_("sb_wg_f32")])
        ctx.op("vector", lambda e: e.tensor_copy(wg_fm[:], wg_f32[:]),
               reads=[bf_("sb_wg_f32")], writes=[bf_("sb_wg_fm")])

        ab_f32 = sb("ab_f32", (128, 1040), F32, es0)
        ab_bf = sb("ab_bf", (128, 1040), BF16, es0)
        aT = [sb(f"aT{e}", (16, 100), BF16, es0) for e in range(6)]
        tokstage = sb("tokstage", (128, 1024), BF16, es0)

        ctx.op("vector", lambda e: e.memset(ab_f32[:], 0.0), writes=[bf_("sb_ab_f32")])
        for e in range(6):
            ctx.dma("sync", "qS", ab_f32[0:100, 0:16], a_in[e], writes=[bf_("sb_ab_f32")])
            ctx.dma("sync", "qS", ab_f32[0:16, 16:1040], b_in[e], writes=[bf_("sb_ab_f32")])
            ctx.op("vector", lambda e_: e_.tensor_copy(ab_bf[:], ab_f32[:]),
                   reads=[bf_("sb_ab_f32")], writes=[bf_("sb_ab_bf")])
            pA, pA_b = pslice(0, 512)
            ctx.op("tensor", lambda e_: e_.transpose(pA[0:16, 0:100], ab_f32[0:100, 0:16],
                                                     IDENT_F32[0:100, 0:100]),
                   reads=[bf_("sb_ab_f32"), CF], writes=pA_b)
            ctx.op("vector", lambda e_, e0=e: e_.tensor_copy(aT[e0][:], pA[0:16, 0:100]),
                   reads=pA_b, writes=[bf_(f"sb_aT{e}")])
            pT, pT_b = pslice(1024, 2048)
            ctx.ops("tensor",
                    [(lambda e_, j=j, e0=e: e_.matmul(pT[0:100, j * 512:(j + 1) * 512],
                                                      aT[e0][:], ab_bf[0:16, 16 + j * 512:16 + (j + 1) * 512],
                                                      start=True, stop=True))
                     for j in range(2)],
                    reads=[bf_(f"sb_aT{e}"), bf_("sb_ab_bf")], writes=pT_b)
            ctx.op("vector", lambda e_: e_.tensor_copy(tokstage[0:100, :], pT[0:100, :]),
                   reads=pT_b, writes=[bf_("sb_tokstage")])
            ctx.dma("gpsimd", "qG", dt["tok_sp"][e * 100:(e + 1) * 100, :], tokstage[0:100, :],
                    reads=[bf_("sb_tokstage")], writes=[bf_("dr_tok_sp")])
        ctx.op("vector", lambda e_: e_.memset(tokstage[0:8, :], 0.0), writes=[bf_("sb_tokstage")])
        ctx.dma("gpsimd", "qG", dt["tok_sp"][600:608, :], tokstage[0:8, :],
                reads=[bf_("sb_tokstage")], writes=[bf_("dr_tok_sp")])
        for cc in range(8):
            ctx.dma("scalar", "qA", tokfm[:, cc * 608:(cc + 1) * 608],
                    dt["tok_sp"][0:608, cc * 128:(cc + 1) * 128],
                    reads=[bf_("dr_tok_sp")], writes=[bf_("sb_tokfm")], transpose=True)

        es0.close()
        with ExitStack() as es_val:
            wtT = sb("wtT", (128, 8 * 1024), BF16, es_val)
            for cc in range(8):
                ctx.dma("scalar", "qA", wtT[:, cc * 1024:(cc + 1) * 1024],
                        dt["wt_bf"][0:1024, cc * 128:(cc + 1) * 128],
                        reads=[bf_("dr_wt_bf")], writes=[bf_("sb_wtT")], transpose=True)
            for e in range(6):
                pV, pV_b = pslice(0, 1024)
                mms = []
                for cc in range(8):
                    for j in range(2):
                        mms.append(lambda e_, cc=cc, j=j, e0=e: e_.matmul(
                            pV[0:100, j * 512:(j + 1) * 512],
                            tokfm[:, cc * 608 + e0 * 100: cc * 608 + (e0 + 1) * 100],
                            wtT[:, cc * 1024 + j * 512: cc * 1024 + (j + 1) * 512],
                            start=(cc == 0), stop=False, skip_group_check=True))
                for j in range(2):
                    mms.append(lambda e_, j=j: e_.matmul(
                        pV[0:100, j * 512:(j + 1) * 512],
                        ONES_ROW_100, rows[0:1, j * 512:(j + 1) * 512],
                        start=False, stop=True, skip_group_check=True))
                ctx.ops("tensor", mms, reads=[bf_("sb_tokfm"), bf_("sb_wtT"), CB, RW], writes=pV_b)
                ctx.op("vector", lambda e_, e0=e: e_.tensor_copy(val_s[e0][0:100, :], pV[0:100, :]),
                       reads=pV_b, writes=[bf_(f"sb_val{e}")])
                ctx.op("vector", lambda e_, e0=e: e_.memset(val_s[e0][0:1, :], 0.0),
                       writes=[bf_(f"sb_val{e}")])

        ctx.barrier_all()
        if upto in ("p1", "p2", "p3"):
         with ExitStack() as es1:
            wTall = sb("wTall", (128, 8 * 3072), BF16, es1)
            xblk = [sb(f"xblk{i}", (128, 2 * 1024), F32, es1) for i in range(2)]
            tstage = sb("tstage", (128, 8 * 1024), BF16, es1)
            tT = [sb(f"tT{i}", (128, 8 * 1024), BF16, es1) for i in range(2)]
            kstage = sb("kstage", (128, 8 * 1024), BF16, es1)
            vstage = sb("vstage", (128, 8 * 1024), BF16, es1)
            qstage = sb("qstage", (128, 8 * 1024), BF16, es1)
            st6 = sb("st6", (128, 12), F32, es1)
            st2 = sb("st2", (128, 2), F32, es1)
            mcol = sb("mcol", (128, 1), F32, es1)
            rcol = sb("rcol", (128, 1), F32, es1)
            tmp4a = sb("tmp4a", (128, 4), F32, es1)
            tmp1a = sb("tmp1a", (128, 1), F32, es1)
            mean2a = sb("mean2a", (128, 1), F32, es1)

            for dc in range(8):
                ctx.dma("scalar", "qA", wTall[:, dc * 3072:(dc + 1) * 3072],
                        dt["wq_bf"][0:3072, dc * 128:(dc + 1) * 128],
                        reads=[bf_("dr_wq_bf")], writes=[bf_("sb_wTall")], transpose=True)

            def WT(dc, c0, c1):
                return wTall[:, dc * 3072 + c0: dc * 3072 + c1]

            x3d = split_pa(x_seq)
            rot = 0
            for s in range(NSLAB):
                xb = [bf_("b_xblk0"), bf_("b_xblk1")]
                for q2 in range(4):
                    par2 = q2 % 2
                    ctx.dma("sync", "qS",
                            xblk[par2][:].rearrange("p (a d) -> p a d", d=1024),
                            x3d[8 * s + 2 * q2: 8 * s + 2 * q2 + 2].rearrange("a p d -> p a d"),
                            reads=[bf_("dr_x_seq")], writes=[xb[par2]])
                    for th in range(2):
                        t8 = 2 * q2 + th
                        xt = xblk[par2][:, th * 1024:(th + 1) * 1024]
                        xbuf = xb[par2]
                        bRC, bMC = emit_ln(ctx, bf_, xt, xbuf,
                                           (st6, st2, tmp4a, tmp1a, mean2a, mcol, rcol, cf32), "ln1_")
                        ctx.op("scalar", lambda e_, xt=xt, t8=t8: e_.activation(
                            tstage[:, t8 * 1024:(t8 + 1) * 1024], xt, AF.Identity,
                            bias=mcol[:], scale=rcol[:]),
                            reads=[xbuf, bMC, bRC], writes=[bf_("sb_tstage")])
                t3d = split_pa(t_sp)
                ctx.dma("gpsimd", "qG",
                        t3d[s * 8:(s + 1) * 8].rearrange("a p d -> p a d"),
                        tstage[:].rearrange("p (a d) -> p a d", d=1024),
                        reads=[bf_("sb_tstage")], writes=[bf_(f"dr_t_sp{s}")])
                par = s % 2
                tTb = bf_(f"sb_tT{par}")
                for dc in range(8):
                    ctx.dma("scalar", "qA", tT[par][:, dc * 1024:(dc + 1) * 1024],
                            dt["t_sp"][s * 1024:(s + 1) * 1024, dc * 128:(dc + 1) * 128],
                            reads=[bf_(f"dr_t_sp{s}")], writes=[tTb], transpose=True)
                for ec in range(8):
                    pm, pm_b = pslice((rot % 2) * 1024, (rot % 2) * 1024 + 1024); rot += 1
                    mms = []
                    for dc in range(8):
                        for j in range(2):
                            mms.append(lambda e_, dc=dc, j=j, ec=ec, par=par: e_.matmul(
                                pm[:, j * 512:(j + 1) * 512],
                                WT(dc, (8 + ec) * 128, (9 + ec) * 128),
                                tT[par][:, dc * 1024 + j * 512: dc * 1024 + (j + 1) * 512],
                                start=(dc == 0), stop=(dc == 7), skip_group_check=True))
                    ctx.ops("tensor", mms, reads=[bf_("sb_wTall"), tTb], writes=pm_b)
                    ctx.op("vector", lambda e_, ec=ec, pm=pm: e_.tensor_scalar(
                        kstage[:, ec * 1024:(ec + 1) * 1024], pm[:], bcol[:, 8 + ec:9 + ec], None, ALU.add),
                        reads=pm_b + [BC], writes=[bf_("sb_kstage")])
                ctx.dma("gpsimd", "qG", dt["k_sp"][s].rearrange("a p d -> p a d"),
                        kstage[:].rearrange("p (a d) -> p a d", d=1024),
                        reads=[bf_("sb_kstage")], writes=[bf_(f"dr_k_sp{s}")])
                for kt in range(8):
                    pm, pm_b = pslice((rot % 2) * 1024, (rot % 2) * 1024 + 1024); rot += 1
                    mms = []
                    for dc in range(8):
                        for j in range(2):
                            mms.append(lambda e_, dc=dc, j=j, kt=kt, par=par: e_.matmul(
                                pm[:, j * 512:(j + 1) * 512],
                                tT[par][:, dc * 1024 + kt * 128: dc * 1024 + (kt + 1) * 128],
                                WT(dc, 2048 + j * 512, 2048 + (j + 1) * 512),
                                start=(dc == 0), stop=False, skip_group_check=True))
                    for j in range(2):
                        mms.append(lambda e_, j=j: e_.matmul(
                            pm[:, j * 512:(j + 1) * 512],
                            ONES_ROW_128, rows[0:1, 2048 + j * 512: 2048 + (j + 1) * 512],
                            start=False, stop=True, skip_group_check=True))
                    ctx.ops("tensor", mms, reads=[bf_("sb_wTall"), tTb, CB, RW], writes=pm_b)
                    ctx.op("vector", lambda e_, kt=kt, pm=pm: e_.tensor_copy(
                        vstage[:, kt * 1024:(kt + 1) * 1024], pm[:]),
                        reads=pm_b, writes=[bf_("sb_vstage")])
                ctx.dma("gpsimd", "qG", dt["v_sp"][s].rearrange("a p d -> p a d"),
                        vstage[:].rearrange("p (a d) -> p a d", d=1024),
                        reads=[bf_("sb_vstage")], writes=[bf_(f"dr_v_sp{s}")])
                if s < QB:
                    for ec in range(8):
                        pm, pm_b = pslice((rot % 2) * 1024, (rot % 2) * 1024 + 1024); rot += 1
                        mms = []
                        for dc in range(8):
                            for j in range(2):
                                mms.append(lambda e_, dc=dc, j=j, ec=ec, par=par: e_.matmul(
                                    pm[:, j * 512:(j + 1) * 512],
                                    WT(dc, ec * 128, (ec + 1) * 128),
                                    tT[par][:, dc * 1024 + j * 512: dc * 1024 + (j + 1) * 512],
                                    start=(dc == 0), stop=(dc == 7), skip_group_check=True))
                        ctx.ops("tensor", mms, reads=[bf_("sb_wTall"), tTb], writes=pm_b)
                        ctx.op("vector", lambda e_, ec=ec, pm=pm: e_.tensor_scalar(
                            qstage[:, ec * 1024:(ec + 1) * 1024], pm[:], bcol[:, ec:ec + 1], None, ALU.add),
                            reads=pm_b + [BC], writes=[bf_("sb_qstage")])
                    ctx.dma("gpsimd", "qG", dt["q_sp"][s].rearrange("a p d -> p a d"),
                            qstage[:].rearrange("p (a d) -> p a d", d=1024),
                            reads=[bf_("sb_qstage")], writes=[bf_(f"dr_q_sp{s}")])

        ctx.barrier_all()
        if upto in ("p2", "p3"):
         with ExitStack() as es2:
            qbuf = sb("qbuf", (128, 8 * 1024), BF16, es2)
            kbuf = [sb(f"kbuf{i}", (128, 8 * 1024), BF16, es2) for i in range(2)]
            vbuf = [sb(f"vbuf{i}", (128, 8 * 1025), BF16, es2) for i in range(2)]
            pTt = sb("pTt", (128, 8 * 1024), BF16, es2)
            outsb = sb("outsb", (128, 8 * 1025), F32, es2)
            xown = sb("xown", (128, 8 * 1024), F32, es2)
            t2st = sb("t2st", (128, 4 * 1024), BF16, es2)
            st6b = sb("st6b", (128, 12), F32, es2)
            st2b = sb("st2b", (128, 2), F32, es2)
            mcol2 = sb("mcol2", (128, 1), F32, es2)
            rcol2 = sb("rcol2", (128, 1), F32, es2)
            rec = sb("rec", (128, 1), F32, es2)
            tmp4b = sb("tmp4b", (128, 4), F32, es2)
            t2f = sb("t2f", (128, 1024), F32, es2)
            tmp1b = sb("tmp1b", (128, 1), F32, es2)
            mean2b = sb("mean2b", (128, 1), F32, es2)

            for i in range(2):
                ctx.ops("vector",
                        [(lambda e_, i=i, t=t: e_.memset(vbuf[i][:, t * 1025 + 1024: t * 1025 + 1025], 1.0))
                         for t in range(8)],
                        writes=[bf_(f"sb_vbuf{i}_ones")])

            S_HALF = [pslice(0, 512), pslice(512, 1024)]
            PV = [pslice(1024, 2560), pslice(2560, 4096)]

            for qb in range(QB):
                qbB = bf_("sb_qbuf")
                ctx.dma("sync", "qS", qbuf[:].rearrange("p (a d) -> p a d", d=1024),
                        dt["q_sp"][qb].rearrange("a p d -> p a d"),
                        reads=[bf_(f"dr_q_sp{qb}")], writes=[qbB])
                ctx.dma("sync", "qS", xown[:].rearrange("p (a d) -> p a d", d=1024),
                        x3d[qb * 8:(qb + 1) * 8].rearrange("a p d -> p a d"),
                        reads=[bf_("dr_x_seq")], writes=[bf_("sb_xown")])
                for s in range(NSLAB):
                    par = s % 2
                    kB, vB = bf_(f"sb_kbuf{par}"), bf_(f"sb_vbuf{par}")
                    vOnes = bf_(f"sb_vbuf{par}_ones")
                    ctx.dma("sync", "qS", kbuf[par][:].rearrange("p (a d) -> p a d", d=1024),
                            dt["k_sp"][s].rearrange("a p d -> p a d"),
                            reads=[bf_(f"dr_k_sp{s}")], writes=[kB])
                    ctx.dma("sync", "qS",
                            vbuf[par][:].rearrange("p (a d) -> p a d", d=1025)[:, :, 0:1024],
                            dt["v_sp"][s].rearrange("a p d -> p a d"),
                            reads=[bf_(f"dr_v_sp{s}")], writes=[vB])
                    for kc in range(8):
                        for qh in range(2):
                            sp, sp_b = S_HALF[qh]
                            mms = []
                            for dc in range(8):
                                mms.append(lambda e_, dc=dc, qh=qh, kc=kc, par=par, sp=sp: e_.matmul(
                                    sp[:],
                                    kbuf[par][:, dc * 1024 + kc * 128: dc * 1024 + (kc + 1) * 128],
                                    qbuf[:, dc * 1024 + qh * 512: dc * 1024 + (qh + 1) * 512],
                                    start=(dc == 0), stop=(dc == 7), skip_group_check=True))
                            ctx.ops("tensor", mms, reads=[kB, qbB], writes=sp_b)
                            ctx.op("scalar", lambda e_, kc=kc, qh=qh, sp=sp: e_.activation(
                                pTt[:, kc * 1024 + qh * 512: kc * 1024 + (qh + 1) * 512], sp[:],
                                AF.Exp, bias=BIAS_NC, scale=1.0 / 32.0),
                                reads=sp_b + [CF], writes=[bf_(f"sb_pT{kc}")])
                    for j in range(8):
                        pv, pv_b = PV[j % 2]
                        mms = []
                        for kc in range(8):
                            for c0, c1 in ((0, 512), (512, 1024), (1024, 1025)):
                                mms.append(lambda e_, kc=kc, c0=c0, c1=c1, j=j, par=par: e_.matmul(
                                    pv[:, c0:c1],
                                    pTt[:, kc * 1024 + j * 128: kc * 1024 + (j + 1) * 128],
                                    vbuf[par][:, kc * 1025 + c0: kc * 1025 + c1],
                                    start=(kc == 0), stop=(kc == 7), skip_group_check=True))
                        ctx.ops("tensor", mms,
                                reads=[bf_(f"sb_pT{kc}") for kc in range(8)] + [vB, vOnes],
                                writes=pv_b)
                        if s == 0:
                            ctx.op("vector", lambda e_, j=j, pv=pv: e_.tensor_copy(
                                outsb[:, j * 1025:(j + 1) * 1025], pv[:, 0:1025]),
                                reads=pv_b, writes=[bf_(f"sb_outsb{j}")])
                        else:
                            ctx.op("vector", lambda e_, j=j, pv=pv: e_.tensor_tensor(
                                outsb[:, j * 1025:(j + 1) * 1025],
                                outsb[:, j * 1025:(j + 1) * 1025], pv[:, 0:1025], ALU.add),
                                reads=pv_b + [bf_(f"sb_outsb{j}")], writes=[bf_(f"sb_outsb{j}")])
                for j in range(8):
                    oj = bf_(f"sb_outsb{j}")
                    ctx.op("vector", lambda e_, j=j: e_.reciprocal(
                        rec[:], outsb[:, j * 1025 + 1024: j * 1025 + 1025]),
                        reads=[oj], writes=[bf_("b_rec")])
                    ctx.op("vector", lambda e_, j=j: e_.scalar_tensor_tensor(
                        xown[:, j * 1024:(j + 1) * 1024], outsb[:, j * 1025: j * 1025 + 1024],
                        rec[:], xown[:, j * 1024:(j + 1) * 1024], ALU.mult, ALU.add),
                        reads=[oj, bf_("b_rec"), bf_("sb_xown")], writes=[bf_("sb_xown")])
                    xt = xown[:, j * 1024:(j + 1) * 1024]
                    bRC, bMC = emit_ln(ctx, bf_, xt, bf_("sb_xown"),
                                       (st6b, st2b, tmp4b, tmp1b, mean2b, mcol2, rcol2, cf32), "ln2_")
                    ctx.op("scalar", lambda e_, xt=xt, j=j: e_.activation(
                        t2st[:, (j % 4) * 1024:(j % 4 + 1) * 1024], xt, AF.Identity,
                        bias=mcol2[:], scale=rcol2[:]),
                        reads=[bf_("sb_xown"), bMC, bRC], writes=[bf_("sb_t2st")])
                    ctx.op("vector", lambda e_, xt=xt: e_.tensor_scalar(
                        t2f[:], xt, rcol2[:], mcol2[:], ALU.mult, op1=ALU.add),
                        reads=[bf_("sb_xown"), bMC, bRC], writes=[bf_("b_t2f")])
                    ctx.op("vector", lambda e_, xt=xt: e_.tensor_tensor(
                        xt, xt, t2f[:], ALU.add),
                        reads=[bf_("sb_xown"), bf_("b_t2f")], writes=[bf_("sb_xown")])
                    if j % 4 == 3:
                        t23d = split_pa(t2_sp)
                        ctx.dma("gpsimd", "qG",
                                t23d[qb * 8 + (j - 3): qb * 8 + j + 1].rearrange("a p d -> p a d"),
                                t2st[:].rearrange("p (a d) -> p a d", d=1024),
                                reads=[bf_("sb_t2st")], writes=[bf_(f"dr_t2_sp{qb}")])
                x13d = split_pa(x1_sp)
                ctx.dma("gpsimd", "qG", x13d[qb * 8:(qb + 1) * 8].rearrange("a p d -> p a d"),
                        xown[:].rearrange("p (a d) -> p a d", d=1024),
                        reads=[bf_("sb_xown")], writes=[bf_(f"dr_x1_sp{qb}")])

        ctx.barrier_all()
        if upto == "p3":
         with ExitStack() as es3:
            wdT = sb("wdT", (128, 8 * 1024), BF16, es3)
            t2T = [sb(f"t2T{i}", (128, 8 * 512), BF16, es3) for i in range(2)]
            x1b = [sb(f"x1b{i}", (128, 4 * 1024), F32, es3) for i in range(2)]
            pf0 = sb("pf0", (128, 6 * 512), BF16, es3)
            pfs = sb("pfs", (128, 6 * 512), BF16, es3)
            yT = sb("yT", (128, 8 * 512), BF16, es3)
            outst = sb("outst", (128, 4 * 1024), F32, es3)
            lnS = sb("lnS", (6, 512), F32, es3)
            maskT = sb("maskT", (6, 512), F32, es3)
            lnG = sb("lnG", (6, 512), F32, es3)
            mx1 = sb("mx1", (128, 1), F32, es3)
            masktok = sb("masktok", (128, 6), F32, es3)

            for dc in range(8):
                ctx.dma("scalar", "qA", wdT[:, dc * 1024:(dc + 1) * 1024],
                        dt["wd_bf"][0:1024, dc * 128:(dc + 1) * 128],
                        reads=[bf_("dr_wd_bf")], writes=[bf_("sb_wdT")], transpose=True)

            for nb in range(NB3):
                par = nb % 2
                qb = nb // 2
                t2Tb = bf_(f"sb_t2T{par}")
                x1B = bf_(f"sb_x1b{par}")
                for cc in range(8):
                    ctx.dma("scalar", "qA", t2T[par][:, cc * 512:(cc + 1) * 512],
                            dt["t2_sp"][nb * 512:(nb + 1) * 512, cc * 128:(cc + 1) * 128],
                            reads=[bf_(f"dr_t2_sp{qb}")], writes=[t2Tb], transpose=True)
                x13d = split_pa(x1_sp)
                ctx.dma("sync", "qS", x1b[par][:].rearrange("p (a d) -> p a d", d=1024),
                        x13d[nb * 4:(nb + 1) * 4].rearrange("a p d -> p a d"),
                        reads=[bf_(f"dr_x1_sp{qb}")], writes=[x1B])
                zslots = []
                for e in range(6):
                    zp = psum[:, e * 512:(e + 1) * 512]
                    zslots.append(zp)
                    mms = []
                    for cc in range(8):
                        mms.append(lambda e_, cc=cc, e0=e, zp=zp, par=par: e_.matmul(
                            zp[0:100, :],
                            tokfm[:, cc * 608 + e0 * 100: cc * 608 + (e0 + 1) * 100],
                            t2T[par][:, cc * 512:(cc + 1) * 512],
                            start=(cc == 0), stop=False, skip_group_check=True))
                    ctx.ops("tensor", mms, reads=[bf_("sb_tokfm"), t2Tb], writes=[pbank[e]])
                LG, LG_B = pslice(3072, 3584)
                MT, MT_B = pslice(3584, 4096)
                for nch in range(4):
                    mms = []
                    for cc in range(8):
                        mms.append(lambda e_, cc=cc, nch=nch, par=par: e_.matmul(
                            LG[:, 0:6],
                            t2T[par][:, cc * 512 + nch * 128: cc * 512 + (nch + 1) * 128],
                            wg_fm[:, cc * 6:(cc + 1) * 6],
                            start=(cc == 0), stop=(cc == 7), skip_group_check=True))
                    ctx.ops("tensor", mms, reads=[t2Tb, bf_("sb_wg_fm")], writes=LG_B)
                    ctx.op("vector", lambda e_: e_.tensor_reduce(
                        mx1[:], LG[:, 0:6], axis=AX.X, op=ALU.max),
                        reads=LG_B, writes=[bf_("b_mx1")])
                    ctx.op("vector", lambda e_: e_.tensor_scalar(
                        masktok[:], LG[:, 0:6], mx1[:], None, ALU.is_ge),
                        reads=LG_B + [bf_("b_mx1")], writes=[bf_("b_masktok")])
                    ctx.op("tensor", lambda e_: e_.transpose(
                        MT[0:6, 0:128], masktok[:], IDENT_F32[:, :]),
                        reads=[bf_("b_masktok"), CF], writes=MT_B)
                    ctx.op("vector", lambda e_, nch=nch: e_.tensor_copy(
                        maskT[:, nch * 128:(nch + 1) * 128], MT[0:6, 0:128]),
                        reads=MT_B, writes=[bf_("sb_maskT")])
                for e in range(6):
                    ctx.op("scalar", lambda e_, e0=e: e_.activation(
                        pf0[0:100, e0 * 512:(e0 + 1) * 512], zslots[e0][0:100, :], AF.Exp,
                        bias=cf32[0:100, 2:3], scale=1.0 / 32.0),
                        reads=[pbank[e], CF], writes=[bf_("sb_pf0")])
                mms = []
                for e in range(6):
                    mms.append(lambda e_, e0=e: e_.matmul(
                        MT[0:6, 128:128 + 512] if False else MT[0:6, 0:512],
                        eind[:, e0 * 6:(e0 + 1) * 6],
                        pf0[0:100, e0 * 512:(e0 + 1) * 512],
                        start=(e0 == 0), stop=(e0 == 5), skip_group_check=True))
                ctx.ops("tensor", mms, reads=[bf_("sb_pf0"), EI, bf_("sb_maskT")], writes=MT_B)
                ctx.op("scalar", lambda e_: e_.activation(lnS[:], MT[0:6, 0:512], AF.Ln,
                                                          bias=cf32[0:6, 2:3], scale=1.0),
                       reads=MT_B + [CF], writes=[bf_("b_lnS")])
                ctx.op("vector", lambda e_: e_.tensor_scalar(lnS[:], lnS[:], -32.0, 2560.0,
                                                             ALU.mult, op1=ALU.add),
                       reads=[bf_("b_lnS")], writes=[bf_("b_lnS")])
                ctx.op("vector", lambda e_: e_.tensor_tensor(lnG[:], lnS[:], maskT[:], ALU.mult),
                       reads=[bf_("b_lnS"), bf_("sb_maskT")], writes=[bf_("b_lnG")])
                ctx.op("vector", lambda e_: e_.tensor_scalar(lnG[:], lnG[:], -2560.0, None, ALU.add),
                       reads=[bf_("b_lnG")], writes=[bf_("b_lnG")])
                for e in range(6):
                    ctx.op("tensor", lambda e_, e0=e: e_.matmul(
                        zslots[e0][0:100, :], selm[:, e0 * 100:(e0 + 1) * 100], lnG[:],
                        start=False, stop=True, skip_group_check=True),
                        reads=[bf_("b_lnG"), SM_], writes=[pbank[e]])
                    ctx.op("scalar", lambda e_, e0=e: e_.activation(
                        pfs[0:100, e0 * 512:(e0 + 1) * 512], zslots[e0][0:100, :], AF.Exp,
                        bias=cf32[0:100, 2:3], scale=1.0 / 32.0),
                        reads=[pbank[e], CF], writes=[bf_("sb_pfs")])
                for dc in range(8):
                    hp, hp_b = pslice((dc % 2) * 512, (dc % 2) * 512 + 512)
                    mms = []
                    for e in range(6):
                        mms.append(lambda e_, e0=e, dc=dc, hp=hp: e_.matmul(
                            hp[:], val_s[e0][0:100, dc * 128:(dc + 1) * 128],
                            pfs[0:100, e0 * 512:(e0 + 1) * 512],
                            start=(e0 == 0), stop=(e0 == 5), skip_group_check=True))
                    ctx.ops("tensor", mms,
                            reads=[bf_("sb_pfs")] + [bf_(f"sb_val{e}") for e in range(6)],
                            writes=hp_b)
                    ctx.op("vector", lambda e_, dc=dc, hp=hp, par=par: e_.tensor_tensor(
                        yT[:, dc * 512:(dc + 1) * 512], hp[:],
                        t2T[par][:, dc * 512:(dc + 1) * 512], ALU.add),
                        reads=hp_b + [t2Tb], writes=[bf_("sb_yT")])
                for ncs in range(4):
                    dp, dp_b = pslice(1024 + (ncs % 2) * 1024, 1024 + (ncs % 2) * 1024 + 1024)
                    mms = []
                    for dc in range(8):
                        for j in range(2):
                            mms.append(lambda e_, dc=dc, j=j, ncs=ncs, dp=dp: e_.matmul(
                                dp[:, j * 512:(j + 1) * 512],
                                yT[:, dc * 512 + ncs * 128: dc * 512 + (ncs + 1) * 128],
                                wdT[:, dc * 1024 + j * 512: dc * 1024 + (j + 1) * 512],
                                start=(dc == 0), stop=False, skip_group_check=True))
                    for j in range(2):
                        mms.append(lambda e_, j=j, dp=dp: e_.matmul(
                            dp[:, j * 512:(j + 1) * 512],
                            ONES_ROW_128, rows[0:1, 1024 + j * 512: 1024 + (j + 1) * 512],
                            start=False, stop=True, skip_group_check=True))
                    ctx.ops("tensor", mms, reads=[bf_("sb_yT"), bf_("sb_wdT"), CB, RW], writes=dp_b)
                    ctx.op("scalar", lambda e_, ncs=ncs, dp=dp: e_.activation(
                        outst[:, ncs * 1024:(ncs + 1) * 1024], dp[:], AF.Exp,
                        bias=BIAS_Z, scale=SCALE_COL),
                        reads=dp_b + [CF], writes=[bf_("sb_outst")])
                    ctx.op("scalar", lambda e_, ncs=ncs: e_.activation(
                        outst[:, ncs * 1024:(ncs + 1) * 1024],
                        outst[:, ncs * 1024:(ncs + 1) * 1024], AF.Ln,
                        bias=BIAS_Z, scale=1.0),
                        reads=[bf_("sb_outst"), CF], writes=[bf_("sb_outst")])
                    ctx.op("vector", lambda e_, ncs=ncs, par=par: e_.tensor_tensor(
                        outst[:, ncs * 1024:(ncs + 1) * 1024], outst[:, ncs * 1024:(ncs + 1) * 1024],
                        x1b[par][:, ncs * 1024:(ncs + 1) * 1024], ALU.add),
                        reads=[bf_("sb_outst"), x1B], writes=[bf_("sb_outst")])
                o3d = split_pa(out_d)
                ctx.dma("gpsimd", "qG", o3d[nb * 4:(nb + 1) * 4].rearrange("a p d -> p a d"),
                        outst[:].rearrange("p (a d) -> p a d", d=1024),
                        reads=[bf_("sb_outst")], writes=[bf_("dr_out")])

        ctx.barrier_all()
        for name in taps:
            ctx.dma("gpsimd", "qG", tap_d[name], dt[name],
                    reads=[b for n, b in bufs.items() if n.startswith("dr_" + name)])
        ctx.barrier_queue("sync", "qG")
        ctx.barrier_queue("sync", "qA")
        ctx.barrier_queue("sync", "qS")

    return nc


def kernel(**inputs):
    x = np.asarray(inputs["L0_attn_t"])
    W = np.ascontiguousarray(inputs["L0_attn_W_qkv"])
    b_qkv = np.asarray(inputs["L0_attn_b_qkv"])
    A = np.ascontiguousarray(inputs["L0_ffn_A"])
    Bm = np.ascontiguousarray(inputs["L0_ffn_B"])
    wg = np.ascontiguousarray(inputs["L0_ffn_w_gate"])
    wtw = np.ascontiguousarray(inputs["L0_ffn_wt_w"])
    wtb = np.asarray(inputs["L0_ffn_wt_b"])
    wdw = np.ascontiguousarray(inputs["L0_ffn_wd_w"])
    wdb = np.asarray(inputs["L0_ffn_wd_b"])
    scale = np.asarray(inputs["L0_ffn_scale"])

    cbf, selm, eind = _host_consts()
    bcol = np.ascontiguousarray(b_qkv[0:2048].reshape(16, 128).T)
    rows_bf = np.concatenate([wtb, wdb, b_qkv[2048:3072]])[None, :].astype(bf16)
    cf32 = np.zeros((128, 132), np.float32)
    cf32[:, 0] = -C_ATTN
    cf32[:, 1] = 1e-5
    cf32[:, 2] = 0.0
    cf32[:, 3] = scale[0]
    cf32[:, 4:132] = np.eye(128, dtype=np.float32)

    nc = build()
    in_maps = []
    for c in range(8):
        b, h = divmod(c, 2)
        xs = np.ascontiguousarray(np.roll(x[b], -h * HALF, axis=0)) if h else np.ascontiguousarray(x[b])
        in_maps.append({
            "x_seq": xs, "w_qkv": W, "bcol": bcol, "a_in": A, "b_in": Bm,
            "wgate": wg, "wt_w": wtw, "wd_w": wdw, "rows_bf": rows_bf,
            "cbf": cbf, "selm": selm, "eind": eind, "cf32": cf32,
        })
    res = run_bass_kernel_spmd(nc, in_maps, core_ids=list(range(8)))
    out = np.empty((B, N, D), np.float32)
    for c in range(8):
        b, h = divmod(c, 2)
        out[b, h * HALF:(h + 1) * HALF] = res.results[c]["out"]
    return out


# version 04
# speedup vs baseline: 1.0320x; 1.0320x over previous
import numpy as np
import ml_dtypes
import concourse.bass as bass
import concourse.mybir as mybir
from concourse.bass_utils import run_bass_kernel_spmd
from contextlib import ExitStack
from collections import defaultdict

F32 = mybir.dt.float32
BF16 = mybir.dt.bfloat16
AF = mybir.ActivationFunctionType
ALU = mybir.AluOpType
AX = mybir.AxisListType
bf16 = ml_dtypes.bfloat16

B, N, D = 4, 8192, 1024
HALF = N // 2
NSLAB = 8
SLAB = 1024
QB = 4
NB3 = 8
BLK3 = 512
C_ATTN = 24.0


class Buf:
    __slots__ = ("writer", "readers", "name")

    def __init__(self, name=""):
        self.writer = None
        self.readers = []
        self.name = name


class Ctx:
    ENGS = ("tensor", "vector", "scalar", "gpsimd", "sync")
    QUEUES = ("qS", "qA", "qG")

    def __init__(self, nc, es):
        self.nc = nc
        self.sem = {}
        for n in self.ENGS + self.QUEUES:
            self.sem[n] = es.enter_context(nc.semaphore("sem_" + n))
        self.cnt = defaultdict(int)
        self.waited = defaultdict(dict)

    def _wait(self, eng, src, val):
        if src == eng:
            return
        assert val <= self.cnt[src], ("wait-ahead deadlock", eng, src, val, self.cnt[src])
        w = self.waited[eng]
        if val > w.get(src, 0):
            getattr(self.nc, eng).wait_ge(self.sem[src], val)
            w[src] = val

    def _deps(self, eng, reads, writes):
        need = {}
        for b in reads:
            if b.writer:
                s, v = b.writer
                if need.get(s, 0) < v:
                    need[s] = v
        for b in writes:
            if b.writer:
                s, v = b.writer
                if need.get(s, 0) < v:
                    need[s] = v
            for s, v in b.readers:
                if need.get(s, 0) < v:
                    need[s] = v
        for s, v in need.items():
            self._wait(eng, s, v)

    def _finish(self, tag, c, reads, writes):
        for b in reads:
            b.readers.append((tag, c))
        for b in writes:
            b.writer = (tag, c)
            b.readers = []

    def _maybe_drain(self, eng, reads):
        if eng in ("vector", "scalar", "gpsimd"):
            c = self.cnt[eng]
            for b in reads:
                if b.writer and b.writer[0] == eng and b.writer[1] >= c - 1:
                    getattr(self.nc, eng).drain()
                    return

    def op(self, eng, fn, reads=(), writes=()):
        self._deps(eng, reads, writes)
        self._maybe_drain(eng, reads)
        ins = fn(getattr(self.nc, eng))
        self.cnt[eng] += 1
        ins.then_inc(self.sem[eng], 1)
        self._finish(eng, self.cnt[eng], reads, writes)

    def ops(self, eng, fns, reads=(), writes=()):
        self._deps(eng, reads, writes)
        self._maybe_drain(eng, reads)
        e = getattr(self.nc, eng)
        ins = None
        for fn in fns:
            ins = fn(e)
        self.cnt[eng] += 1
        ins.then_inc(self.sem[eng], 1)
        self._finish(eng, self.cnt[eng], reads, writes)

    def dma(self, eng, queue, out, in_, reads=(), writes=(), transpose=False):
        self._deps(eng, reads, writes)
        getattr(self.nc, eng).dma_start(out=out, in_=in_, transpose=transpose) \
            .then_inc(self.sem[queue], 16)
        self.cnt[queue] += 16
        self._finish(queue, self.cnt[queue], reads, writes)

    def barrier_queue(self, eng, queue):
        self._wait(eng, queue, self.cnt[queue])

    def barrier_all(self):
        snap = {n: self.cnt[n] for n in self.ENGS + self.QUEUES}
        for e in self.ENGS:
            for n, v in snap.items():
                if n != e and v > 0:
                    self._wait(e, n, v)


def _host_consts():
    ident = np.eye(128, dtype=np.float32).astype(bf16)
    onesb = np.ones((128, 136), dtype=np.float32).astype(bf16)
    cbf = np.concatenate([ident, onesb], axis=1)
    selm = np.zeros((6, 600), dtype=np.float32)
    for e in range(6):
        selm[e, e * 100:(e + 1) * 100] = 1.0
    eind = np.zeros((100, 36), dtype=np.float32)
    for e in range(6):
        eind[:, e * 6 + e] = 1.0
    return cbf, selm.astype(np.float32), eind.astype(bf16)


def emit_ln8(ctx, bf_, tiles, st96, sq32, agg8, rstd8, mcol8, cf32, cfbuf, tag):
    V, S = "vector", "scalar"
    n = len(tiles)
    w = 12 * n
    bST = bf_(tag + "st96")
    bSM, bCV, bSQ = bf_(tag + "sm"), bf_(tag + "cv"), bf_(tag + "sq")
    bM2, bVAR, bRC, bMC = bf_(tag + "m2"), bf_(tag + "var"), bf_(tag + "rc"), bf_(tag + "mc")
    m = [st96[:, c:w:12] for c in (1, 4, 7, 10)]
    cv = [st96[:, c:w:12] for c in (2, 5, 8, 11)]
    SM8, CV8, M2 = agg8[:, 0:n], agg8[:, 8:8 + n], agg8[:, 24:24 + n]
    ctx.op(V, lambda e: e.tensor_tensor(SM8, m[0], m[1], ALU.add), reads=[bST], writes=[bSM])
    ctx.op(V, lambda e: e.tensor_tensor(agg8[:, 16:16 + n], m[2], m[3], ALU.add),
           reads=[bST], writes=[bSQ])
    ctx.op(V, lambda e: e.tensor_tensor(SM8, SM8, agg8[:, 16:16 + n], ALU.add),
           reads=[bSM, bSQ], writes=[bSM])
    ctx.op(V, lambda e: e.tensor_tensor(CV8, cv[0], cv[1], ALU.add), reads=[bST], writes=[bCV])
    ctx.op(V, lambda e: e.tensor_tensor(agg8[:, 16:16 + n], cv[2], cv[3], ALU.add),
           reads=[bST], writes=[bSQ])
    ctx.op(V, lambda e: e.tensor_tensor(CV8, CV8, agg8[:, 16:16 + n], ALU.add),
           reads=[bCV, bSQ], writes=[bCV])
    for i in range(4):
        ctx.op(V, lambda e, i=i: e.tensor_tensor(sq32[:, 8 * i:8 * i + n], m[i], m[i], ALU.mult),
               reads=[bST], writes=[bSQ])
    ctx.op(V, lambda e: e.tensor_tensor(sq32[:, 0:n], sq32[:, 0:n], sq32[:, 8:8 + n], ALU.add),
           reads=[bSQ], writes=[bSQ])
    ctx.op(V, lambda e: e.tensor_tensor(sq32[:, 16:16 + n], sq32[:, 16:16 + n], sq32[:, 24:24 + n],
                                        ALU.add),
           reads=[bSQ], writes=[bSQ])
    ctx.op(V, lambda e: e.tensor_tensor(sq32[:, 0:n], sq32[:, 0:n], sq32[:, 16:16 + n], ALU.add),
           reads=[bSQ], writes=[bSQ])
    ctx.op(V, lambda e: e.tensor_scalar(SM8, SM8, 0.25, None, ALU.mult), reads=[bSM], writes=[bSM])
    ctx.op(V, lambda e: e.tensor_tensor(M2, SM8, SM8, ALU.mult), reads=[bSM], writes=[bM2])
    ctx.op(V, lambda e: e.tensor_scalar(CV8, CV8, 1.0 / 1024.0, None, ALU.mult),
           reads=[bCV], writes=[bVAR])
    ctx.op(V, lambda e: e.scalar_tensor_tensor(CV8, sq32[:, 0:n], 0.25, CV8, ALU.mult, ALU.add),
           reads=[bSQ, bVAR], writes=[bVAR])
    ctx.op(V, lambda e: e.tensor_tensor(CV8, CV8, M2, ALU.subtract), reads=[bVAR, bM2], writes=[bVAR])
    ctx.op(S, lambda e: e.activation(rstd8[:, 0:n], CV8, AF.Ln, bias=cf32[:, 1:2], scale=1.0),
           reads=[bVAR, cfbuf], writes=[bRC])
    ctx.op(S, lambda e: e.activation(rstd8[:, 0:n], rstd8[:, 0:n], AF.Exp,
                                     bias=cf32[:, 2:3], scale=-0.5),
           reads=[bRC, cfbuf], writes=[bRC])
    ctx.op(V, lambda e: e.tensor_scalar(mcol8[:, 0:n], SM8, -1.0, None, ALU.mult),
           reads=[bSM], writes=[bMC])
    ctx.op(V, lambda e: e.tensor_tensor(mcol8[:, 0:n], mcol8[:, 0:n], rstd8[:, 0:n], ALU.mult),
           reads=[bMC, bRC], writes=[bMC])
    return bRC, bMC


def emit_ln(ctx, bf_, xt, xbuf, sbufs, tag):
    st6, st2, tmp4, tmp1, mean2, mcol, rcol, cf32 = sbufs
    V, S = "vector", "scalar"
    bST6, bSM, bSCV, bT4, bSM2 = bf_(tag + "st6"), bf_(tag + "sm"), bf_(tag + "scv"), bf_(tag + "t4"), bf_(tag + "sm2")
    bMEAN, bM2, bVAR, bRC, bMC = bf_(tag + "mean"), bf_(tag + "m2"), bf_(tag + "var"), bf_(tag + "rc"), bf_(tag + "mc")
    ctx.op(V, lambda e: e.bn_stats(st6[:, 0:6], xt[:, 0:512]), reads=[xbuf], writes=[bST6])
    ctx.op(V, lambda e: e.bn_stats(st6[:, 6:12], xt[:, 512:1024]), reads=[xbuf], writes=[bST6])
    ctx.op(V, lambda e: e.reduce_sum(st2[:, 0:1], st6[:, 1:12:3], axis=AX.X), reads=[bST6], writes=[bSM])
    ctx.op(V, lambda e: e.reduce_sum(st2[:, 1:2], st6[:, 2:12:3], axis=AX.X), reads=[bST6], writes=[bSCV])
    ctx.op(V, lambda e: e.tensor_tensor(tmp4[:], st6[:, 1:12:3], st6[:, 1:12:3], ALU.mult),
           reads=[bST6], writes=[bT4])
    ctx.op(V, lambda e: e.reduce_sum(tmp1[:], tmp4[:], axis=AX.X), reads=[bT4], writes=[bSM2])
    ctx.op(V, lambda e: e.tensor_scalar(st2[:, 0:1], st2[:, 0:1], 0.25, None, ALU.mult),
           reads=[bSM], writes=[bMEAN])
    ctx.op(V, lambda e: e.tensor_tensor(mean2[:], st2[:, 0:1], st2[:, 0:1], ALU.mult),
           reads=[bMEAN], writes=[bM2])
    ctx.op(V, lambda e: e.tensor_scalar(st2[:, 1:2], st2[:, 1:2], 1.0 / 1024.0, None, ALU.mult),
           reads=[bSCV], writes=[bVAR])
    ctx.op(V, lambda e: e.scalar_tensor_tensor(st2[:, 1:2], tmp1[:], 0.25, st2[:, 1:2], ALU.mult, ALU.add),
           reads=[bSM2, bVAR], writes=[bVAR])
    ctx.op(V, lambda e: e.tensor_tensor(st2[:, 1:2], st2[:, 1:2], mean2[:], ALU.subtract),
           reads=[bVAR, bM2], writes=[bVAR])
    ctx.op(S, lambda e: e.activation(rcol[:], st2[:, 1:2], AF.Ln, bias=cf32[:, 1:2], scale=1.0),
           reads=[bVAR, bf_("sb_cf32_s")], writes=[bRC])
    ctx.op(S, lambda e: e.activation(rcol[:], rcol[:], AF.Exp, bias=cf32[:, 2:3], scale=-0.5),
           reads=[bRC, bf_("sb_cf32_s")], writes=[bRC])
    ctx.op(V, lambda e: e.tensor_scalar(mcol[:], st2[:, 0:1], rcol[:], -1.0, ALU.mult, op1=ALU.mult),
           reads=[bMEAN, bRC], writes=[bMC])
    return bRC, bMC


def split_pa(ap, p=128):
    return ap.rearrange("(a p) d -> a p d", p=p)


def build(upto='p3', taps=()):
    nc = bass.Bass("TRN2", target_bir_lowering=False)
    dt = {}

    def din(name, shape, dtype):
        dt[name] = nc.dram_tensor(name, list(shape), dtype, kind="ExternalInput").ap()
        return dt[name]

    def dout(name, shape, dtype):
        dt[name] = nc.dram_tensor(name, list(shape), dtype, kind="ExternalOutput").ap()
        return dt[name]

    def dint(name, shape, dtype):
        dt[name] = nc.dram_tensor(name, list(shape), dtype, kind="Internal").ap()
        return dt[name]

    x_seq = din("x_seq", (N, D), F32)
    w_qkv = din("w_qkv", (3072, D), F32)
    bcol_d = din("bcol", (128, 16), F32)
    a_in = din("a_in", (6, 100, 16), F32)
    b_in = din("b_in", (6, 16, D), F32)
    wgate = din("wgate", (D, 6), F32)
    wt_w = din("wt_w", (D, D), F32)
    wd_w = din("wd_w", (D, D), F32)
    rows_d = din("rows_bf", (1, 3072), BF16)
    cbf_d = din("cbf", (128, 264), BF16)
    selm_d = din("selm", (6, 600), F32)
    eind_d = din("eind", (100, 36), BF16)
    cf32_d = din("cf32", (128, 132), F32)
    out_d = dout("out", (HALF, D), F32)

    t_sp = dint("t_sp", (N, D), BF16)
    t2_sp = dint("t2_sp", (HALF, D), BF16)
    x1_sp = dint("x1_sp", (HALF, D), F32)
    k_sp = dint("k_sp", (NSLAB, 8, 128, SLAB), BF16)
    v_sp = dint("v_sp", (NSLAB, 8, 128, SLAB), BF16)
    q_sp = dint("q_sp", (QB, 8, 128, SLAB), BF16)
    wq_bf = dint("wq_bf", (3072, D), BF16)
    wt_bf = dint("wt_bf", (D, D), BF16)
    wd_bf = dint("wd_bf", (D, D), BF16)
    tok_sp = dint("tok_sp", (608, D), BF16)

    tap_d = {}
    for name in taps:
        t = dt[name].tensor
        tap_d[name] = dout("tap_" + name, list(t.shape), t.dtype)

    bufs = {}

    def bf_(name):
        if name not in bufs:
            bufs[name] = Buf(name)
        return bufs[name]

    es_prog = ExitStack()
    with es_prog:
        ctx = Ctx(nc, es_prog)
        blk = es_prog.enter_context(nc.Block())
        pers = ExitStack()
        es_prog.enter_context(pers)

        def sb(name, shape, dtype, stack=None):
            return (stack or pers).enter_context(nc.sbuf_tensor(name, list(shape), dtype))

        cbf = sb("cbf_s", (128, 264), BF16)
        rows = sb("rows_s", (1, 3072), BF16)
        bcol = sb("bcol_s", (128, 16), F32)
        cf32 = sb("cf32_s", (128, 132), F32)
        selm = sb("selm_s", (6, 600), F32)
        eind = sb("eind_s", (100, 36), BF16)
        wg_fm = sb("wg_fm", (128, 48), BF16)
        tokfm = sb("tokfm", (128, 8 * 608), BF16)
        val_s = [sb(f"val{e}", (128, D), BF16) for e in range(6)]

        IDENT_F32 = cf32[:, 4:132]
        ONES_ROW_128 = cbf[0:1, 128:256]
        ONES_ROW_100 = cbf[0:1, 128:228]
        BIAS_NC = cf32[:, 0:1]
        BIAS_EPS = cf32[:, 1:2]
        BIAS_Z = cf32[:, 2:3]
        SCALE_COL = cf32[:, 3:4]

        psum = es_prog.enter_context(nc.psum_tensor("psA", [128, 4096], F32))
        pbank = [bf_(f"pbank{i}") for i in range(8)]

        def pslice(c0, c1):
            return psum[:, c0:c1], [pbank[i] for i in range(c0 // 512, (c1 + 511) // 512)]

        for dstt, src in ((cbf, cbf_d), (rows, rows_d), (bcol, bcol_d),
                          (cf32, cf32_d), (selm, selm_d), (eind, eind_d)):
            ctx.dma("sync", "qS", dstt[:], src, writes=[bf_("sb_" + dstt.name)])
        CB, RW, BC, CF, SM_, EI = [bufs["sb_" + t.name] for t in (cbf, rows, bcol, cf32, selm, eind)]

        ctx.dma("gpsimd", "qG", dt["wq_bf"][:], w_qkv, writes=[bf_("dr_wq_bf")])
        ctx.dma("gpsimd", "qG", dt["wt_bf"][:], wt_w, writes=[bf_("dr_wt_bf")])
        ctx.dma("gpsimd", "qG", dt["wd_bf"][:], wd_w, writes=[bf_("dr_wd_bf")])

        es0 = ExitStack()
        wg_f32 = sb("wg_f32", (128, 48), F32, es0)
        for cc in range(8):
            ctx.dma("sync", "qS", wg_f32[:, cc * 6:(cc + 1) * 6],
                    wgate[cc * 128:(cc + 1) * 128, :], writes=[bf_("sb_wg_f32")])
        ctx.op("vector", lambda e: e.tensor_copy(wg_fm[:], wg_f32[:]),
               reads=[bf_("sb_wg_f32")], writes=[bf_("sb_wg_fm")])

        ctx.barrier_all()
        if upto in ("p1", "p2", "p3"):
         with ExitStack() as es1:
            wTall = sb("wTall", (128, 8 * 3072), BF16, es1)
            xblk = [sb(f"xblk{i}", (128, 2 * 1024), F32, es1) for i in range(2)]
            tstage = sb("tstage", (128, 8 * 1024), BF16, es1)
            tT = [sb(f"tT{i}", (128, 8 * 1024), BF16, es1) for i in range(2)]
            kstage = sb("kstage", (128, 8 * 1024), BF16, es1)
            vstage = sb("vstage", (128, 8 * 1024), BF16, es1)
            qstage = sb("qstage", (128, 8 * 1024), BF16, es1)
            st96 = sb("st96", (128, 96), F32, es1)
            sq32a = sb("sq32a", (128, 64), F32, es1)
            agg8a = sb("agg8a", (128, 64), F32, es1)
            rstd8a = sb("rstd8a", (128, 8), F32, es1)
            mcol8a = sb("mcol8a", (128, 8), F32, es1)

            for dc in range(8):
                ctx.dma("scalar", "qA", wTall[:, dc * 3072:(dc + 1) * 3072],
                        dt["wq_bf"][0:3072, dc * 128:(dc + 1) * 128],
                        reads=[bf_("dr_wq_bf")], writes=[bf_("sb_wTall")], transpose=True)

            def WT(dc, c0, c1):
                return wTall[:, dc * 3072 + c0: dc * 3072 + c1]

            x3d = split_pa(x_seq)
            rot = 0
            for s in range(NSLAB):
                xb = [bf_("b_xblk0"), bf_("b_xblk1")]
                for half in range(2):
                    tile_list = []
                    for q2 in (2 * half, 2 * half + 1):
                        par2 = q2 % 2
                        ctx.dma("sync", "qS",
                                xblk[par2][:].rearrange("p (a d) -> p a d", d=1024),
                                x3d[8 * s + 2 * q2: 8 * s + 2 * q2 + 2].rearrange("a p d -> p a d"),
                                reads=[bf_("dr_x_seq")], writes=[xb[par2]])
                        for th in range(2):
                            t8 = 2 * q2 + th
                            xt = xblk[par2][:, th * 1024:(th + 1) * 1024]
                            ctx.op("vector", lambda e_, xt=xt, t8=t8, half=half: e_.bn_stats(
                                st96[:, t8 * 12: t8 * 12 + 6], xt[:, 0:512]),
                                reads=[xb[par2]], writes=[bf_(f"ln1_{half}st96")])
                            ctx.op("vector", lambda e_, xt=xt, t8=t8, half=half: e_.bn_stats(
                                st96[:, t8 * 12 + 6: t8 * 12 + 12], xt[:, 512:1024]),
                                reads=[xb[par2]], writes=[bf_(f"ln1_{half}st96")])
                            tile_list.append((xt, xb[par2]))
                    bRC, bMC = emit_ln8(ctx, bf_, tile_list, st96[:, 48 * half:48 * half + 48],
                                        sq32a[:, 32 * half:32 * half + 32],
                                        agg8a[:, 32 * half:32 * half + 32],
                                        rstd8a[:, 4 * half:4 * half + 4],
                                        mcol8a[:, 4 * half:4 * half + 4],
                                        cf32, CF, f"ln1_{half}")
                    for ti, (xt, xbuf) in enumerate(tile_list):
                        t8 = 4 * half + ti
                        ctx.op("scalar", lambda e_, xt=xt, t8=t8, ti=ti, half=half: e_.activation(
                            tstage[:, t8 * 1024:(t8 + 1) * 1024], xt, AF.Identity,
                            bias=mcol8a[:, 4 * half + ti: 4 * half + ti + 1],
                            scale=rstd8a[:, 4 * half + ti: 4 * half + ti + 1]),
                            reads=[xbuf, bMC, bRC], writes=[bf_("sb_tstage")])
                t3d = split_pa(t_sp)
                ctx.dma("sync", "qS",
                        t3d[s * 8:(s + 1) * 8].rearrange("a p d -> p a d"),
                        tstage[:].rearrange("p (a d) -> p a d", d=1024),
                        reads=[bf_("sb_tstage")], writes=[bf_(f"dr_t_sp{s}")])
                par = s % 2
                tTb = bf_(f"sb_tT{par}")
                for dc in range(8):
                    ctx.dma("scalar", "qA", tT[par][:, dc * 1024:(dc + 1) * 1024],
                            dt["t_sp"][s * 1024:(s + 1) * 1024, dc * 128:(dc + 1) * 128],
                            reads=[bf_(f"dr_t_sp{s}")], writes=[tTb], transpose=True)
                for ec in range(8):
                    pm, pm_b = pslice((rot % 2) * 1024, (rot % 2) * 1024 + 1024); rot += 1
                    mms = []
                    for dc in range(8):
                        for j in range(2):
                            mms.append(lambda e_, dc=dc, j=j, ec=ec, par=par: e_.matmul(
                                pm[:, j * 512:(j + 1) * 512],
                                WT(dc, (8 + ec) * 128, (9 + ec) * 128),
                                tT[par][:, dc * 1024 + j * 512: dc * 1024 + (j + 1) * 512],
                                start=(dc == 0), stop=(dc == 7), skip_group_check=True))
                    ctx.ops("tensor", mms, reads=[bf_("sb_wTall"), tTb], writes=pm_b)
                    ctx.op("vector", lambda e_, ec=ec, pm=pm: e_.tensor_scalar(
                        kstage[:, ec * 1024:(ec + 1) * 1024], pm[:], bcol[:, 8 + ec:9 + ec], None, ALU.add),
                        reads=pm_b + [BC], writes=[bf_("sb_kstage")])
                ctx.dma("gpsimd", "qG", dt["k_sp"][s].rearrange("a p d -> p a d"),
                        kstage[:].rearrange("p (a d) -> p a d", d=1024),
                        reads=[bf_("sb_kstage")], writes=[bf_(f"dr_k_sp{s}")])
                for kt in range(8):
                    pm, pm_b = pslice((rot % 2) * 1024, (rot % 2) * 1024 + 1024); rot += 1
                    mms = []
                    for dc in range(8):
                        for j in range(2):
                            mms.append(lambda e_, dc=dc, j=j, kt=kt, par=par: e_.matmul(
                                pm[:, j * 512:(j + 1) * 512],
                                tT[par][:, dc * 1024 + kt * 128: dc * 1024 + (kt + 1) * 128],
                                WT(dc, 2048 + j * 512, 2048 + (j + 1) * 512),
                                start=(dc == 0), stop=False, skip_group_check=True))
                    for j in range(2):
                        mms.append(lambda e_, j=j: e_.matmul(
                            pm[:, j * 512:(j + 1) * 512],
                            ONES_ROW_128, rows[0:1, 2048 + j * 512: 2048 + (j + 1) * 512],
                            start=False, stop=True, skip_group_check=True))
                    ctx.ops("tensor", mms, reads=[bf_("sb_wTall"), tTb, CB, RW], writes=pm_b)
                    ctx.op("vector", lambda e_, kt=kt, pm=pm: e_.tensor_copy(
                        vstage[:, kt * 1024:(kt + 1) * 1024], pm[:]),
                        reads=pm_b, writes=[bf_("sb_vstage")])
                ctx.dma("gpsimd", "qG", dt["v_sp"][s].rearrange("a p d -> p a d"),
                        vstage[:].rearrange("p (a d) -> p a d", d=1024),
                        reads=[bf_("sb_vstage")], writes=[bf_(f"dr_v_sp{s}")])
                if s < QB:
                    for ec in range(8):
                        pm, pm_b = pslice((rot % 2) * 1024, (rot % 2) * 1024 + 1024); rot += 1
                        mms = []
                        for dc in range(8):
                            for j in range(2):
                                mms.append(lambda e_, dc=dc, j=j, ec=ec, par=par: e_.matmul(
                                    pm[:, j * 512:(j + 1) * 512],
                                    WT(dc, ec * 128, (ec + 1) * 128),
                                    tT[par][:, dc * 1024 + j * 512: dc * 1024 + (j + 1) * 512],
                                    start=(dc == 0), stop=(dc == 7), skip_group_check=True))
                        ctx.ops("tensor", mms, reads=[bf_("sb_wTall"), tTb], writes=pm_b)
                        ctx.op("vector", lambda e_, ec=ec, pm=pm: e_.tensor_scalar(
                            qstage[:, ec * 1024:(ec + 1) * 1024], pm[:], bcol[:, ec:ec + 1], None, ALU.add),
                            reads=pm_b + [BC], writes=[bf_("sb_qstage")])
                    ctx.dma("gpsimd", "qG", dt["q_sp"][s].rearrange("a p d -> p a d"),
                            qstage[:].rearrange("p (a d) -> p a d", d=1024),
                            reads=[bf_("sb_qstage")], writes=[bf_(f"dr_q_sp{s}")])

        if upto in ("p1", "p2", "p3"):
            ab_f32 = sb("ab_f32", (128, 1040), F32, es0)
            ab_bf = sb("ab_bf", (128, 1040), BF16, es0)
            aT = [sb(f"aT{e}", (16, 100), BF16, es0) for e in range(6)]
            tokstage = sb("tokstage", (128, 1024), BF16, es0)

            ctx.op("vector", lambda e: e.memset(ab_f32[:], 0.0), writes=[bf_("sb_ab_f32")])
            for e in range(6):
                ctx.dma("sync", "qS", ab_f32[0:100, 0:16], a_in[e], writes=[bf_("sb_ab_f32")])
                ctx.dma("sync", "qS", ab_f32[0:16, 16:1040], b_in[e], writes=[bf_("sb_ab_f32")])
                ctx.op("vector", lambda e_: e_.tensor_copy(ab_bf[:], ab_f32[:]),
                       reads=[bf_("sb_ab_f32")], writes=[bf_("sb_ab_bf")])
                pA, pA_b = pslice(0, 512)
                ctx.op("tensor", lambda e_: e_.transpose(pA[0:16, 0:100], ab_f32[0:100, 0:16],
                                                         IDENT_F32[0:100, 0:100]),
                       reads=[bf_("sb_ab_f32"), CF], writes=pA_b)
                ctx.op("vector", lambda e_, e0=e: e_.tensor_copy(aT[e0][:], pA[0:16, 0:100]),
                       reads=pA_b, writes=[bf_(f"sb_aT{e}")])
                pT, pT_b = pslice(1024, 2048)
                ctx.ops("tensor",
                        [(lambda e_, j=j, e0=e: e_.matmul(pT[0:100, j * 512:(j + 1) * 512],
                                                          aT[e0][:], ab_bf[0:16, 16 + j * 512:16 + (j + 1) * 512],
                                                          start=True, stop=True))
                         for j in range(2)],
                        reads=[bf_(f"sb_aT{e}"), bf_("sb_ab_bf")], writes=pT_b)
                ctx.op("vector", lambda e_: e_.tensor_copy(tokstage[0:100, :], pT[0:100, :]),
                       reads=pT_b, writes=[bf_("sb_tokstage")])
                ctx.dma("gpsimd", "qG", dt["tok_sp"][e * 100:(e + 1) * 100, :], tokstage[0:100, :],
                        reads=[bf_("sb_tokstage")], writes=[bf_("dr_tok_sp")])
            ctx.op("vector", lambda e_: e_.memset(tokstage[0:8, :], 0.0), writes=[bf_("sb_tokstage")])
            ctx.dma("gpsimd", "qG", dt["tok_sp"][600:608, :], tokstage[0:8, :],
                    reads=[bf_("sb_tokstage")], writes=[bf_("dr_tok_sp")])
            for cc in range(8):
                ctx.dma("scalar", "qA", tokfm[:, cc * 608:(cc + 1) * 608],
                        dt["tok_sp"][0:608, cc * 128:(cc + 1) * 128],
                        reads=[bf_("dr_tok_sp")], writes=[bf_("sb_tokfm")], transpose=True)

            es0.close()
            with ExitStack() as es_val:
                wtT = sb("wtT", (128, 8 * 1024), BF16, es_val)
                for cc in range(8):
                    ctx.dma("scalar", "qA", wtT[:, cc * 1024:(cc + 1) * 1024],
                            dt["wt_bf"][0:1024, cc * 128:(cc + 1) * 128],
                            reads=[bf_("dr_wt_bf")], writes=[bf_("sb_wtT")], transpose=True)
                for e in range(6):
                    pV, pV_b = pslice(0, 1024)
                    mms = []
                    for cc in range(8):
                        for j in range(2):
                            mms.append(lambda e_, cc=cc, j=j, e0=e: e_.matmul(
                                pV[0:100, j * 512:(j + 1) * 512],
                                tokfm[:, cc * 608 + e0 * 100: cc * 608 + (e0 + 1) * 100],
                                wtT[:, cc * 1024 + j * 512: cc * 1024 + (j + 1) * 512],
                                start=(cc == 0), stop=False, skip_group_check=True))
                    for j in range(2):
                        mms.append(lambda e_, j=j: e_.matmul(
                            pV[0:100, j * 512:(j + 1) * 512],
                            ONES_ROW_100, rows[0:1, j * 512:(j + 1) * 512],
                            start=False, stop=True, skip_group_check=True))
                    ctx.ops("tensor", mms, reads=[bf_("sb_tokfm"), bf_("sb_wtT"), CB, RW], writes=pV_b)
                    ctx.op("vector", lambda e_, e0=e: e_.tensor_copy(val_s[e0][0:100, :], pV[0:100, :]),
                           reads=pV_b, writes=[bf_(f"sb_val{e}")])
                    ctx.op("vector", lambda e_, e0=e: e_.memset(val_s[e0][0:1, :], 0.0),
                           writes=[bf_(f"sb_val{e}")])

        ctx.barrier_all()
        if upto in ("p2", "p3"):
         with ExitStack() as es2:
            qbuf = sb("qbuf", (128, 8 * 1024), BF16, es2)
            kbuf = [sb(f"kbuf{i}", (128, 8 * 1024), BF16, es2) for i in range(2)]
            vbuf = [sb(f"vbuf{i}", (128, 8 * 1025), BF16, es2) for i in range(2)]
            pTt = sb("pTt", (128, 8 * 1024), BF16, es2)
            outsb = sb("outsb", (128, 8 * 1025), F32, es2)
            xown = sb("xown", (128, 8 * 1024), F32, es2)
            t2st = sb("t2st", (128, 4 * 1024), BF16, es2)
            st96b = sb("st96b", (128, 96), F32, es2)
            sq32b = sb("sq32b", (128, 32), F32, es2)
            agg8b = sb("agg8b", (128, 32), F32, es2)
            rstd8b = sb("rstd8b", (128, 8), F32, es2)
            mcol8b = sb("mcol8b", (128, 8), F32, es2)
            rec = sb("rec", (128, 1), F32, es2)
            t2f = sb("t2f", (128, 1024), F32, es2)

            for i in range(2):
                ctx.ops("vector",
                        [(lambda e_, i=i, t=t: e_.memset(vbuf[i][:, t * 1025 + 1024: t * 1025 + 1025], 1.0))
                         for t in range(8)],
                        writes=[bf_(f"sb_vbuf{i}_ones")])

            S_HALF = [pslice(0, 512), pslice(512, 1024)]
            PV = [pslice(1024, 2560), pslice(2560, 4096)]

            for qb in range(QB):
                qbB = bf_("sb_qbuf")
                ctx.dma("sync", "qS", qbuf[:].rearrange("p (a d) -> p a d", d=1024),
                        dt["q_sp"][qb].rearrange("a p d -> p a d"),
                        reads=[bf_(f"dr_q_sp{qb}")], writes=[qbB])
                ctx.dma("sync", "qS", xown[:].rearrange("p (a d) -> p a d", d=1024),
                        x3d[qb * 8:(qb + 1) * 8].rearrange("a p d -> p a d"),
                        reads=[bf_("dr_x_seq")], writes=[bf_("sb_xown")])
                for s in range(NSLAB):
                    par = s % 2
                    kB, vB = bf_(f"sb_kbuf{par}"), bf_(f"sb_vbuf{par}")
                    vOnes = bf_(f"sb_vbuf{par}_ones")
                    ctx.dma("sync", "qS", kbuf[par][:].rearrange("p (a d) -> p a d", d=1024),
                            dt["k_sp"][s].rearrange("a p d -> p a d"),
                            reads=[bf_(f"dr_k_sp{s}")], writes=[kB])
                    ctx.dma("sync", "qS",
                            vbuf[par][:].rearrange("p (a d) -> p a d", d=1025)[:, :, 0:1024],
                            dt["v_sp"][s].rearrange("a p d -> p a d"),
                            reads=[bf_(f"dr_v_sp{s}")], writes=[vB])
                    for kc in range(8):
                        for qh in range(2):
                            sp, sp_b = S_HALF[qh]
                            mms = []
                            for dc in range(8):
                                mms.append(lambda e_, dc=dc, qh=qh, kc=kc, par=par, sp=sp: e_.matmul(
                                    sp[:],
                                    kbuf[par][:, dc * 1024 + kc * 128: dc * 1024 + (kc + 1) * 128],
                                    qbuf[:, dc * 1024 + qh * 512: dc * 1024 + (qh + 1) * 512],
                                    start=(dc == 0), stop=(dc == 7), skip_group_check=True))
                            ctx.ops("tensor", mms, reads=[kB, qbB], writes=sp_b)
                            ctx.op("scalar", lambda e_, kc=kc, qh=qh, sp=sp: e_.activation(
                                pTt[:, kc * 1024 + qh * 512: kc * 1024 + (qh + 1) * 512], sp[:],
                                AF.Exp, bias=BIAS_NC, scale=1.0 / 32.0),
                                reads=sp_b + [CF], writes=[bf_(f"sb_pT{kc}")])
                    for j in range(8):
                        pv, pv_b = PV[j % 2]
                        mms = []
                        for kc in range(8):
                            for c0, c1 in ((0, 512), (512, 1024), (1024, 1025)):
                                mms.append(lambda e_, kc=kc, c0=c0, c1=c1, j=j, par=par: e_.matmul(
                                    pv[:, c0:c1],
                                    pTt[:, kc * 1024 + j * 128: kc * 1024 + (j + 1) * 128],
                                    vbuf[par][:, kc * 1025 + c0: kc * 1025 + c1],
                                    start=(kc == 0), stop=(kc == 7), skip_group_check=True))
                        ctx.ops("tensor", mms,
                                reads=[bf_(f"sb_pT{kc}") for kc in range(8)] + [vB, vOnes],
                                writes=pv_b)
                        if s == 0:
                            ctx.op("vector", lambda e_, j=j, pv=pv: e_.tensor_copy(
                                outsb[:, j * 1025:(j + 1) * 1025], pv[:, 0:1025]),
                                reads=pv_b, writes=[bf_(f"sb_outsb{j}")])
                        else:
                            ctx.op("vector", lambda e_, j=j, pv=pv: e_.tensor_tensor(
                                outsb[:, j * 1025:(j + 1) * 1025],
                                outsb[:, j * 1025:(j + 1) * 1025], pv[:, 0:1025], ALU.add),
                                reads=pv_b + [bf_(f"sb_outsb{j}")], writes=[bf_(f"sb_outsb{j}")])
                for j in range(8):
                    oj = bf_(f"sb_outsb{j}")
                    xt = xown[:, j * 1024:(j + 1) * 1024]
                    ctx.op("vector", lambda e_, j=j: e_.reciprocal(
                        rec[:], outsb[:, j * 1025 + 1024: j * 1025 + 1025]),
                        reads=[oj], writes=[bf_("b_rec")])
                    ctx.op("vector", lambda e_, j=j: e_.scalar_tensor_tensor(
                        xown[:, j * 1024:(j + 1) * 1024], outsb[:, j * 1025: j * 1025 + 1024],
                        rec[:], xown[:, j * 1024:(j + 1) * 1024], ALU.mult, ALU.add),
                        reads=[oj, bf_("b_rec"), bf_("sb_xown")], writes=[bf_("sb_xown")])
                    ctx.op("vector", lambda e_, xt=xt, j=j: e_.bn_stats(
                        st96b[:, j * 12: j * 12 + 6], xt[:, 0:512]),
                        reads=[bf_("sb_xown")], writes=[bf_("ln2_st96")])
                    ctx.op("vector", lambda e_, xt=xt, j=j: e_.bn_stats(
                        st96b[:, j * 12 + 6: j * 12 + 12], xt[:, 512:1024]),
                        reads=[bf_("sb_xown")], writes=[bf_("ln2_st96")])
                bRC, bMC = emit_ln8(ctx, bf_, [None] * 8, st96b, sq32b, agg8b, rstd8b, mcol8b,
                                    cf32, CF, "ln2_")
                for j in range(8):
                    xt = xown[:, j * 1024:(j + 1) * 1024]
                    ctx.op("scalar", lambda e_, xt=xt, j=j: e_.activation(
                        t2st[:, (j % 4) * 1024:(j % 4 + 1) * 1024], xt, AF.Identity,
                        bias=mcol8b[:, j:j + 1], scale=rstd8b[:, j:j + 1]),
                        reads=[bf_("sb_xown"), bMC, bRC], writes=[bf_("sb_t2st")])
                    ctx.op("vector", lambda e_, xt=xt, j=j: e_.tensor_scalar(
                        t2f[:], xt, rstd8b[:, j:j + 1], mcol8b[:, j:j + 1], ALU.mult, op1=ALU.add),
                        reads=[bf_("sb_xown"), bMC, bRC], writes=[bf_("b_t2f")])
                    ctx.op("vector", lambda e_, xt=xt: e_.tensor_tensor(
                        xt, xt, t2f[:], ALU.add),
                        reads=[bf_("sb_xown"), bf_("b_t2f")], writes=[bf_("sb_xown")])
                    if j % 4 == 3:
                        t23d = split_pa(t2_sp)
                        ctx.dma("gpsimd", "qG",
                                t23d[qb * 8 + (j - 3): qb * 8 + j + 1].rearrange("a p d -> p a d"),
                                t2st[:].rearrange("p (a d) -> p a d", d=1024),
                                reads=[bf_("sb_t2st")], writes=[bf_(f"dr_t2_sp{qb}")])
                x13d = split_pa(x1_sp)
                ctx.dma("gpsimd", "qG", x13d[qb * 8:(qb + 1) * 8].rearrange("a p d -> p a d"),
                        xown[:].rearrange("p (a d) -> p a d", d=1024),
                        reads=[bf_("sb_xown")], writes=[bf_(f"dr_x1_sp{qb}")])

        ctx.barrier_all()
        if upto == "p3":
         with ExitStack() as es3:
            wdT = sb("wdT", (128, 8 * 1024), BF16, es3)
            t2T = [sb(f"t2T{i}", (128, 8 * 512), BF16, es3) for i in range(2)]
            x1b = [sb(f"x1b{i}", (128, 4 * 1024), F32, es3) for i in range(2)]
            pf0 = sb("pf0", (128, 6 * 512), BF16, es3)
            pfs = sb("pfs", (128, 6 * 512), BF16, es3)
            yT = sb("yT", (128, 8 * 512), BF16, es3)
            outst = sb("outst", (128, 4 * 1024), F32, es3)
            lnS = sb("lnS", (6, 512), F32, es3)
            maskT = sb("maskT", (6, 512), F32, es3)
            lnG = sb("lnG", (6, 512), F32, es3)
            mx1 = sb("mx1", (128, 1), F32, es3)
            masktok = sb("masktok", (128, 6), F32, es3)

            for dc in range(8):
                ctx.dma("scalar", "qA", wdT[:, dc * 1024:(dc + 1) * 1024],
                        dt["wd_bf"][0:1024, dc * 128:(dc + 1) * 128],
                        reads=[bf_("dr_wd_bf")], writes=[bf_("sb_wdT")], transpose=True)

            for nb in range(NB3):
                par = nb % 2
                qb = nb // 2
                t2Tb = bf_(f"sb_t2T{par}")
                x1B = bf_(f"sb_x1b{par}")
                for cc in range(8):
                    ctx.dma("scalar", "qA", t2T[par][:, cc * 512:(cc + 1) * 512],
                            dt["t2_sp"][nb * 512:(nb + 1) * 512, cc * 128:(cc + 1) * 128],
                            reads=[bf_(f"dr_t2_sp{qb}")], writes=[t2Tb], transpose=True)
                x13d = split_pa(x1_sp)
                ctx.dma("sync", "qS", x1b[par][:].rearrange("p (a d) -> p a d", d=1024),
                        x13d[nb * 4:(nb + 1) * 4].rearrange("a p d -> p a d"),
                        reads=[bf_(f"dr_x1_sp{qb}")], writes=[x1B])
                zslots = []
                for e in range(6):
                    zp = psum[:, e * 512:(e + 1) * 512]
                    zslots.append(zp)
                    mms = []
                    for cc in range(8):
                        mms.append(lambda e_, cc=cc, e0=e, zp=zp, par=par: e_.matmul(
                            zp[0:100, :],
                            tokfm[:, cc * 608 + e0 * 100: cc * 608 + (e0 + 1) * 100],
                            t2T[par][:, cc * 512:(cc + 1) * 512],
                            start=(cc == 0), stop=False, skip_group_check=True))
                    ctx.ops("tensor", mms, reads=[bf_("sb_tokfm"), t2Tb], writes=[pbank[e]])
                LG, LG_B = pslice(3072, 3584)
                MT, MT_B = pslice(3584, 4096)
                for nch in range(4):
                    mms = []
                    for cc in range(8):
                        mms.append(lambda e_, cc=cc, nch=nch, par=par: e_.matmul(
                            LG[:, 0:6],
                            t2T[par][:, cc * 512 + nch * 128: cc * 512 + (nch + 1) * 128],
                            wg_fm[:, cc * 6:(cc + 1) * 6],
                            start=(cc == 0), stop=(cc == 7), skip_group_check=True))
                    ctx.ops("tensor", mms, reads=[t2Tb, bf_("sb_wg_fm")], writes=LG_B)
                    ctx.op("vector", lambda e_: e_.tensor_reduce(
                        mx1[:], LG[:, 0:6], axis=AX.X, op=ALU.max),
                        reads=LG_B, writes=[bf_("b_mx1")])
                    ctx.op("vector", lambda e_: e_.tensor_scalar(
                        masktok[:], LG[:, 0:6], mx1[:], None, ALU.is_ge),
                        reads=LG_B + [bf_("b_mx1")], writes=[bf_("b_masktok")])
                    ctx.op("tensor", lambda e_: e_.transpose(
                        MT[0:6, 0:128], masktok[:], IDENT_F32[:, :]),
                        reads=[bf_("b_masktok"), CF], writes=MT_B)
                    ctx.op("vector", lambda e_, nch=nch: e_.tensor_copy(
                        maskT[:, nch * 128:(nch + 1) * 128], MT[0:6, 0:128]),
                        reads=MT_B, writes=[bf_("sb_maskT")])
                for e in range(6):
                    ctx.op("scalar", lambda e_, e0=e: e_.activation(
                        pf0[0:100, e0 * 512:(e0 + 1) * 512], zslots[e0][0:100, :], AF.Exp,
                        bias=cf32[0:100, 2:3], scale=1.0 / 32.0),
                        reads=[pbank[e], CF], writes=[bf_("sb_pf0")])
                mms = []
                for e in range(6):
                    mms.append(lambda e_, e0=e: e_.matmul(
                        MT[0:6, 128:128 + 512] if False else MT[0:6, 0:512],
                        eind[:, e0 * 6:(e0 + 1) * 6],
                        pf0[0:100, e0 * 512:(e0 + 1) * 512],
                        start=(e0 == 0), stop=(e0 == 5), skip_group_check=True))
                ctx.ops("tensor", mms, reads=[bf_("sb_pf0"), EI, bf_("sb_maskT")], writes=MT_B)
                ctx.op("scalar", lambda e_: e_.activation(lnS[:], MT[0:6, 0:512], AF.Ln,
                                                          bias=cf32[0:6, 2:3], scale=1.0),
                       reads=MT_B + [CF], writes=[bf_("b_lnS")])
                ctx.op("vector", lambda e_: e_.tensor_scalar(lnS[:], lnS[:], -32.0, 2560.0,
                                                             ALU.mult, op1=ALU.add),
                       reads=[bf_("b_lnS")], writes=[bf_("b_lnS")])
                ctx.op("vector", lambda e_: e_.tensor_tensor(lnG[:], lnS[:], maskT[:], ALU.mult),
                       reads=[bf_("b_lnS"), bf_("sb_maskT")], writes=[bf_("b_lnG")])
                ctx.op("vector", lambda e_: e_.tensor_scalar(lnG[:], lnG[:], -2560.0, None, ALU.add),
                       reads=[bf_("b_lnG")], writes=[bf_("b_lnG")])
                for e in range(6):
                    ctx.op("tensor", lambda e_, e0=e: e_.matmul(
                        zslots[e0][0:100, :], selm[:, e0 * 100:(e0 + 1) * 100], lnG[:],
                        start=False, stop=True, skip_group_check=True),
                        reads=[bf_("b_lnG"), SM_], writes=[pbank[e]])
                    ctx.op("scalar", lambda e_, e0=e: e_.activation(
                        pfs[0:100, e0 * 512:(e0 + 1) * 512], zslots[e0][0:100, :], AF.Exp,
                        bias=cf32[0:100, 2:3], scale=1.0 / 32.0),
                        reads=[pbank[e], CF], writes=[bf_("sb_pfs")])
                for dc in range(8):
                    hp, hp_b = pslice((dc % 2) * 512, (dc % 2) * 512 + 512)
                    mms = []
                    for e in range(6):
                        mms.append(lambda e_, e0=e, dc=dc, hp=hp: e_.matmul(
                            hp[:], val_s[e0][0:100, dc * 128:(dc + 1) * 128],
                            pfs[0:100, e0 * 512:(e0 + 1) * 512],
                            start=(e0 == 0), stop=(e0 == 5), skip_group_check=True))
                    ctx.ops("tensor", mms,
                            reads=[bf_("sb_pfs")] + [bf_(f"sb_val{e}") for e in range(6)],
                            writes=hp_b)
                    ctx.op("vector", lambda e_, dc=dc, hp=hp, par=par: e_.tensor_tensor(
                        yT[:, dc * 512:(dc + 1) * 512], hp[:],
                        t2T[par][:, dc * 512:(dc + 1) * 512], ALU.add),
                        reads=hp_b + [t2Tb], writes=[bf_("sb_yT")])
                for ncs in range(4):
                    dp, dp_b = pslice(1024 + (ncs % 2) * 1024, 1024 + (ncs % 2) * 1024 + 1024)
                    mms = []
                    for dc in range(8):
                        for j in range(2):
                            mms.append(lambda e_, dc=dc, j=j, ncs=ncs, dp=dp: e_.matmul(
                                dp[:, j * 512:(j + 1) * 512],
                                yT[:, dc * 512 + ncs * 128: dc * 512 + (ncs + 1) * 128],
                                wdT[:, dc * 1024 + j * 512: dc * 1024 + (j + 1) * 512],
                                start=(dc == 0), stop=False, skip_group_check=True))
                    for j in range(2):
                        mms.append(lambda e_, j=j, dp=dp: e_.matmul(
                            dp[:, j * 512:(j + 1) * 512],
                            ONES_ROW_128, rows[0:1, 1024 + j * 512: 1024 + (j + 1) * 512],
                            start=False, stop=True, skip_group_check=True))
                    ctx.ops("tensor", mms, reads=[bf_("sb_yT"), bf_("sb_wdT"), CB, RW], writes=dp_b)
                    ctx.op("scalar", lambda e_, ncs=ncs, dp=dp: e_.activation(
                        outst[:, ncs * 1024:(ncs + 1) * 1024], dp[:], AF.Exp,
                        bias=BIAS_Z, scale=SCALE_COL),
                        reads=dp_b + [CF], writes=[bf_("sb_outst")])
                    ctx.op("scalar", lambda e_, ncs=ncs: e_.activation(
                        outst[:, ncs * 1024:(ncs + 1) * 1024],
                        outst[:, ncs * 1024:(ncs + 1) * 1024], AF.Ln,
                        bias=BIAS_Z, scale=1.0),
                        reads=[bf_("sb_outst"), CF], writes=[bf_("sb_outst")])
                    ctx.op("vector", lambda e_, ncs=ncs, par=par: e_.tensor_tensor(
                        outst[:, ncs * 1024:(ncs + 1) * 1024], outst[:, ncs * 1024:(ncs + 1) * 1024],
                        x1b[par][:, ncs * 1024:(ncs + 1) * 1024], ALU.add),
                        reads=[bf_("sb_outst"), x1B], writes=[bf_("sb_outst")])
                o3d = split_pa(out_d)
                ctx.dma("gpsimd", "qG", o3d[nb * 4:(nb + 1) * 4].rearrange("a p d -> p a d"),
                        outst[:].rearrange("p (a d) -> p a d", d=1024),
                        reads=[bf_("sb_outst")], writes=[bf_("dr_out")])

        ctx.barrier_all()
        for name in taps:
            ctx.dma("gpsimd", "qG", tap_d[name], dt[name],
                    reads=[b for n, b in bufs.items() if n.startswith("dr_" + name)])
        ctx.barrier_queue("sync", "qG")
        ctx.barrier_queue("sync", "qA")
        ctx.barrier_queue("sync", "qS")

    return nc


def kernel(**inputs):
    x = np.asarray(inputs["L0_attn_t"])
    W = np.ascontiguousarray(inputs["L0_attn_W_qkv"])
    b_qkv = np.asarray(inputs["L0_attn_b_qkv"])
    A = np.ascontiguousarray(inputs["L0_ffn_A"])
    Bm = np.ascontiguousarray(inputs["L0_ffn_B"])
    wg = np.ascontiguousarray(inputs["L0_ffn_w_gate"])
    wtw = np.ascontiguousarray(inputs["L0_ffn_wt_w"])
    wtb = np.asarray(inputs["L0_ffn_wt_b"])
    wdw = np.ascontiguousarray(inputs["L0_ffn_wd_w"])
    wdb = np.asarray(inputs["L0_ffn_wd_b"])
    scale = np.asarray(inputs["L0_ffn_scale"])

    cbf, selm, eind = _host_consts()
    bcol = np.ascontiguousarray(b_qkv[0:2048].reshape(16, 128).T)
    rows_bf = np.concatenate([wtb, wdb, b_qkv[2048:3072]])[None, :].astype(bf16)
    cf32 = np.zeros((128, 132), np.float32)
    cf32[:, 0] = -C_ATTN
    cf32[:, 1] = 1e-5
    cf32[:, 2] = 0.0
    cf32[:, 3] = scale[0]
    cf32[:, 4:132] = np.eye(128, dtype=np.float32)

    nc = build()
    in_maps = []
    for c in range(8):
        b, h = divmod(c, 2)
        xs = np.ascontiguousarray(np.roll(x[b], -h * HALF, axis=0)) if h else np.ascontiguousarray(x[b])
        in_maps.append({
            "x_seq": xs, "w_qkv": W, "bcol": bcol, "a_in": A, "b_in": Bm,
            "wgate": wg, "wt_w": wtw, "wd_w": wdw, "rows_bf": rows_bf,
            "cbf": cbf, "selm": selm, "eind": eind, "cf32": cf32,
        })
    res = run_bass_kernel_spmd(nc, in_maps, core_ids=list(range(8)))
    out = np.empty((B, N, D), np.float32)
    for c in range(8):
        b, h = divmod(c, 2)
        out[b, h * HALF:(h + 1) * HALF] = res.results[c]["out"]
    return out


# version 07
# speedup vs baseline: 1.0977x; 1.0637x over previous
import numpy as np
import ml_dtypes
import concourse.bass as bass
import concourse.mybir as mybir
from concourse.bass_utils import run_bass_kernel_spmd
from contextlib import ExitStack
from collections import defaultdict

F32 = mybir.dt.float32
BF16 = mybir.dt.bfloat16
AF = mybir.ActivationFunctionType
ALU = mybir.AluOpType
AX = mybir.AxisListType
bf16 = ml_dtypes.bfloat16

B, N, D = 4, 8192, 1024
HALF = N // 2
NSLAB = 8
SLAB = 1024
QB = 4
NB3 = 8
BLK3 = 512
C_ATTN = 24.0


class Buf:
    __slots__ = ("writer", "readers", "name")

    def __init__(self, name=""):
        self.writer = None
        self.readers = []
        self.name = name


class Ctx:
    ENGS = ("tensor", "vector", "scalar", "gpsimd", "sync")
    QUEUES = ("qS", "qA", "qG")

    def __init__(self, nc, es):
        self.nc = nc
        self.sem = {}
        for n in self.ENGS + self.QUEUES:
            self.sem[n] = es.enter_context(nc.semaphore("sem_" + n))
        self.cnt = defaultdict(int)
        self.waited = defaultdict(dict)

    def _wait(self, eng, src, val):
        if src == eng:
            return
        assert val <= self.cnt[src], ("wait-ahead deadlock", eng, src, val, self.cnt[src])
        w = self.waited[eng]
        if val > w.get(src, 0):
            getattr(self.nc, eng).wait_ge(self.sem[src], val)
            w[src] = val

    def _deps(self, eng, reads, writes):
        need = {}
        for b in reads:
            if b.writer:
                s, v = b.writer
                if need.get(s, 0) < v:
                    need[s] = v
        for b in writes:
            if b.writer:
                s, v = b.writer
                if need.get(s, 0) < v:
                    need[s] = v
            for s, v in b.readers:
                if need.get(s, 0) < v:
                    need[s] = v
        for s, v in need.items():
            self._wait(eng, s, v)

    def _finish(self, tag, c, reads, writes):
        for b in reads:
            b.readers.append((tag, c))
        for b in writes:
            b.writer = (tag, c)
            b.readers = []

    def _maybe_drain(self, eng, reads):
        if eng in ("vector", "scalar", "gpsimd"):
            c = self.cnt[eng]
            for b in reads:
                if b.writer and b.writer[0] == eng and b.writer[1] >= c - 1:
                    getattr(self.nc, eng).drain()
                    return

    def op(self, eng, fn, reads=(), writes=()):
        self._deps(eng, reads, writes)
        self._maybe_drain(eng, reads)
        ins = fn(getattr(self.nc, eng))
        self.cnt[eng] += 1
        ins.then_inc(self.sem[eng], 1)
        self._finish(eng, self.cnt[eng], reads, writes)

    def ops(self, eng, fns, reads=(), writes=()):
        self._deps(eng, reads, writes)
        self._maybe_drain(eng, reads)
        e = getattr(self.nc, eng)
        ins = None
        for fn in fns:
            ins = fn(e)
        self.cnt[eng] += 1
        ins.then_inc(self.sem[eng], 1)
        self._finish(eng, self.cnt[eng], reads, writes)

    def dma(self, eng, queue, out, in_, reads=(), writes=(), transpose=False):
        self._deps(eng, reads, writes)
        if eng in ("vector", "scalar", "gpsimd"):
            for b in reads:
                if b.writer and b.writer[0] == eng and b.writer[1] >= self.cnt[eng] - 4:
                    getattr(self.nc, eng).drain()
                    break
        getattr(self.nc, eng).dma_start(out=out, in_=in_, transpose=transpose) \
            .then_inc(self.sem[queue], 16)
        self.cnt[queue] += 16
        self._finish(queue, self.cnt[queue], reads, writes)

    def barrier_queue(self, eng, queue):
        self._wait(eng, queue, self.cnt[queue])

    def barrier_all(self):
        snap = {n: self.cnt[n] for n in self.ENGS + self.QUEUES}
        for e in self.ENGS:
            for n, v in snap.items():
                if n != e and v > 0:
                    self._wait(e, n, v)


def _host_consts():
    ident = np.eye(128, dtype=np.float32).astype(bf16)
    onesb = np.ones((128, 136), dtype=np.float32).astype(bf16)
    cbf = np.concatenate([ident, onesb], axis=1)
    selm = np.zeros((6, 600), dtype=np.float32)
    for e in range(6):
        selm[e, e * 100:(e + 1) * 100] = 1.0
    eind = np.zeros((100, 36), dtype=np.float32)
    for e in range(6):
        eind[:, e * 6 + e] = 1.0
    return cbf, selm.astype(np.float32), eind.astype(bf16)


def emit_ln8(ctx, bf_, tiles, st96, sq32, agg8, rstd8, mcol8, cf32, cfbuf, tag):
    V, S = "vector", "scalar"
    n = len(tiles)
    w = 12 * n
    bST = bf_(tag + "st96")
    bSM, bCV, bSQ = bf_(tag + "sm"), bf_(tag + "cv"), bf_(tag + "sq")
    bM2, bVAR, bRC, bMC = bf_(tag + "m2"), bf_(tag + "var"), bf_(tag + "rc"), bf_(tag + "mc")
    m = [st96[:, c:w:12] for c in (1, 4, 7, 10)]
    cv = [st96[:, c:w:12] for c in (2, 5, 8, 11)]
    SM8, CV8, M2 = agg8[:, 0:n], agg8[:, 8:8 + n], agg8[:, 24:24 + n]
    ctx.op(V, lambda e: e.tensor_tensor(SM8, m[0], m[1], ALU.add), reads=[bST], writes=[bSM])
    ctx.op(V, lambda e: e.tensor_tensor(agg8[:, 16:16 + n], m[2], m[3], ALU.add),
           reads=[bST], writes=[bSQ])
    ctx.op(V, lambda e: e.tensor_tensor(SM8, SM8, agg8[:, 16:16 + n], ALU.add),
           reads=[bSM, bSQ], writes=[bSM])
    ctx.op(V, lambda e: e.tensor_tensor(CV8, cv[0], cv[1], ALU.add), reads=[bST], writes=[bCV])
    ctx.op(V, lambda e: e.tensor_tensor(agg8[:, 16:16 + n], cv[2], cv[3], ALU.add),
           reads=[bST], writes=[bSQ])
    ctx.op(V, lambda e: e.tensor_tensor(CV8, CV8, agg8[:, 16:16 + n], ALU.add),
           reads=[bCV, bSQ], writes=[bCV])
    for i in range(4):
        ctx.op(V, lambda e, i=i: e.tensor_tensor(sq32[:, 8 * i:8 * i + n], m[i], m[i], ALU.mult),
               reads=[bST], writes=[bSQ])
    ctx.op(V, lambda e: e.tensor_tensor(sq32[:, 0:n], sq32[:, 0:n], sq32[:, 8:8 + n], ALU.add),
           reads=[bSQ], writes=[bSQ])
    ctx.op(V, lambda e: e.tensor_tensor(sq32[:, 16:16 + n], sq32[:, 16:16 + n], sq32[:, 24:24 + n],
                                        ALU.add),
           reads=[bSQ], writes=[bSQ])
    ctx.op(V, lambda e: e.tensor_tensor(sq32[:, 0:n], sq32[:, 0:n], sq32[:, 16:16 + n], ALU.add),
           reads=[bSQ], writes=[bSQ])
    ctx.op(V, lambda e: e.tensor_scalar(SM8, SM8, 0.25, None, ALU.mult), reads=[bSM], writes=[bSM])
    ctx.op(V, lambda e: e.tensor_tensor(M2, SM8, SM8, ALU.mult), reads=[bSM], writes=[bM2])
    ctx.op(V, lambda e: e.tensor_scalar(CV8, CV8, 1.0 / 1024.0, None, ALU.mult),
           reads=[bCV], writes=[bVAR])
    ctx.op(V, lambda e: e.scalar_tensor_tensor(CV8, sq32[:, 0:n], 0.25, CV8, ALU.mult, ALU.add),
           reads=[bSQ, bVAR], writes=[bVAR])
    ctx.op(V, lambda e: e.tensor_tensor(CV8, CV8, M2, ALU.subtract), reads=[bVAR, bM2], writes=[bVAR])
    ctx.op(S, lambda e: e.activation(rstd8[:, 0:n], CV8, AF.Ln, bias=cf32[:, 1:2], scale=1.0),
           reads=[bVAR, cfbuf], writes=[bRC])
    ctx.op(S, lambda e: e.activation(rstd8[:, 0:n], rstd8[:, 0:n], AF.Exp,
                                     bias=cf32[:, 2:3], scale=-0.5),
           reads=[bRC, cfbuf], writes=[bRC])
    ctx.op(V, lambda e: e.tensor_scalar(mcol8[:, 0:n], SM8, -1.0, None, ALU.mult),
           reads=[bSM], writes=[bMC])
    ctx.op(V, lambda e: e.tensor_tensor(mcol8[:, 0:n], mcol8[:, 0:n], rstd8[:, 0:n], ALU.mult),
           reads=[bMC, bRC], writes=[bMC])
    return bRC, bMC


def emit_ln(ctx, bf_, xt, xbuf, sbufs, tag):
    st6, st2, tmp4, tmp1, mean2, mcol, rcol, cf32 = sbufs
    V, S = "vector", "scalar"
    bST6, bSM, bSCV, bT4, bSM2 = bf_(tag + "st6"), bf_(tag + "sm"), bf_(tag + "scv"), bf_(tag + "t4"), bf_(tag + "sm2")
    bMEAN, bM2, bVAR, bRC, bMC = bf_(tag + "mean"), bf_(tag + "m2"), bf_(tag + "var"), bf_(tag + "rc"), bf_(tag + "mc")
    ctx.op(V, lambda e: e.bn_stats(st6[:, 0:6], xt[:, 0:512]), reads=[xbuf], writes=[bST6])
    ctx.op(V, lambda e: e.bn_stats(st6[:, 6:12], xt[:, 512:1024]), reads=[xbuf], writes=[bST6])
    ctx.op(V, lambda e: e.reduce_sum(st2[:, 0:1], st6[:, 1:12:3], axis=AX.X), reads=[bST6], writes=[bSM])
    ctx.op(V, lambda e: e.reduce_sum(st2[:, 1:2], st6[:, 2:12:3], axis=AX.X), reads=[bST6], writes=[bSCV])
    ctx.op(V, lambda e: e.tensor_tensor(tmp4[:], st6[:, 1:12:3], st6[:, 1:12:3], ALU.mult),
           reads=[bST6], writes=[bT4])
    ctx.op(V, lambda e: e.reduce_sum(tmp1[:], tmp4[:], axis=AX.X), reads=[bT4], writes=[bSM2])
    ctx.op(V, lambda e: e.tensor_scalar(st2[:, 0:1], st2[:, 0:1], 0.25, None, ALU.mult),
           reads=[bSM], writes=[bMEAN])
    ctx.op(V, lambda e: e.tensor_tensor(mean2[:], st2[:, 0:1], st2[:, 0:1], ALU.mult),
           reads=[bMEAN], writes=[bM2])
    ctx.op(V, lambda e: e.tensor_scalar(st2[:, 1:2], st2[:, 1:2], 1.0 / 1024.0, None, ALU.mult),
           reads=[bSCV], writes=[bVAR])
    ctx.op(V, lambda e: e.scalar_tensor_tensor(st2[:, 1:2], tmp1[:], 0.25, st2[:, 1:2], ALU.mult, ALU.add),
           reads=[bSM2, bVAR], writes=[bVAR])
    ctx.op(V, lambda e: e.tensor_tensor(st2[:, 1:2], st2[:, 1:2], mean2[:], ALU.subtract),
           reads=[bVAR, bM2], writes=[bVAR])
    ctx.op(S, lambda e: e.activation(rcol[:], st2[:, 1:2], AF.Ln, bias=cf32[:, 1:2], scale=1.0),
           reads=[bVAR, bf_("sb_cf32_s")], writes=[bRC])
    ctx.op(S, lambda e: e.activation(rcol[:], rcol[:], AF.Exp, bias=cf32[:, 2:3], scale=-0.5),
           reads=[bRC, bf_("sb_cf32_s")], writes=[bRC])
    ctx.op(V, lambda e: e.tensor_scalar(mcol[:], st2[:, 0:1], rcol[:], -1.0, ALU.mult, op1=ALU.mult),
           reads=[bMEAN, bRC], writes=[bMC])
    return bRC, bMC


def split_pa(ap, p=128):
    return ap.rearrange("(a p) d -> a p d", p=p)


def build(upto='p3', taps=()):
    nc = bass.Bass("TRN2", target_bir_lowering=False)
    dt = {}

    def din(name, shape, dtype):
        dt[name] = nc.dram_tensor(name, list(shape), dtype, kind="ExternalInput").ap()
        return dt[name]

    def dout(name, shape, dtype):
        dt[name] = nc.dram_tensor(name, list(shape), dtype, kind="ExternalOutput").ap()
        return dt[name]

    def dint(name, shape, dtype):
        dt[name] = nc.dram_tensor(name, list(shape), dtype, kind="Internal").ap()
        return dt[name]

    x_seq = din("x_seq", (N, D), F32)
    w_qkv = din("w_qkv", (3072, D), F32)
    bcol_d = din("bcol", (128, 16), F32)
    a_in = din("a_in", (6, 100, 16), F32)
    b_in = din("b_in", (6, 16, D), F32)
    wgate = din("wgate", (D, 6), F32)
    wt_w = din("wt_w", (D, D), F32)
    wd_w = din("wd_w", (D, D), F32)
    rows_d = din("rows_bf", (1, 3072), BF16)
    cbf_d = din("cbf", (128, 264), BF16)
    selm_d = din("selm", (6, 600), F32)
    eind_d = din("eind", (100, 36), BF16)
    cf32_d = din("cf32", (128, 132), F32)
    out_d = dout("out", (HALF, D), F32)

    t_sp = dint("t_sp", (N, D), BF16)
    t2_sp = dint("t2_sp", (HALF, D), BF16)
    x1_sp = dint("x1_sp", (HALF, D), F32)
    k_sp = dint("k_sp", (NSLAB, 8, 128, SLAB), BF16)
    v_sp = dint("v_sp", (NSLAB, 8, 128, SLAB), BF16)
    q_sp = dint("q_sp", (QB, 8, 128, SLAB), BF16)
    wq_bf = dint("wq_bf", (3072, D), BF16)
    wt_bf = dint("wt_bf", (D, D), BF16)
    wd_bf = dint("wd_bf", (D, D), BF16)
    tok_sp = dint("tok_sp", (608, D), BF16)

    tap_d = {}
    for name in taps:
        t = dt[name].tensor
        tap_d[name] = dout("tap_" + name, list(t.shape), t.dtype)

    bufs = {}

    def bf_(name):
        if name not in bufs:
            bufs[name] = Buf(name)
        return bufs[name]

    es_prog = ExitStack()
    with es_prog:
        ctx = Ctx(nc, es_prog)
        blk = es_prog.enter_context(nc.Block())
        pers = ExitStack()
        es_prog.enter_context(pers)

        def sb(name, shape, dtype, stack=None):
            return (stack or pers).enter_context(nc.sbuf_tensor(name, list(shape), dtype))

        cbf = sb("cbf_s", (128, 264), BF16)
        rows = sb("rows_s", (1, 3072), BF16)
        bcol = sb("bcol_s", (128, 16), F32)
        cf32 = sb("cf32_s", (128, 132), F32)
        selm = sb("selm_s", (6, 600), F32)
        eind = sb("eind_s", (100, 36), BF16)
        wg_fm = sb("wg_fm", (128, 48), BF16)
        tokfm = sb("tokfm", (128, 8 * 608), BF16)
        val_s = [sb(f"val{e}", (128, D), BF16) for e in range(6)]

        IDENT_F32 = cf32[:, 4:132]
        ONES_ROW_128 = cbf[0:1, 128:256]
        ONES_ROW_100 = cbf[0:1, 128:228]
        BIAS_NC = cf32[:, 0:1]
        BIAS_EPS = cf32[:, 1:2]
        BIAS_Z = cf32[:, 2:3]
        SCALE_COL = cf32[:, 3:4]

        psum = es_prog.enter_context(nc.psum_tensor("psA", [128, 4096], F32))
        pbank = [bf_(f"pbank{i}") for i in range(8)]

        def pslice(c0, c1):
            return psum[:, c0:c1], [pbank[i] for i in range(c0 // 512, (c1 + 511) // 512)]

        for dstt, src in ((cbf, cbf_d), (rows, rows_d), (bcol, bcol_d),
                          (cf32, cf32_d), (selm, selm_d), (eind, eind_d)):
            ctx.dma("sync", "qS", dstt[:], src, writes=[bf_("sb_" + dstt.name)])
        CB, RW, BC, CF, SM_, EI = [bufs["sb_" + t.name] for t in (cbf, rows, bcol, cf32, selm, eind)]

        ctx.dma("gpsimd", "qG", dt["wq_bf"][:], w_qkv, writes=[bf_("dr_wq_bf")])
        ctx.dma("gpsimd", "qG", dt["wt_bf"][:], wt_w, writes=[bf_("dr_wt_bf")])
        ctx.dma("gpsimd", "qG", dt["wd_bf"][:], wd_w, writes=[bf_("dr_wd_bf")])

        es0 = ExitStack()
        wg_f32 = sb("wg_f32", (128, 48), F32, es0)
        for cc in range(8):
            ctx.dma("sync", "qS", wg_f32[:, cc * 6:(cc + 1) * 6],
                    wgate[cc * 128:(cc + 1) * 128, :], writes=[bf_("sb_wg_f32")])
        ctx.op("vector", lambda e: e.tensor_copy(wg_fm[:], wg_f32[:]),
               reads=[bf_("sb_wg_f32")], writes=[bf_("sb_wg_fm")])

        ctx.barrier_all()
        if upto in ("p1", "p2", "p3"):
         with ExitStack() as es1:
            wTall = sb("wTall", (128, 8 * 3072), BF16, es1)
            xblk = [sb(f"xblk{i}", (128, 2 * 1024), F32, es1) for i in range(2)]
            tstage = sb("tstage", (128, 4 * 1024), BF16, es1)
            tT = [sb(f"tT{i}", (128, 8 * 1024), BF16, es1) for i in range(3)]
            kstage = sb("kstage", (128, 8 * 1024), BF16, es1)
            vstage = sb("vstage", (128, 8 * 1024), BF16, es1)
            qstage = sb("qstage", (128, 8 * 1024), BF16, es1)
            st96 = sb("st96", (128, 96), F32, es1)
            sq32a = sb("sq32a", (128, 64), F32, es1)
            agg8a = sb("agg8a", (128, 64), F32, es1)
            rstd8a = sb("rstd8a", (128, 8), F32, es1)
            mcol8a = sb("mcol8a", (128, 8), F32, es1)

            for dc in range(8):
                ctx.dma("scalar", "qA", wTall[:, dc * 3072:(dc + 1) * 3072],
                        dt["wq_bf"][0:3072, dc * 128:(dc + 1) * 128],
                        reads=[bf_("dr_wq_bf")], writes=[bf_("sb_wTall")], transpose=True)

            def WT(dc, c0, c1):
                return wTall[:, dc * 3072 + c0: dc * 3072 + c1]

            x3d = split_pa(x_seq)
            rot = 0

            def p1_chain(s):
                xb = [bf_("b_xblk0"), bf_("b_xblk1")]
                for half in range(2):
                    tile_list = []
                    for q2 in (2 * half, 2 * half + 1):
                        par2 = q2 % 2
                        ctx.dma("sync", "qS",
                                xblk[par2][:].rearrange("p (a d) -> p a d", d=1024),
                                x3d[8 * s + 2 * q2: 8 * s + 2 * q2 + 2].rearrange("a p d -> p a d"),
                                reads=[bf_("dr_x_seq")], writes=[xb[par2]])
                        for th in range(2):
                            t8 = 2 * q2 + th
                            xt = xblk[par2][:, th * 1024:(th + 1) * 1024]
                            ctx.op("vector", lambda e_, xt=xt, t8=t8: e_.bn_stats(
                                st96[:, t8 * 12: t8 * 12 + 6], xt[:, 0:512]),
                                reads=[xb[par2]], writes=[bf_(f"ln1_{half}st96")])
                            ctx.op("vector", lambda e_, xt=xt, t8=t8: e_.bn_stats(
                                st96[:, t8 * 12 + 6: t8 * 12 + 12], xt[:, 512:1024]),
                                reads=[xb[par2]], writes=[bf_(f"ln1_{half}st96")])
                            tile_list.append((xt, xb[par2]))
                    bRC, bMC = emit_ln8(ctx, bf_, tile_list, st96[:, 48 * half:48 * half + 48],
                                        sq32a[:, 32 * half:32 * half + 32],
                                        agg8a[:, 32 * half:32 * half + 32],
                                        rstd8a[:, 4 * half:4 * half + 4],
                                        mcol8a[:, 4 * half:4 * half + 4],
                                        cf32, CF, f"ln1_{half}")
                    for ti, (xt, xbuf) in enumerate(tile_list):
                        t8 = 4 * half + ti
                        ctx.op("scalar", lambda e_, xt=xt, t8=t8, ti=ti, half=half: e_.activation(
                            tstage[:, ti * 1024:(ti + 1) * 1024], xt, AF.Identity,
                            bias=mcol8a[:, 4 * half + ti: 4 * half + ti + 1],
                            scale=rstd8a[:, 4 * half + ti: 4 * half + ti + 1]),
                            reads=[xbuf, bMC, bRC], writes=[bf_("sb_tstage")])
                    t3d = split_pa(t_sp)
                    ctx.dma("scalar", "qA",
                            t3d[s * 8 + 4 * half: s * 8 + 4 * half + 4].rearrange("a p d -> p a d"),
                            tstage[:].rearrange("p (a d) -> p a d", d=1024),
                            reads=[bf_("sb_tstage")], writes=[bf_(f"dr_t_sp{s}")])
                par = s % 3
                tTb = bf_(f"sb_tT{par}")
                for dc in range(8):
                    eng, q = (("scalar", "qA") if dc % 2 == 0 else ("sync", "qS"))
                    ctx.dma(eng, q, tT[par][:, dc * 1024:(dc + 1) * 1024],
                            dt["t_sp"][s * 1024:(s + 1) * 1024, dc * 128:(dc + 1) * 128],
                            reads=[bf_(f"dr_t_sp{s}")], writes=[tTb], transpose=True)
                return tTb

            def p1_mms(s):
                nonlocal rot
                par = s % 3
                tTb = bf_(f"sb_tT{par}")
                for ec in range(8):
                    pm, pm_b = pslice((rot % 2) * 1024, (rot % 2) * 1024 + 1024); rot += 1
                    mms = []
                    for dc in range(8):
                        for j in range(2):
                            mms.append(lambda e_, dc=dc, j=j, ec=ec, par=par, pm=pm: e_.matmul(
                                pm[:, j * 512:(j + 1) * 512],
                                WT(dc, (8 + ec) * 128, (9 + ec) * 128),
                                tT[par][:, dc * 1024 + j * 512: dc * 1024 + (j + 1) * 512],
                                start=(dc == 0), stop=(dc == 7), skip_group_check=True))
                    ctx.ops("tensor", mms, reads=[bf_("sb_wTall"), tTb], writes=pm_b)
                    ctx.op("scalar", lambda e_, ec=ec, pm=pm: e_.activation(
                        kstage[:, ec * 1024:(ec + 1) * 1024], pm[:], AF.Identity,
                        bias=bcol[:, 8 + ec:9 + ec], scale=1.0),
                        reads=pm_b + [BC], writes=[bf_("sb_kstage")])
                ctx.dma("gpsimd", "qG", dt["k_sp"][s].rearrange("a p d -> p a d"),
                        kstage[:].rearrange("p (a d) -> p a d", d=1024),
                        reads=[bf_("sb_kstage")], writes=[bf_(f"dr_k_sp{s}")])
                for kt in range(8):
                    pm, pm_b = pslice((rot % 2) * 1024, (rot % 2) * 1024 + 1024); rot += 1
                    mms = []
                    for dc in range(8):
                        for j in range(2):
                            mms.append(lambda e_, dc=dc, j=j, kt=kt, par=par, pm=pm: e_.matmul(
                                pm[:, j * 512:(j + 1) * 512],
                                tT[par][:, dc * 1024 + kt * 128: dc * 1024 + (kt + 1) * 128],
                                WT(dc, 2048 + j * 512, 2048 + (j + 1) * 512),
                                start=(dc == 0), stop=False, skip_group_check=True))
                    for j in range(2):
                        mms.append(lambda e_, j=j, pm=pm: e_.matmul(
                            pm[:, j * 512:(j + 1) * 512],
                            ONES_ROW_128, rows[0:1, 2048 + j * 512: 2048 + (j + 1) * 512],
                            start=False, stop=True, skip_group_check=True))
                    ctx.ops("tensor", mms, reads=[bf_("sb_wTall"), tTb, CB, RW], writes=pm_b)
                    ctx.op("vector", lambda e_, kt=kt, pm=pm: e_.tensor_copy(
                        vstage[:, kt * 1024:(kt + 1) * 1024], pm[:]),
                        reads=pm_b, writes=[bf_("sb_vstage")])
                ctx.dma("gpsimd", "qG", dt["v_sp"][s].rearrange("a p d -> p a d"),
                        vstage[:].rearrange("p (a d) -> p a d", d=1024),
                        reads=[bf_("sb_vstage")], writes=[bf_(f"dr_v_sp{s}")])
                if s < QB:
                    for ec in range(8):
                        pm, pm_b = pslice((rot % 2) * 1024, (rot % 2) * 1024 + 1024); rot += 1
                        mms = []
                        for dc in range(8):
                            for j in range(2):
                                mms.append(lambda e_, dc=dc, j=j, ec=ec, par=par, pm=pm: e_.matmul(
                                    pm[:, j * 512:(j + 1) * 512],
                                    WT(dc, ec * 128, (ec + 1) * 128),
                                    tT[par][:, dc * 1024 + j * 512: dc * 1024 + (j + 1) * 512],
                                    start=(dc == 0), stop=(dc == 7), skip_group_check=True))
                        ctx.ops("tensor", mms, reads=[bf_("sb_wTall"), tTb], writes=pm_b)
                        ctx.op("vector", lambda e_, ec=ec, pm=pm: e_.tensor_scalar(
                            qstage[:, ec * 1024:(ec + 1) * 1024], pm[:], bcol[:, ec:ec + 1], None,
                            ALU.add),
                            reads=pm_b + [BC], writes=[bf_("sb_qstage")])
                    ctx.dma("gpsimd", "qG", dt["q_sp"][s].rearrange("a p d -> p a d"),
                            qstage[:].rearrange("p (a d) -> p a d", d=1024),
                            reads=[bf_("sb_qstage")], writes=[bf_(f"dr_q_sp{s}")])

            p1_chain(0)
            p1_chain(1)
            for s in range(NSLAB):
                if s + 2 < NSLAB:
                    p1_chain(s + 2)
                p1_mms(s)

        if upto in ("p1", "p2", "p3"):
            ab_f32 = sb("ab_f32", (128, 1040), F32, es0)
            ab_bf = sb("ab_bf", (128, 1040), BF16, es0)
            aT = [sb(f"aT{e}", (16, 100), BF16, es0) for e in range(6)]
            tokstage = sb("tokstage", (128, 1024), BF16, es0)

            ctx.op("vector", lambda e: e.memset(ab_f32[:], 0.0), writes=[bf_("sb_ab_f32")])
            for e in range(6):
                ctx.dma("sync", "qS", ab_f32[0:100, 0:16], a_in[e], writes=[bf_("sb_ab_f32")])
                ctx.dma("sync", "qS", ab_f32[0:16, 16:1040], b_in[e], writes=[bf_("sb_ab_f32")])
                ctx.op("vector", lambda e_: e_.tensor_copy(ab_bf[:], ab_f32[:]),
                       reads=[bf_("sb_ab_f32")], writes=[bf_("sb_ab_bf")])
                pA, pA_b = pslice(0, 512)
                ctx.op("tensor", lambda e_: e_.transpose(pA[0:16, 0:100], ab_f32[0:100, 0:16],
                                                         IDENT_F32[0:100, 0:100]),
                       reads=[bf_("sb_ab_f32"), CF], writes=pA_b)
                ctx.op("vector", lambda e_, e0=e: e_.tensor_copy(aT[e0][:], pA[0:16, 0:100]),
                       reads=pA_b, writes=[bf_(f"sb_aT{e}")])
                pT, pT_b = pslice(1024, 2048)
                ctx.ops("tensor",
                        [(lambda e_, j=j, e0=e: e_.matmul(pT[0:100, j * 512:(j + 1) * 512],
                                                          aT[e0][:], ab_bf[0:16, 16 + j * 512:16 + (j + 1) * 512],
                                                          start=True, stop=True))
                         for j in range(2)],
                        reads=[bf_(f"sb_aT{e}"), bf_("sb_ab_bf")], writes=pT_b)
                ctx.op("vector", lambda e_: e_.tensor_copy(tokstage[0:100, :], pT[0:100, :]),
                       reads=pT_b, writes=[bf_("sb_tokstage")])
                ctx.dma("gpsimd", "qG", dt["tok_sp"][e * 100:(e + 1) * 100, :], tokstage[0:100, :],
                        reads=[bf_("sb_tokstage")], writes=[bf_("dr_tok_sp")])
            ctx.op("vector", lambda e_: e_.memset(tokstage[0:8, :], 0.0), writes=[bf_("sb_tokstage")])
            ctx.dma("gpsimd", "qG", dt["tok_sp"][600:608, :], tokstage[0:8, :],
                    reads=[bf_("sb_tokstage")], writes=[bf_("dr_tok_sp")])
            for cc in range(8):
                ctx.dma("scalar", "qA", tokfm[:, cc * 608:(cc + 1) * 608],
                        dt["tok_sp"][0:608, cc * 128:(cc + 1) * 128],
                        reads=[bf_("dr_tok_sp")], writes=[bf_("sb_tokfm")], transpose=True)

            es0.close()
            with ExitStack() as es_val:
                wtT = sb("wtT", (128, 8 * 1024), BF16, es_val)
                for cc in range(8):
                    ctx.dma("scalar", "qA", wtT[:, cc * 1024:(cc + 1) * 1024],
                            dt["wt_bf"][0:1024, cc * 128:(cc + 1) * 128],
                            reads=[bf_("dr_wt_bf")], writes=[bf_("sb_wtT")], transpose=True)
                for e in range(6):
                    pV, pV_b = pslice(0, 1024)
                    mms = []
                    for cc in range(8):
                        for j in range(2):
                            mms.append(lambda e_, cc=cc, j=j, e0=e: e_.matmul(
                                pV[0:100, j * 512:(j + 1) * 512],
                                tokfm[:, cc * 608 + e0 * 100: cc * 608 + (e0 + 1) * 100],
                                wtT[:, cc * 1024 + j * 512: cc * 1024 + (j + 1) * 512],
                                start=(cc == 0), stop=False, skip_group_check=True))
                    for j in range(2):
                        mms.append(lambda e_, j=j: e_.matmul(
                            pV[0:100, j * 512:(j + 1) * 512],
                            ONES_ROW_100, rows[0:1, j * 512:(j + 1) * 512],
                            start=False, stop=True, skip_group_check=True))
                    ctx.ops("tensor", mms, reads=[bf_("sb_tokfm"), bf_("sb_wtT"), CB, RW], writes=pV_b)
                    ctx.op("vector", lambda e_, e0=e: e_.tensor_copy(val_s[e0][0:100, :], pV[0:100, :]),
                           reads=pV_b, writes=[bf_(f"sb_val{e}")])
                    ctx.op("vector", lambda e_, e0=e: e_.memset(val_s[e0][0:1, :], 0.0),
                           writes=[bf_(f"sb_val{e}")])

        ctx.barrier_all()
        if upto in ("p2", "p3"):
         with ExitStack() as es2:
            qbuf = sb("qbuf", (128, 8 * 1024), BF16, es2)
            kbuf = [sb(f"kbuf{i}", (128, 8 * 1024), BF16, es2) for i in range(2)]
            vbuf = [sb(f"vbuf{i}", (128, 8 * 1025), BF16, es2) for i in range(2)]
            pTt = sb("pTt", (128, 8 * 1024), BF16, es2)
            outsb = sb("outsb", (128, 8 * 1025), F32, es2)
            xown = sb("xown", (128, 8 * 1024), F32, es2)
            t2st = sb("t2st", (128, 4 * 1024), BF16, es2)
            st96b = sb("st96b", (128, 96), F32, es2)
            sq32b = sb("sq32b", (128, 32), F32, es2)
            agg8b = sb("agg8b", (128, 32), F32, es2)
            rstd8b = sb("rstd8b", (128, 8), F32, es2)
            mcol8b = sb("mcol8b", (128, 8), F32, es2)
            rec = sb("rec", (128, 1), F32, es2)
            rp1b = sb("rp1b", (128, 8), F32, es2)

            for i in range(2):
                ctx.ops("vector",
                        [(lambda e_, i=i, t=t: e_.memset(vbuf[i][:, t * 1025 + 1024: t * 1025 + 1025], 1.0))
                         for t in range(8)],
                        writes=[bf_(f"sb_vbuf{i}_ones")])

            S_HALF = [pslice(0, 512), pslice(512, 1024)]
            PV = [pslice(1024, 2560), pslice(2560, 4096)]

            for qb in range(QB):
                qbB = bf_("sb_qbuf")
                ctx.dma("sync", "qS", qbuf[:].rearrange("p (a d) -> p a d", d=1024),
                        dt["q_sp"][qb].rearrange("a p d -> p a d"),
                        reads=[bf_(f"dr_q_sp{qb}")], writes=[qbB])
                for s in range(NSLAB):
                    if s == 2:
                        ctx.dma("sync", "qS", xown[:].rearrange("p (a d) -> p a d", d=1024),
                                x3d[qb * 8:(qb + 1) * 8].rearrange("a p d -> p a d"),
                                reads=[bf_("dr_x_seq")], writes=[bf_("sb_xown")])
                    par = s % 2
                    kB, vB = bf_(f"sb_kbuf{par}"), bf_(f"sb_vbuf{par}")
                    vOnes = bf_(f"sb_vbuf{par}_ones")
                    ctx.dma("sync", "qS", kbuf[par][:].rearrange("p (a d) -> p a d", d=1024),
                            dt["k_sp"][s].rearrange("a p d -> p a d"),
                            reads=[bf_(f"dr_k_sp{s}")], writes=[kB])
                    ctx.dma("sync", "qS",
                            vbuf[par][:].rearrange("p (a d) -> p a d", d=1025)[:, :, 0:1024],
                            dt["v_sp"][s].rearrange("a p d -> p a d"),
                            reads=[bf_(f"dr_v_sp{s}")], writes=[vB])
                    for kc in range(8):
                        for qh in range(2):
                            sp, sp_b = S_HALF[qh]
                            mms = []
                            for dc in range(8):
                                mms.append(lambda e_, dc=dc, qh=qh, kc=kc, par=par, sp=sp: e_.matmul(
                                    sp[:],
                                    kbuf[par][:, dc * 1024 + kc * 128: dc * 1024 + (kc + 1) * 128],
                                    qbuf[:, dc * 1024 + qh * 512: dc * 1024 + (qh + 1) * 512],
                                    start=(dc == 0), stop=(dc == 7), skip_group_check=True))
                            ctx.ops("tensor", mms, reads=[kB, qbB], writes=sp_b)
                            ctx.op("scalar", lambda e_, kc=kc, qh=qh, sp=sp: e_.activation(
                                pTt[:, kc * 1024 + qh * 512: kc * 1024 + (qh + 1) * 512], sp[:],
                                AF.Exp, bias=BIAS_NC, scale=1.0 / 32.0),
                                reads=sp_b + [CF], writes=[bf_(f"sb_pT{kc}")])
                    for j in range(8):
                        pv, pv_b = PV[j % 2]
                        mms = []
                        for kc in range(8):
                            for c0, c1 in ((0, 512), (512, 1024), (1024, 1025)):
                                mms.append(lambda e_, kc=kc, c0=c0, c1=c1, j=j, par=par: e_.matmul(
                                    pv[:, c0:c1],
                                    pTt[:, kc * 1024 + j * 128: kc * 1024 + (j + 1) * 128],
                                    vbuf[par][:, kc * 1025 + c0: kc * 1025 + c1],
                                    start=(kc == 0), stop=(kc == 7), skip_group_check=True))
                        ctx.ops("tensor", mms,
                                reads=[bf_(f"sb_pT{kc}") for kc in range(8)] + [vB, vOnes],
                                writes=pv_b)
                        if s == 0:
                            ctx.op("vector", lambda e_, j=j, pv=pv: e_.tensor_copy(
                                outsb[:, j * 1025:(j + 1) * 1025], pv[:, 0:1025]),
                                reads=pv_b, writes=[bf_(f"sb_outsb{j}")])
                        else:
                            ctx.op("vector", lambda e_, j=j, pv=pv: e_.tensor_tensor(
                                outsb[:, j * 1025:(j + 1) * 1025],
                                outsb[:, j * 1025:(j + 1) * 1025], pv[:, 0:1025], ALU.add),
                                reads=pv_b + [bf_(f"sb_outsb{j}")], writes=[bf_(f"sb_outsb{j}")])
                        if s == NSLAB - 1:
                            oj = bf_(f"sb_outsb{j}")
                            xt = xown[:, j * 1024:(j + 1) * 1024]
                            ctx.op("vector", lambda e_, j=j: e_.reciprocal(
                                rec[:], outsb[:, j * 1025 + 1024: j * 1025 + 1025]),
                                reads=[oj], writes=[bf_("b_rec")])
                            ctx.op("vector", lambda e_, j=j: e_.scalar_tensor_tensor(
                                xown[:, j * 1024:(j + 1) * 1024],
                                outsb[:, j * 1025: j * 1025 + 1024],
                                rec[:], xown[:, j * 1024:(j + 1) * 1024], ALU.mult, ALU.add),
                                reads=[oj, bf_("b_rec"), bf_("sb_xown")], writes=[bf_("sb_xown")])
                            ctx.op("vector", lambda e_, xt=xt, j=j: e_.bn_stats(
                                st96b[:, j * 12: j * 12 + 6], xt[:, 0:512]),
                                reads=[bf_("sb_xown")], writes=[bf_("ln2_st96")])
                            ctx.op("vector", lambda e_, xt=xt, j=j: e_.bn_stats(
                                st96b[:, j * 12 + 6: j * 12 + 12], xt[:, 512:1024]),
                                reads=[bf_("sb_xown")], writes=[bf_("ln2_st96")])
                bRC, bMC = emit_ln8(ctx, bf_, [None] * 8, st96b, sq32b, agg8b, rstd8b, mcol8b,
                                    cf32, CF, "ln2_")
                ctx.op("vector", lambda e_: e_.tensor_scalar(rp1b[:], rstd8b[:], 1.0, None, ALU.add),
                       reads=[bRC], writes=[bf_("b_rp1b")])
                for j in range(8):
                    xt = xown[:, j * 1024:(j + 1) * 1024]
                    ctx.op("scalar", lambda e_, xt=xt, j=j: e_.activation(
                        t2st[:, (j % 4) * 1024:(j % 4 + 1) * 1024], xt, AF.Identity,
                        bias=mcol8b[:, j:j + 1], scale=rstd8b[:, j:j + 1]),
                        reads=[bf_("sb_xown"), bMC, bRC], writes=[bf_("sb_t2st")])
                    ctx.op("vector", lambda e_, xt=xt, j=j: e_.tensor_scalar(
                        xt, xt, rp1b[:, j:j + 1], mcol8b[:, j:j + 1], ALU.mult, op1=ALU.add),
                        reads=[bf_("sb_xown"), bMC, bf_("b_rp1b")], writes=[bf_("sb_xown")])
                    if j % 4 == 3:
                        t23d = split_pa(t2_sp)
                        ctx.dma("gpsimd", "qG",
                                t23d[qb * 8 + (j - 3): qb * 8 + j + 1].rearrange("a p d -> p a d"),
                                t2st[:].rearrange("p (a d) -> p a d", d=1024),
                                reads=[bf_("sb_t2st")], writes=[bf_(f"dr_t2_sp{qb}")])
                x13d = split_pa(x1_sp)
                ctx.dma("gpsimd", "qG", x13d[qb * 8:(qb + 1) * 8].rearrange("a p d -> p a d"),
                        xown[:].rearrange("p (a d) -> p a d", d=1024),
                        reads=[bf_("sb_xown")], writes=[bf_(f"dr_x1_sp{qb}")])

        ctx.barrier_all()
        if upto == "p3":
         with ExitStack() as es3:
            wdT = sb("wdT", (128, 8 * 1024), BF16, es3)
            t2T = [sb(f"t2T{i}", (128, 8 * 512), BF16, es3) for i in range(2)]
            x1b = [sb(f"x1b{i}", (128, 4 * 1024), F32, es3) for i in range(2)]
            pf0 = sb("pf0", (128, 6 * 512), BF16, es3)
            pfs = sb("pfs", (128, 6 * 512), BF16, es3)
            yT = sb("yT", (128, 8 * 512), BF16, es3)
            outst = sb("outst", (128, 4 * 1024), F32, es3)
            lnS = sb("lnS", (6, 512), F32, es3)
            maskT = sb("maskT", (6, 512), F32, es3)
            lnG = sb("lnG", (6, 512), F32, es3)
            mx1 = sb("mx1", (128, 1), F32, es3)
            masktok = sb("masktok", (128, 6), F32, es3)

            for dc in range(8):
                ctx.dma("scalar", "qA", wdT[:, dc * 1024:(dc + 1) * 1024],
                        dt["wd_bf"][0:1024, dc * 128:(dc + 1) * 128],
                        reads=[bf_("dr_wd_bf")], writes=[bf_("sb_wdT")], transpose=True)

            for nb in range(NB3):
                par = nb % 2
                qb = nb // 2
                t2Tb = bf_(f"sb_t2T{par}")
                x1B = bf_(f"sb_x1b{par}")
                for cc in range(8):
                    ctx.dma("scalar", "qA", t2T[par][:, cc * 512:(cc + 1) * 512],
                            dt["t2_sp"][nb * 512:(nb + 1) * 512, cc * 128:(cc + 1) * 128],
                            reads=[bf_(f"dr_t2_sp{qb}")], writes=[t2Tb], transpose=True)
                x13d = split_pa(x1_sp)
                ctx.dma("sync", "qS", x1b[par][:].rearrange("p (a d) -> p a d", d=1024),
                        x13d[nb * 4:(nb + 1) * 4].rearrange("a p d -> p a d"),
                        reads=[bf_(f"dr_x1_sp{qb}")], writes=[x1B])
                zslots = []
                for e in range(6):
                    zp = psum[:, e * 512:(e + 1) * 512]
                    zslots.append(zp)
                    mms = []
                    for cc in range(8):
                        mms.append(lambda e_, cc=cc, e0=e, zp=zp, par=par: e_.matmul(
                            zp[0:100, :],
                            tokfm[:, cc * 608 + e0 * 100: cc * 608 + (e0 + 1) * 100],
                            t2T[par][:, cc * 512:(cc + 1) * 512],
                            start=(cc == 0), stop=False, skip_group_check=True))
                    ctx.ops("tensor", mms, reads=[bf_("sb_tokfm"), t2Tb], writes=[pbank[e]])
                LG, LG_B = pslice(3072, 3584)
                MT, MT_B = pslice(3584, 4096)
                for nch in range(4):
                    mms = []
                    for cc in range(8):
                        mms.append(lambda e_, cc=cc, nch=nch, par=par: e_.matmul(
                            LG[:, 0:6],
                            t2T[par][:, cc * 512 + nch * 128: cc * 512 + (nch + 1) * 128],
                            wg_fm[:, cc * 6:(cc + 1) * 6],
                            start=(cc == 0), stop=(cc == 7), skip_group_check=True))
                    ctx.ops("tensor", mms, reads=[t2Tb, bf_("sb_wg_fm")], writes=LG_B)
                    ctx.op("vector", lambda e_: e_.tensor_reduce(
                        mx1[:], LG[:, 0:6], axis=AX.X, op=ALU.max),
                        reads=LG_B, writes=[bf_("b_mx1")])
                    ctx.op("vector", lambda e_: e_.tensor_scalar(
                        masktok[:], LG[:, 0:6], mx1[:], None, ALU.is_ge),
                        reads=LG_B + [bf_("b_mx1")], writes=[bf_("b_masktok")])
                    ctx.op("tensor", lambda e_: e_.transpose(
                        MT[0:6, 0:128], masktok[:], IDENT_F32[:, :]),
                        reads=[bf_("b_masktok"), CF], writes=MT_B)
                    ctx.op("vector", lambda e_, nch=nch: e_.tensor_copy(
                        maskT[:, nch * 128:(nch + 1) * 128], MT[0:6, 0:128]),
                        reads=MT_B, writes=[bf_("sb_maskT")])
                for e in range(6):
                    ctx.op("scalar", lambda e_, e0=e: e_.activation(
                        pf0[0:100, e0 * 512:(e0 + 1) * 512], zslots[e0][0:100, :], AF.Exp,
                        bias=cf32[0:100, 2:3], scale=1.0 / 32.0),
                        reads=[pbank[e], CF], writes=[bf_("sb_pf0")])
                mms = []
                for e in range(6):
                    mms.append(lambda e_, e0=e: e_.matmul(
                        MT[0:6, 128:128 + 512] if False else MT[0:6, 0:512],
                        eind[:, e0 * 6:(e0 + 1) * 6],
                        pf0[0:100, e0 * 512:(e0 + 1) * 512],
                        start=(e0 == 0), stop=(e0 == 5), skip_group_check=True))
                ctx.ops("tensor", mms, reads=[bf_("sb_pf0"), EI, bf_("sb_maskT")], writes=MT_B)
                ctx.op("scalar", lambda e_: e_.activation(lnS[:], MT[0:6, 0:512], AF.Ln,
                                                          bias=cf32[0:6, 2:3], scale=1.0),
                       reads=MT_B + [CF], writes=[bf_("b_lnS")])
                ctx.op("vector", lambda e_: e_.tensor_scalar(lnS[:], lnS[:], -32.0, 2560.0,
                                                             ALU.mult, op1=ALU.add),
                       reads=[bf_("b_lnS")], writes=[bf_("b_lnS")])
                ctx.op("vector", lambda e_: e_.tensor_tensor(lnG[:], lnS[:], maskT[:], ALU.mult),
                       reads=[bf_("b_lnS"), bf_("sb_maskT")], writes=[bf_("b_lnG")])
                ctx.op("vector", lambda e_: e_.tensor_scalar(lnG[:], lnG[:], -2560.0, None, ALU.add),
                       reads=[bf_("b_lnG")], writes=[bf_("b_lnG")])
                for e in range(6):
                    ctx.op("tensor", lambda e_, e0=e: e_.matmul(
                        zslots[e0][0:100, :], selm[:, e0 * 100:(e0 + 1) * 100], lnG[:],
                        start=False, stop=True, skip_group_check=True),
                        reads=[bf_("b_lnG"), SM_], writes=[pbank[e]])
                    ctx.op("scalar", lambda e_, e0=e: e_.activation(
                        pfs[0:100, e0 * 512:(e0 + 1) * 512], zslots[e0][0:100, :], AF.Exp,
                        bias=cf32[0:100, 2:3], scale=1.0 / 32.0),
                        reads=[pbank[e], CF], writes=[bf_("sb_pfs")])
                for dc in range(8):
                    hp, hp_b = pslice((dc % 2) * 512, (dc % 2) * 512 + 512)
                    mms = []
                    for e in range(6):
                        mms.append(lambda e_, e0=e, dc=dc, hp=hp: e_.matmul(
                            hp[:], val_s[e0][0:100, dc * 128:(dc + 1) * 128],
                            pfs[0:100, e0 * 512:(e0 + 1) * 512],
                            start=(e0 == 0), stop=(e0 == 5), skip_group_check=True))
                    ctx.ops("tensor", mms,
                            reads=[bf_("sb_pfs")] + [bf_(f"sb_val{e}") for e in range(6)],
                            writes=hp_b)
                    ctx.op("vector", lambda e_, dc=dc, hp=hp, par=par: e_.tensor_tensor(
                        yT[:, dc * 512:(dc + 1) * 512], hp[:],
                        t2T[par][:, dc * 512:(dc + 1) * 512], ALU.add),
                        reads=hp_b + [t2Tb], writes=[bf_("sb_yT")])
                for ncs in range(4):
                    dp, dp_b = pslice(1024 + (ncs % 2) * 1024, 1024 + (ncs % 2) * 1024 + 1024)
                    mms = []
                    for dc in range(8):
                        for j in range(2):
                            mms.append(lambda e_, dc=dc, j=j, ncs=ncs, dp=dp: e_.matmul(
                                dp[:, j * 512:(j + 1) * 512],
                                yT[:, dc * 512 + ncs * 128: dc * 512 + (ncs + 1) * 128],
                                wdT[:, dc * 1024 + j * 512: dc * 1024 + (j + 1) * 512],
                                start=(dc == 0), stop=False, skip_group_check=True))
                    for j in range(2):
                        mms.append(lambda e_, j=j, dp=dp: e_.matmul(
                            dp[:, j * 512:(j + 1) * 512],
                            ONES_ROW_128, rows[0:1, 1024 + j * 512: 1024 + (j + 1) * 512],
                            start=False, stop=True, skip_group_check=True))
                    ctx.ops("tensor", mms, reads=[bf_("sb_yT"), bf_("sb_wdT"), CB, RW], writes=dp_b)
                    ctx.op("scalar", lambda e_, ncs=ncs, dp=dp: e_.activation(
                        outst[:, ncs * 1024:(ncs + 1) * 1024], dp[:], AF.Exp,
                        bias=BIAS_Z, scale=SCALE_COL),
                        reads=dp_b + [CF], writes=[bf_("sb_outst")])
                    ctx.op("scalar", lambda e_, ncs=ncs: e_.activation(
                        outst[:, ncs * 1024:(ncs + 1) * 1024],
                        outst[:, ncs * 1024:(ncs + 1) * 1024], AF.Ln,
                        bias=BIAS_Z, scale=1.0),
                        reads=[bf_("sb_outst"), CF], writes=[bf_("sb_outst")])
                    ctx.op("vector", lambda e_, ncs=ncs, par=par: e_.tensor_tensor(
                        outst[:, ncs * 1024:(ncs + 1) * 1024], outst[:, ncs * 1024:(ncs + 1) * 1024],
                        x1b[par][:, ncs * 1024:(ncs + 1) * 1024], ALU.add),
                        reads=[bf_("sb_outst"), x1B], writes=[bf_("sb_outst")])
                o3d = split_pa(out_d)
                ctx.dma("gpsimd", "qG", o3d[nb * 4:(nb + 1) * 4].rearrange("a p d -> p a d"),
                        outst[:].rearrange("p (a d) -> p a d", d=1024),
                        reads=[bf_("sb_outst")], writes=[bf_("dr_out")])

        ctx.barrier_all()
        for name in taps:
            ctx.dma("gpsimd", "qG", tap_d[name], dt[name],
                    reads=[b for n, b in bufs.items() if n.startswith("dr_" + name)])
        ctx.barrier_queue("sync", "qG")
        ctx.barrier_queue("sync", "qA")
        ctx.barrier_queue("sync", "qS")

    return nc


def kernel(**inputs):
    x = np.asarray(inputs["L0_attn_t"])
    W = np.ascontiguousarray(inputs["L0_attn_W_qkv"])
    b_qkv = np.asarray(inputs["L0_attn_b_qkv"])
    A = np.ascontiguousarray(inputs["L0_ffn_A"])
    Bm = np.ascontiguousarray(inputs["L0_ffn_B"])
    wg = np.ascontiguousarray(inputs["L0_ffn_w_gate"])
    wtw = np.ascontiguousarray(inputs["L0_ffn_wt_w"])
    wtb = np.asarray(inputs["L0_ffn_wt_b"])
    wdw = np.ascontiguousarray(inputs["L0_ffn_wd_w"])
    wdb = np.asarray(inputs["L0_ffn_wd_b"])
    scale = np.asarray(inputs["L0_ffn_scale"])

    cbf, selm, eind = _host_consts()
    bcol = np.ascontiguousarray(b_qkv[0:2048].reshape(16, 128).T)
    rows_bf = np.concatenate([wtb, wdb, b_qkv[2048:3072]])[None, :].astype(bf16)
    cf32 = np.zeros((128, 132), np.float32)
    cf32[:, 0] = -C_ATTN
    cf32[:, 1] = 1e-5
    cf32[:, 2] = 0.0
    cf32[:, 3] = scale[0]
    cf32[:, 4:132] = np.eye(128, dtype=np.float32)

    nc = build()
    in_maps = []
    for c in range(8):
        b, h = divmod(c, 2)
        xs = np.ascontiguousarray(np.roll(x[b], -h * HALF, axis=0)) if h else np.ascontiguousarray(x[b])
        in_maps.append({
            "x_seq": xs, "w_qkv": W, "bcol": bcol, "a_in": A, "b_in": Bm,
            "wgate": wg, "wt_w": wtw, "wd_w": wdw, "rows_bf": rows_bf,
            "cbf": cbf, "selm": selm, "eind": eind, "cf32": cf32,
        })
    res = run_bass_kernel_spmd(nc, in_maps, core_ids=list(range(8)))
    out = np.empty((B, N, D), np.float32)
    for c in range(8):
        b, h = divmod(c, 2)
        out[b, h * HALF:(h + 1) * HALF] = res.results[c]["out"]
    return out


# version 08 (final)
# speedup vs baseline: 1.2326x; 1.1229x over previous
import numpy as np
import ml_dtypes
import concourse.bass as bass
import concourse.mybir as mybir
from concourse.bass_utils import run_bass_kernel_spmd
from contextlib import ExitStack
from collections import defaultdict

F32 = mybir.dt.float32
BF16 = mybir.dt.bfloat16
AF = mybir.ActivationFunctionType
ALU = mybir.AluOpType
AX = mybir.AxisListType
bf16 = ml_dtypes.bfloat16

B, N, D = 4, 8192, 1024
HALF = N // 2
NSLAB = 8
SLAB = 1024
QB = 4
NB3 = 8
BLK3 = 512
C_ATTN = 24.0


class Buf:
    __slots__ = ("writer", "readers", "name")

    def __init__(self, name=""):
        self.writer = None
        self.readers = []
        self.name = name


class Ctx:
    ENGS = ("tensor", "vector", "scalar", "gpsimd", "sync")
    QUEUES = ("qS", "qA", "qG")

    def __init__(self, nc, es):
        self.nc = nc
        self.sem = {}
        for n in self.ENGS + self.QUEUES:
            self.sem[n] = es.enter_context(nc.semaphore("sem_" + n))
        self.cnt = defaultdict(int)
        self.waited = defaultdict(dict)

    def _wait(self, eng, src, val):
        if src == eng:
            return
        assert val <= self.cnt[src], ("wait-ahead deadlock", eng, src, val, self.cnt[src])
        w = self.waited[eng]
        if val > w.get(src, 0):
            getattr(self.nc, eng).wait_ge(self.sem[src], val)
            w[src] = val

    def _deps(self, eng, reads, writes):
        need = {}
        for b in reads:
            if b.writer:
                s, v = b.writer
                if need.get(s, 0) < v:
                    need[s] = v
        for b in writes:
            if b.writer:
                s, v = b.writer
                if need.get(s, 0) < v:
                    need[s] = v
            for s, v in b.readers:
                if need.get(s, 0) < v:
                    need[s] = v
        for s, v in need.items():
            self._wait(eng, s, v)

    def _finish(self, tag, c, reads, writes):
        for b in reads:
            b.readers.append((tag, c))
        for b in writes:
            b.writer = (tag, c)
            b.readers = []

    def _maybe_drain(self, eng, reads):
        if eng in ("vector", "scalar", "gpsimd"):
            c = self.cnt[eng]
            for b in reads:
                if b.writer and b.writer[0] == eng and b.writer[1] >= c - 1:
                    getattr(self.nc, eng).drain()
                    return

    def op(self, eng, fn, reads=(), writes=()):
        self._deps(eng, reads, writes)
        self._maybe_drain(eng, reads)
        ins = fn(getattr(self.nc, eng))
        self.cnt[eng] += 1
        ins.then_inc(self.sem[eng], 1)
        self._finish(eng, self.cnt[eng], reads, writes)

    def ops(self, eng, fns, reads=(), writes=()):
        self._deps(eng, reads, writes)
        self._maybe_drain(eng, reads)
        e = getattr(self.nc, eng)
        ins = None
        for fn in fns:
            ins = fn(e)
        self.cnt[eng] += 1
        ins.then_inc(self.sem[eng], 1)
        self._finish(eng, self.cnt[eng], reads, writes)

    def dma(self, eng, queue, out, in_, reads=(), writes=(), transpose=False):
        self._deps(eng, reads, writes)
        if eng in ("vector", "scalar", "gpsimd"):
            for b in reads:
                if b.writer and b.writer[0] == eng and b.writer[1] >= self.cnt[eng] - 4:
                    getattr(self.nc, eng).drain()
                    break
        getattr(self.nc, eng).dma_start(out=out, in_=in_, transpose=transpose) \
            .then_inc(self.sem[queue], 16)
        self.cnt[queue] += 16
        self._finish(queue, self.cnt[queue], reads, writes)

    def barrier_queue(self, eng, queue):
        self._wait(eng, queue, self.cnt[queue])

    def barrier_all(self):
        snap = {n: self.cnt[n] for n in self.ENGS + self.QUEUES}
        for e in self.ENGS:
            for n, v in snap.items():
                if n != e and v > 0:
                    self._wait(e, n, v)


def _host_consts():
    ident = np.eye(128, dtype=np.float32).astype(bf16)
    onesb = np.ones((128, 136), dtype=np.float32).astype(bf16)
    cbf = np.concatenate([ident, onesb], axis=1)
    selm = np.zeros((6, 600), dtype=np.float32)
    for e in range(6):
        selm[e, e * 100:(e + 1) * 100] = 1.0
    eind = np.zeros((100, 36), dtype=np.float32)
    for e in range(6):
        eind[:, e * 6 + e] = 1.0
    return cbf, selm.astype(np.float32), eind.astype(bf16)


def emit_ln8(ctx, bf_, tiles, st96, sq32, agg8, rstd8, mcol8, cf32, cfbuf, tag):
    V, S = "vector", "scalar"
    n = len(tiles)
    w = 12 * n
    bST = bf_(tag + "st96")
    bSM, bCV, bSQ = bf_(tag + "sm"), bf_(tag + "cv"), bf_(tag + "sq")
    bM2, bVAR, bRC, bMC = bf_(tag + "m2"), bf_(tag + "var"), bf_(tag + "rc"), bf_(tag + "mc")
    m = [st96[:, c:w:12] for c in (1, 4, 7, 10)]
    cv = [st96[:, c:w:12] for c in (2, 5, 8, 11)]
    SM8, CV8, M2 = agg8[:, 0:n], agg8[:, 8:8 + n], agg8[:, 24:24 + n]
    ctx.op(V, lambda e: e.tensor_tensor(SM8, m[0], m[1], ALU.add), reads=[bST], writes=[bSM])
    ctx.op(V, lambda e: e.tensor_tensor(agg8[:, 16:16 + n], m[2], m[3], ALU.add),
           reads=[bST], writes=[bSQ])
    ctx.op(V, lambda e: e.tensor_tensor(SM8, SM8, agg8[:, 16:16 + n], ALU.add),
           reads=[bSM, bSQ], writes=[bSM])
    ctx.op(V, lambda e: e.tensor_tensor(CV8, cv[0], cv[1], ALU.add), reads=[bST], writes=[bCV])
    ctx.op(V, lambda e: e.tensor_tensor(agg8[:, 16:16 + n], cv[2], cv[3], ALU.add),
           reads=[bST], writes=[bSQ])
    ctx.op(V, lambda e: e.tensor_tensor(CV8, CV8, agg8[:, 16:16 + n], ALU.add),
           reads=[bCV, bSQ], writes=[bCV])
    for i in range(4):
        ctx.op(V, lambda e, i=i: e.tensor_tensor(sq32[:, 8 * i:8 * i + n], m[i], m[i], ALU.mult),
               reads=[bST], writes=[bSQ])
    ctx.op(V, lambda e: e.tensor_tensor(sq32[:, 0:n], sq32[:, 0:n], sq32[:, 8:8 + n], ALU.add),
           reads=[bSQ], writes=[bSQ])
    ctx.op(V, lambda e: e.tensor_tensor(sq32[:, 16:16 + n], sq32[:, 16:16 + n], sq32[:, 24:24 + n],
                                        ALU.add),
           reads=[bSQ], writes=[bSQ])
    ctx.op(V, lambda e: e.tensor_tensor(sq32[:, 0:n], sq32[:, 0:n], sq32[:, 16:16 + n], ALU.add),
           reads=[bSQ], writes=[bSQ])
    ctx.op(V, lambda e: e.tensor_scalar(SM8, SM8, 0.25, None, ALU.mult), reads=[bSM], writes=[bSM])
    ctx.op(V, lambda e: e.tensor_tensor(M2, SM8, SM8, ALU.mult), reads=[bSM], writes=[bM2])
    ctx.op(V, lambda e: e.tensor_scalar(CV8, CV8, 1.0 / 1024.0, None, ALU.mult),
           reads=[bCV], writes=[bVAR])
    ctx.op(V, lambda e: e.scalar_tensor_tensor(CV8, sq32[:, 0:n], 0.25, CV8, ALU.mult, ALU.add),
           reads=[bSQ, bVAR], writes=[bVAR])
    ctx.op(V, lambda e: e.tensor_tensor(CV8, CV8, M2, ALU.subtract), reads=[bVAR, bM2], writes=[bVAR])
    ctx.op(S, lambda e: e.activation(rstd8[:, 0:n], CV8, AF.Ln, bias=cf32[:, 1:2], scale=1.0),
           reads=[bVAR, cfbuf], writes=[bRC])
    ctx.op(S, lambda e: e.activation(rstd8[:, 0:n], rstd8[:, 0:n], AF.Exp,
                                     bias=cf32[:, 2:3], scale=-0.5),
           reads=[bRC, cfbuf], writes=[bRC])
    ctx.op(V, lambda e: e.tensor_scalar(mcol8[:, 0:n], SM8, -1.0, None, ALU.mult),
           reads=[bSM], writes=[bMC])
    ctx.op(V, lambda e: e.tensor_tensor(mcol8[:, 0:n], mcol8[:, 0:n], rstd8[:, 0:n], ALU.mult),
           reads=[bMC, bRC], writes=[bMC])
    return bRC, bMC


def emit_ln(ctx, bf_, xt, xbuf, sbufs, tag):
    st6, st2, tmp4, tmp1, mean2, mcol, rcol, cf32 = sbufs
    V, S = "vector", "scalar"
    bST6, bSM, bSCV, bT4, bSM2 = bf_(tag + "st6"), bf_(tag + "sm"), bf_(tag + "scv"), bf_(tag + "t4"), bf_(tag + "sm2")
    bMEAN, bM2, bVAR, bRC, bMC = bf_(tag + "mean"), bf_(tag + "m2"), bf_(tag + "var"), bf_(tag + "rc"), bf_(tag + "mc")
    ctx.op(V, lambda e: e.bn_stats(st6[:, 0:6], xt[:, 0:512]), reads=[xbuf], writes=[bST6])
    ctx.op(V, lambda e: e.bn_stats(st6[:, 6:12], xt[:, 512:1024]), reads=[xbuf], writes=[bST6])
    ctx.op(V, lambda e: e.reduce_sum(st2[:, 0:1], st6[:, 1:12:3], axis=AX.X), reads=[bST6], writes=[bSM])
    ctx.op(V, lambda e: e.reduce_sum(st2[:, 1:2], st6[:, 2:12:3], axis=AX.X), reads=[bST6], writes=[bSCV])
    ctx.op(V, lambda e: e.tensor_tensor(tmp4[:], st6[:, 1:12:3], st6[:, 1:12:3], ALU.mult),
           reads=[bST6], writes=[bT4])
    ctx.op(V, lambda e: e.reduce_sum(tmp1[:], tmp4[:], axis=AX.X), reads=[bT4], writes=[bSM2])
    ctx.op(V, lambda e: e.tensor_scalar(st2[:, 0:1], st2[:, 0:1], 0.25, None, ALU.mult),
           reads=[bSM], writes=[bMEAN])
    ctx.op(V, lambda e: e.tensor_tensor(mean2[:], st2[:, 0:1], st2[:, 0:1], ALU.mult),
           reads=[bMEAN], writes=[bM2])
    ctx.op(V, lambda e: e.tensor_scalar(st2[:, 1:2], st2[:, 1:2], 1.0 / 1024.0, None, ALU.mult),
           reads=[bSCV], writes=[bVAR])
    ctx.op(V, lambda e: e.scalar_tensor_tensor(st2[:, 1:2], tmp1[:], 0.25, st2[:, 1:2], ALU.mult, ALU.add),
           reads=[bSM2, bVAR], writes=[bVAR])
    ctx.op(V, lambda e: e.tensor_tensor(st2[:, 1:2], st2[:, 1:2], mean2[:], ALU.subtract),
           reads=[bVAR, bM2], writes=[bVAR])
    ctx.op(S, lambda e: e.activation(rcol[:], st2[:, 1:2], AF.Ln, bias=cf32[:, 1:2], scale=1.0),
           reads=[bVAR, bf_("sb_cf32_s")], writes=[bRC])
    ctx.op(S, lambda e: e.activation(rcol[:], rcol[:], AF.Exp, bias=cf32[:, 2:3], scale=-0.5),
           reads=[bRC, bf_("sb_cf32_s")], writes=[bRC])
    ctx.op(V, lambda e: e.tensor_scalar(mcol[:], st2[:, 0:1], rcol[:], -1.0, ALU.mult, op1=ALU.mult),
           reads=[bMEAN, bRC], writes=[bMC])
    return bRC, bMC


def split_pa(ap, p=128):
    return ap.rearrange("(a p) d -> a p d", p=p)


def build(upto='p3', taps=()):
    nc = bass.Bass("TRN2", target_bir_lowering=False)
    dt = {}

    def din(name, shape, dtype):
        dt[name] = nc.dram_tensor(name, list(shape), dtype, kind="ExternalInput").ap()
        return dt[name]

    def dout(name, shape, dtype):
        dt[name] = nc.dram_tensor(name, list(shape), dtype, kind="ExternalOutput").ap()
        return dt[name]

    def dint(name, shape, dtype):
        dt[name] = nc.dram_tensor(name, list(shape), dtype, kind="Internal").ap()
        return dt[name]

    x_seq = din("x_seq", (N, D), F32)
    w_qkv = din("w_qkv", (3072, D), F32)
    bcol_d = din("bcol", (128, 16), F32)
    a_in = din("a_in", (6, 100, 16), F32)
    b_in = din("b_in", (6, 16, D), F32)
    wgate = din("wgate", (D, 6), F32)
    wt_w = din("wt_w", (D, D), F32)
    wd_w = din("wd_w", (D, D), F32)
    rows_d = din("rows_bf", (1, 3072), BF16)
    cbf_d = din("cbf", (128, 264), BF16)
    selm_d = din("selm", (6, 600), F32)
    eind_d = din("eind", (100, 36), BF16)
    cf32_d = din("cf32", (128, 132), F32)
    out_d = dout("out", (HALF, D), F32)

    t_sp = dint("t_sp", (N, D), BF16)
    t2_sp = dint("t2_sp", (HALF, D), BF16)
    x1_sp = dint("x1_sp", (HALF, D), F32)
    k_sp = dint("k_sp", (NSLAB, 8, 128, SLAB), BF16)
    v_sp = dint("v_sp", (NSLAB, 8, 128, SLAB), BF16)
    q_sp = dint("q_sp", (QB, 8, 128, SLAB), BF16)
    wq_bf = dint("wq_bf", (3072, D), BF16)
    wt_bf = dint("wt_bf", (D, D), BF16)
    wd_bf = dint("wd_bf", (D, D), BF16)
    tok_sp = dint("tok_sp", (608, D), BF16)

    tap_d = {}
    for name in taps:
        t = dt[name].tensor
        tap_d[name] = dout("tap_" + name, list(t.shape), t.dtype)

    bufs = {}

    def bf_(name):
        if name not in bufs:
            bufs[name] = Buf(name)
        return bufs[name]

    es_prog = ExitStack()
    with es_prog:
        ctx = Ctx(nc, es_prog)
        blk = es_prog.enter_context(nc.Block())
        pers = ExitStack()
        es_prog.enter_context(pers)

        def sb(name, shape, dtype, stack=None):
            return (stack or pers).enter_context(nc.sbuf_tensor(name, list(shape), dtype))

        cbf = sb("cbf_s", (128, 264), BF16)
        rows = sb("rows_s", (1, 3072), BF16)
        bcol = sb("bcol_s", (128, 16), F32)
        cf32 = sb("cf32_s", (128, 132), F32)
        selm = sb("selm_s", (6, 600), F32)
        eind = sb("eind_s", (100, 36), BF16)
        wg_fm = sb("wg_fm", (128, 48), BF16)
        tokfm = sb("tokfm", (128, 8 * 608), BF16)
        val_s = [sb(f"val{e}", (128, D), BF16) for e in range(6)]

        IDENT_F32 = cf32[:, 4:132]
        ONES_ROW_128 = cbf[0:1, 128:256]
        ONES_ROW_100 = cbf[0:1, 128:228]
        BIAS_NC = cf32[:, 0:1]
        BIAS_EPS = cf32[:, 1:2]
        BIAS_Z = cf32[:, 2:3]
        SCALE_COL = cf32[:, 3:4]

        psum = es_prog.enter_context(nc.psum_tensor("psA", [128, 4096], F32))
        pbank = [bf_(f"pbank{i}") for i in range(8)]

        def pslice(c0, c1):
            return psum[:, c0:c1], [pbank[i] for i in range(c0 // 512, (c1 + 511) // 512)]

        for dstt, src in ((cbf, cbf_d), (rows, rows_d), (bcol, bcol_d),
                          (cf32, cf32_d), (selm, selm_d), (eind, eind_d)):
            ctx.dma("sync", "qS", dstt[:], src, writes=[bf_("sb_" + dstt.name)])
        CB, RW, BC, CF, SM_, EI = [bufs["sb_" + t.name] for t in (cbf, rows, bcol, cf32, selm, eind)]

        ctx.dma("gpsimd", "qG", dt["wq_bf"][:], w_qkv, writes=[bf_("dr_wq_bf")])
        ctx.dma("gpsimd", "qG", dt["wt_bf"][:], wt_w, writes=[bf_("dr_wt_bf")])
        ctx.dma("gpsimd", "qG", dt["wd_bf"][:], wd_w, writes=[bf_("dr_wd_bf")])

        es0 = ExitStack()
        wg_f32 = sb("wg_f32", (128, 48), F32, es0)
        for cc in range(8):
            ctx.dma("sync", "qS", wg_f32[:, cc * 6:(cc + 1) * 6],
                    wgate[cc * 128:(cc + 1) * 128, :], writes=[bf_("sb_wg_f32")])
        ctx.op("vector", lambda e: e.tensor_copy(wg_fm[:], wg_f32[:]),
               reads=[bf_("sb_wg_f32")], writes=[bf_("sb_wg_fm")])

        ctx.barrier_all()
        if upto in ("p1", "p2", "p3"):
         with ExitStack() as es1:
            wTall = sb("wTall", (128, 8 * 3072), BF16, es1)
            xblk = [sb(f"xblk{i}", (128, 2 * 1024), F32, es1) for i in range(2)]
            tstage = sb("tstage", (128, 4 * 1024), BF16, es1)
            tT = [sb(f"tT{i}", (128, 8 * 2048), BF16, es1) for i in range(2)]
            kqstage = sb("kqstage", (128, 8 * 1024), BF16, es1)
            vstage = sb("vstage", (128, 8 * 1024), BF16, es1)
            st96 = sb("st96", (128, 96), F32, es1)
            sq32a = sb("sq32a", (128, 64), F32, es1)
            agg8a = sb("agg8a", (128, 64), F32, es1)
            rstd8a = sb("rstd8a", (128, 8), F32, es1)
            mcol8a = sb("mcol8a", (128, 8), F32, es1)

            for dc in range(8):
                ctx.dma("scalar", "qA", wTall[:, dc * 3072:(dc + 1) * 3072],
                        dt["wq_bf"][0:3072, dc * 128:(dc + 1) * 128],
                        reads=[bf_("dr_wq_bf")], writes=[bf_("sb_wTall")], transpose=True)

            def WT(dc, c0, c1):
                return wTall[:, dc * 3072 + c0: dc * 3072 + c1]

            x3d = split_pa(x_seq)
            rot = 0

            def p1_chain(s):
                xb = [bf_("b_xblk0"), bf_("b_xblk1")]
                for half in range(2):
                    tile_list = []
                    for q2 in (2 * half, 2 * half + 1):
                        par2 = q2 % 2
                        ctx.dma("scalar", "qA",
                                xblk[par2][:].rearrange("p (a d) -> p a d", d=1024),
                                x3d[8 * s + 2 * q2: 8 * s + 2 * q2 + 2].rearrange("a p d -> p a d"),
                                reads=[bf_("dr_x_seq")], writes=[xb[par2]])
                        for th in range(2):
                            t8 = 2 * q2 + th
                            xt = xblk[par2][:, th * 1024:(th + 1) * 1024]
                            ctx.op("vector", lambda e_, xt=xt, t8=t8: e_.bn_stats(
                                st96[:, t8 * 12: t8 * 12 + 6], xt[:, 0:512]),
                                reads=[xb[par2]], writes=[bf_(f"ln1_{half}st96")])
                            ctx.op("vector", lambda e_, xt=xt, t8=t8: e_.bn_stats(
                                st96[:, t8 * 12 + 6: t8 * 12 + 12], xt[:, 512:1024]),
                                reads=[xb[par2]], writes=[bf_(f"ln1_{half}st96")])
                            tile_list.append((xt, xb[par2]))
                    bRC, bMC = emit_ln8(ctx, bf_, tile_list, st96[:, 48 * half:48 * half + 48],
                                        sq32a[:, 32 * half:32 * half + 32],
                                        agg8a[:, 32 * half:32 * half + 32],
                                        rstd8a[:, 4 * half:4 * half + 4],
                                        mcol8a[:, 4 * half:4 * half + 4],
                                        cf32, CF, f"ln1_{half}")
                    for ti, (xt, xbuf) in enumerate(tile_list):
                        ctx.op("scalar", lambda e_, xt=xt, ti=ti, half=half: e_.activation(
                            tstage[:, ti * 1024:(ti + 1) * 1024], xt, AF.Identity,
                            bias=mcol8a[:, 4 * half + ti: 4 * half + ti + 1],
                            scale=rstd8a[:, 4 * half + ti: 4 * half + ti + 1]),
                            reads=[xbuf, bMC, bRC], writes=[bf_("sb_tstage")])
                    t3d = split_pa(t_sp)
                    ctx.dma("sync", "qS",
                            t3d[s * 8 + 4 * half: s * 8 + 4 * half + 4].rearrange("a p d -> p a d"),
                            tstage[:].rearrange("p (a d) -> p a d", d=1024),
                            reads=[bf_("sb_tstage")], writes=[bf_(f"dr_t_sp{s}")])
                if s % 2 == 1:
                    par = (s // 2) % 2
                    tTb = bf_(f"sb_tT{par}")
                    for dc in range(8):
                        ctx.dma("sync", "qS", tT[par][:, dc * 2048:(dc + 1) * 2048],
                                dt["t_sp"][(s - 1) * 1024:(s + 1) * 1024, dc * 128:(dc + 1) * 128],
                                reads=[bf_(f"dr_t_sp{s-1}"), bf_(f"dr_t_sp{s}")],
                                writes=[tTb], transpose=True)

            def TTs(s, dc, c0, c1):
                par = (s // 2) % 2
                base = dc * 2048 + (s % 2) * 1024
                return tT[par][:, base + c0: base + c1]

            def p1_mms(s):
                nonlocal rot
                tTb = bf_(f"sb_tT{(s // 2) % 2}")
                for ec in range(8):
                    pm, pm_b = pslice((rot % 2) * 1024, (rot % 2) * 1024 + 1024); rot += 1
                    mms = []
                    for dc in range(8):
                        for j in range(2):
                            mms.append(lambda e_, dc=dc, j=j, ec=ec, pm=pm: e_.matmul(
                                pm[:, j * 512:(j + 1) * 512],
                                WT(dc, (8 + ec) * 128, (9 + ec) * 128),
                                TTs(s, dc, j * 512, (j + 1) * 512),
                                start=(dc == 0), stop=(dc == 7), skip_group_check=True))
                    ctx.ops("tensor", mms, reads=[bf_("sb_wTall"), tTb], writes=pm_b)
                    ctx.op("scalar", lambda e_, ec=ec, pm=pm: e_.activation(
                        kqstage[:, ec * 1024:(ec + 1) * 1024], pm[:], AF.Identity,
                        bias=bcol[:, 8 + ec:9 + ec], scale=1.0),
                        reads=pm_b + [BC], writes=[bf_("sb_kqstage")])
                ctx.dma("gpsimd", "qG", dt["k_sp"][s].rearrange("a p d -> p a d"),
                        kqstage[:].rearrange("p (a d) -> p a d", d=1024),
                        reads=[bf_("sb_kqstage")], writes=[bf_(f"dr_k_sp{s}")])
                for kt in range(8):
                    pm, pm_b = pslice((rot % 2) * 1024, (rot % 2) * 1024 + 1024); rot += 1
                    mms = []
                    for dc in range(8):
                        for j in range(2):
                            mms.append(lambda e_, dc=dc, j=j, kt=kt, pm=pm: e_.matmul(
                                pm[:, j * 512:(j + 1) * 512],
                                TTs(s, dc, kt * 128, (kt + 1) * 128),
                                WT(dc, 2048 + j * 512, 2048 + (j + 1) * 512),
                                start=(dc == 0), stop=False, skip_group_check=True))
                    for j in range(2):
                        mms.append(lambda e_, j=j, pm=pm: e_.matmul(
                            pm[:, j * 512:(j + 1) * 512],
                            ONES_ROW_128, rows[0:1, 2048 + j * 512: 2048 + (j + 1) * 512],
                            start=False, stop=True, skip_group_check=True))
                    ctx.ops("tensor", mms, reads=[bf_("sb_wTall"), tTb, CB, RW], writes=pm_b)
                    ctx.op("vector", lambda e_, kt=kt, pm=pm: e_.tensor_copy(
                        vstage[:, kt * 1024:(kt + 1) * 1024], pm[:]),
                        reads=pm_b, writes=[bf_("sb_vstage")])
                ctx.dma("gpsimd", "qG", dt["v_sp"][s].rearrange("a p d -> p a d"),
                        vstage[:].rearrange("p (a d) -> p a d", d=1024),
                        reads=[bf_("sb_vstage")], writes=[bf_(f"dr_v_sp{s}")])
                if s < QB:
                    for ec in range(8):
                        pm, pm_b = pslice((rot % 2) * 1024, (rot % 2) * 1024 + 1024); rot += 1
                        mms = []
                        for dc in range(8):
                            for j in range(2):
                                mms.append(lambda e_, dc=dc, j=j, ec=ec, pm=pm: e_.matmul(
                                    pm[:, j * 512:(j + 1) * 512],
                                    WT(dc, ec * 128, (ec + 1) * 128),
                                    TTs(s, dc, j * 512, (j + 1) * 512),
                                    start=(dc == 0), stop=(dc == 7), skip_group_check=True))
                        ctx.ops("tensor", mms, reads=[bf_("sb_wTall"), tTb], writes=pm_b)
                        ctx.op("vector", lambda e_, ec=ec, pm=pm: e_.tensor_scalar(
                            kqstage[:, ec * 1024:(ec + 1) * 1024], pm[:], bcol[:, ec:ec + 1], None,
                            ALU.add),
                            reads=pm_b + [BC], writes=[bf_("sb_kqstage")])
                    ctx.dma("gpsimd", "qG", dt["q_sp"][s].rearrange("a p d -> p a d"),
                            kqstage[:].rearrange("p (a d) -> p a d", d=1024),
                            reads=[bf_("sb_kqstage")], writes=[bf_(f"dr_q_sp{s}")])

            p1_chain(0)
            p1_chain(1)
            for s in range(NSLAB):
                if s + 2 < NSLAB:
                    p1_chain(s + 2)
                p1_mms(s)

        if upto in ("p1", "p2", "p3"):
            ab_f32 = sb("ab_f32", (128, 1040), F32, es0)
            ab_bf = sb("ab_bf", (128, 1040), BF16, es0)
            aT = [sb(f"aT{e}", (16, 100), BF16, es0) for e in range(6)]
            tokstage = sb("tokstage", (128, 1024), BF16, es0)

            ctx.op("vector", lambda e: e.memset(ab_f32[:], 0.0), writes=[bf_("sb_ab_f32")])
            for e in range(6):
                ctx.dma("sync", "qS", ab_f32[0:100, 0:16], a_in[e], writes=[bf_("sb_ab_f32")])
                ctx.dma("sync", "qS", ab_f32[0:16, 16:1040], b_in[e], writes=[bf_("sb_ab_f32")])
                ctx.op("vector", lambda e_: e_.tensor_copy(ab_bf[:], ab_f32[:]),
                       reads=[bf_("sb_ab_f32")], writes=[bf_("sb_ab_bf")])
                pA, pA_b = pslice(0, 512)
                ctx.op("tensor", lambda e_: e_.transpose(pA[0:16, 0:100], ab_f32[0:100, 0:16],
                                                         IDENT_F32[0:100, 0:100]),
                       reads=[bf_("sb_ab_f32"), CF], writes=pA_b)
                ctx.op("vector", lambda e_, e0=e: e_.tensor_copy(aT[e0][:], pA[0:16, 0:100]),
                       reads=pA_b, writes=[bf_(f"sb_aT{e}")])
                pT, pT_b = pslice(1024, 2048)
                ctx.ops("tensor",
                        [(lambda e_, j=j, e0=e: e_.matmul(pT[0:100, j * 512:(j + 1) * 512],
                                                          aT[e0][:], ab_bf[0:16, 16 + j * 512:16 + (j + 1) * 512],
                                                          start=True, stop=True))
                         for j in range(2)],
                        reads=[bf_(f"sb_aT{e}"), bf_("sb_ab_bf")], writes=pT_b)
                ctx.op("vector", lambda e_: e_.tensor_copy(tokstage[0:100, :], pT[0:100, :]),
                       reads=pT_b, writes=[bf_("sb_tokstage")])
                ctx.dma("gpsimd", "qG", dt["tok_sp"][e * 100:(e + 1) * 100, :], tokstage[0:100, :],
                        reads=[bf_("sb_tokstage")], writes=[bf_("dr_tok_sp")])
            ctx.op("vector", lambda e_: e_.memset(tokstage[0:8, :], 0.0), writes=[bf_("sb_tokstage")])
            ctx.dma("gpsimd", "qG", dt["tok_sp"][600:608, :], tokstage[0:8, :],
                    reads=[bf_("sb_tokstage")], writes=[bf_("dr_tok_sp")])
            for cc in range(8):
                ctx.dma("scalar", "qA", tokfm[:, cc * 608:(cc + 1) * 608],
                        dt["tok_sp"][0:608, cc * 128:(cc + 1) * 128],
                        reads=[bf_("dr_tok_sp")], writes=[bf_("sb_tokfm")], transpose=True)

            es0.close()
            with ExitStack() as es_val:
                wtT = sb("wtT", (128, 8 * 1024), BF16, es_val)
                for cc in range(8):
                    ctx.dma("scalar", "qA", wtT[:, cc * 1024:(cc + 1) * 1024],
                            dt["wt_bf"][0:1024, cc * 128:(cc + 1) * 128],
                            reads=[bf_("dr_wt_bf")], writes=[bf_("sb_wtT")], transpose=True)
                for e in range(6):
                    pV, pV_b = pslice(0, 1024)
                    mms = []
                    for cc in range(8):
                        for j in range(2):
                            mms.append(lambda e_, cc=cc, j=j, e0=e: e_.matmul(
                                pV[0:100, j * 512:(j + 1) * 512],
                                tokfm[:, cc * 608 + e0 * 100: cc * 608 + (e0 + 1) * 100],
                                wtT[:, cc * 1024 + j * 512: cc * 1024 + (j + 1) * 512],
                                start=(cc == 0), stop=False, skip_group_check=True))
                    for j in range(2):
                        mms.append(lambda e_, j=j: e_.matmul(
                            pV[0:100, j * 512:(j + 1) * 512],
                            ONES_ROW_100, rows[0:1, j * 512:(j + 1) * 512],
                            start=False, stop=True, skip_group_check=True))
                    ctx.ops("tensor", mms, reads=[bf_("sb_tokfm"), bf_("sb_wtT"), CB, RW], writes=pV_b)
                    ctx.op("vector", lambda e_, e0=e: e_.tensor_copy(val_s[e0][0:100, :], pV[0:100, :]),
                           reads=pV_b, writes=[bf_(f"sb_val{e}")])
                    ctx.op("vector", lambda e_, e0=e: e_.memset(val_s[e0][0:1, :], 0.0),
                           writes=[bf_(f"sb_val{e}")])

        ctx.barrier_all()
        if upto in ("p2", "p3"):
         with ExitStack() as es2:
            qbuf = sb("qbuf", (128, 8 * 1024), BF16, es2)
            kbuf = [sb(f"kbuf{i}", (128, 8 * 1024), BF16, es2) for i in range(2)]
            vbuf = [sb(f"vbuf{i}", (128, 8 * 1025), BF16, es2) for i in range(2)]
            pTt = sb("pTt", (128, 8 * 1024), BF16, es2)
            outsb = sb("outsb", (128, 8 * 1025), F32, es2)
            xown = sb("xown", (128, 8 * 1024), F32, es2)
            t2st = sb("t2st", (128, 4 * 1024), BF16, es2)
            st96b = sb("st96b", (128, 96), F32, es2)
            sq32b = sb("sq32b", (128, 32), F32, es2)
            agg8b = sb("agg8b", (128, 32), F32, es2)
            rstd8b = sb("rstd8b", (128, 8), F32, es2)
            mcol8b = sb("mcol8b", (128, 8), F32, es2)
            rec = sb("rec", (128, 1), F32, es2)
            rp1b = sb("rp1b", (128, 8), F32, es2)

            for i in range(2):
                ctx.ops("vector",
                        [(lambda e_, i=i, t=t: e_.memset(vbuf[i][:, t * 1025 + 1024: t * 1025 + 1025], 1.0))
                         for t in range(8)],
                        writes=[bf_(f"sb_vbuf{i}_ones")])

            S_HALF = [pslice(0, 512), pslice(512, 1024)]
            PV = [pslice(1024, 2560), pslice(2560, 4096)]

            for qb in range(QB):
                qbB = bf_("sb_qbuf")
                ctx.dma("sync", "qS", qbuf[:].rearrange("p (a d) -> p a d", d=1024),
                        dt["q_sp"][qb].rearrange("a p d -> p a d"),
                        reads=[bf_(f"dr_q_sp{qb}")], writes=[qbB])
                for s in range(NSLAB):
                    if s == 2:
                        ctx.dma("sync", "qS", xown[:].rearrange("p (a d) -> p a d", d=1024),
                                x3d[qb * 8:(qb + 1) * 8].rearrange("a p d -> p a d"),
                                reads=[bf_("dr_x_seq")], writes=[bf_("sb_xown")])
                    par = s % 2
                    kB, vB = bf_(f"sb_kbuf{par}"), bf_(f"sb_vbuf{par}")
                    vOnes = bf_(f"sb_vbuf{par}_ones")
                    ctx.dma("sync", "qS", kbuf[par][:].rearrange("p (a d) -> p a d", d=1024),
                            dt["k_sp"][s].rearrange("a p d -> p a d"),
                            reads=[bf_(f"dr_k_sp{s}")], writes=[kB])
                    ctx.dma("sync", "qS",
                            vbuf[par][:].rearrange("p (a d) -> p a d", d=1025)[:, :, 0:1024],
                            dt["v_sp"][s].rearrange("a p d -> p a d"),
                            reads=[bf_(f"dr_v_sp{s}")], writes=[vB])
                    for kc in range(8):
                        for qh in range(2):
                            sp, sp_b = S_HALF[qh]
                            mms = []
                            for dc in range(8):
                                mms.append(lambda e_, dc=dc, qh=qh, kc=kc, par=par, sp=sp: e_.matmul(
                                    sp[:],
                                    kbuf[par][:, dc * 1024 + kc * 128: dc * 1024 + (kc + 1) * 128],
                                    qbuf[:, dc * 1024 + qh * 512: dc * 1024 + (qh + 1) * 512],
                                    start=(dc == 0), stop=(dc == 7), skip_group_check=True))
                            ctx.ops("tensor", mms, reads=[kB, qbB], writes=sp_b)
                            ctx.op("scalar", lambda e_, kc=kc, qh=qh, sp=sp: e_.activation(
                                pTt[:, kc * 1024 + qh * 512: kc * 1024 + (qh + 1) * 512], sp[:],
                                AF.Exp, bias=BIAS_NC, scale=1.0 / 32.0),
                                reads=sp_b + [CF], writes=[bf_(f"sb_pT{kc}")])
                    for j in range(8):
                        pv, pv_b = PV[j % 2]
                        mms = []
                        for kc in range(8):
                            for c0, c1 in ((0, 512), (512, 1024), (1024, 1025)):
                                mms.append(lambda e_, kc=kc, c0=c0, c1=c1, j=j, par=par: e_.matmul(
                                    pv[:, c0:c1],
                                    pTt[:, kc * 1024 + j * 128: kc * 1024 + (j + 1) * 128],
                                    vbuf[par][:, kc * 1025 + c0: kc * 1025 + c1],
                                    start=(kc == 0), stop=(kc == 7), skip_group_check=True))
                        ctx.ops("tensor", mms,
                                reads=[bf_(f"sb_pT{kc}") for kc in range(8)] + [vB, vOnes],
                                writes=pv_b)
                        if s == 0:
                            ctx.op("vector", lambda e_, j=j, pv=pv: e_.tensor_copy(
                                outsb[:, j * 1025:(j + 1) * 1025], pv[:, 0:1025]),
                                reads=pv_b, writes=[bf_(f"sb_outsb{j}")])
                        else:
                            ctx.op("vector", lambda e_, j=j, pv=pv: e_.tensor_tensor(
                                outsb[:, j * 1025:(j + 1) * 1025],
                                outsb[:, j * 1025:(j + 1) * 1025], pv[:, 0:1025], ALU.add),
                                reads=pv_b + [bf_(f"sb_outsb{j}")], writes=[bf_(f"sb_outsb{j}")])
                        if s == NSLAB - 1:
                            oj = bf_(f"sb_outsb{j}")
                            xt = xown[:, j * 1024:(j + 1) * 1024]
                            ctx.op("vector", lambda e_, j=j: e_.reciprocal(
                                rec[:], outsb[:, j * 1025 + 1024: j * 1025 + 1025]),
                                reads=[oj], writes=[bf_("b_rec")])
                            ctx.op("vector", lambda e_, j=j: e_.scalar_tensor_tensor(
                                xown[:, j * 1024:(j + 1) * 1024],
                                outsb[:, j * 1025: j * 1025 + 1024],
                                rec[:], xown[:, j * 1024:(j + 1) * 1024], ALU.mult, ALU.add),
                                reads=[oj, bf_("b_rec"), bf_("sb_xown")], writes=[bf_("sb_xown")])
                            ctx.op("vector", lambda e_, xt=xt, j=j: e_.bn_stats(
                                st96b[:, j * 12: j * 12 + 6], xt[:, 0:512]),
                                reads=[bf_("sb_xown")], writes=[bf_("ln2_st96")])
                            ctx.op("vector", lambda e_, xt=xt, j=j: e_.bn_stats(
                                st96b[:, j * 12 + 6: j * 12 + 12], xt[:, 512:1024]),
                                reads=[bf_("sb_xown")], writes=[bf_("ln2_st96")])
                bRC, bMC = emit_ln8(ctx, bf_, [None] * 8, st96b, sq32b, agg8b, rstd8b, mcol8b,
                                    cf32, CF, "ln2_")
                ctx.op("vector", lambda e_: e_.tensor_scalar(rp1b[:], rstd8b[:], 1.0, None, ALU.add),
                       reads=[bRC], writes=[bf_("b_rp1b")])
                for j in range(8):
                    xt = xown[:, j * 1024:(j + 1) * 1024]
                    ctx.op("scalar", lambda e_, xt=xt, j=j: e_.activation(
                        t2st[:, (j % 4) * 1024:(j % 4 + 1) * 1024], xt, AF.Identity,
                        bias=mcol8b[:, j:j + 1], scale=rstd8b[:, j:j + 1]),
                        reads=[bf_("sb_xown"), bMC, bRC], writes=[bf_("sb_t2st")])
                    ctx.op("vector", lambda e_, xt=xt, j=j: e_.tensor_scalar(
                        xt, xt, rp1b[:, j:j + 1], mcol8b[:, j:j + 1], ALU.mult, op1=ALU.add),
                        reads=[bf_("sb_xown"), bMC, bf_("b_rp1b")], writes=[bf_("sb_xown")])
                    if j % 4 == 3:
                        t23d = split_pa(t2_sp)
                        ctx.dma("gpsimd", "qG",
                                t23d[qb * 8 + (j - 3): qb * 8 + j + 1].rearrange("a p d -> p a d"),
                                t2st[:].rearrange("p (a d) -> p a d", d=1024),
                                reads=[bf_("sb_t2st")], writes=[bf_(f"dr_t2_sp{qb}")])
                x13d = split_pa(x1_sp)
                ctx.dma("gpsimd", "qG", x13d[qb * 8:(qb + 1) * 8].rearrange("a p d -> p a d"),
                        xown[:].rearrange("p (a d) -> p a d", d=1024),
                        reads=[bf_("sb_xown")], writes=[bf_(f"dr_x1_sp{qb}")])

        ctx.barrier_all()
        if upto == "p3":
         with ExitStack() as es3:
            wdT = sb("wdT", (128, 8 * 1024), BF16, es3)
            t2T = [sb(f"t2T{i}", (128, 8 * 512), BF16, es3) for i in range(3)]
            x1b = [sb(f"x1b{i}", (128, 4 * 1024), F32, es3) for i in range(3)]
            pf0 = sb("pf0", (128, 6 * 512), BF16, es3)
            pfs = [sb(f"pfs{i}", (128, 6 * 512), BF16, es3) for i in range(2)]
            yT = [sb(f"yT{i}", (128, 8 * 512), BF16, es3) for i in range(2)]
            outst = sb("outst", (128, 4 * 1024), F32, es3)
            lnS = sb("lnS", (6, 512), F32, es3)
            maskT = sb("maskT", (6, 512), F32, es3)
            lnG = sb("lnG", (6, 512), F32, es3)
            mx1 = sb("mx1", (128, 1), F32, es3)
            masktok = sb("masktok", (128, 6), F32, es3)

            for dc in range(8):
                ctx.dma("scalar", "qA", wdT[:, dc * 1024:(dc + 1) * 1024],
                        dt["wd_bf"][0:1024, dc * 128:(dc + 1) * 128],
                        reads=[bf_("dr_wd_bf")], writes=[bf_("sb_wdT")], transpose=True)

            HB = [pslice(3072, 3584), pslice(3584, 4096)]

            def p3_loads(nb):
                par = nb % 3
                qb = nb // 2
                t2Tb = bf_(f"sb_t2T{par}")
                x1B = bf_(f"sb_x1b{par}")
                for cc in range(8):
                    ctx.dma("scalar", "qA", t2T[par][:, cc * 512:(cc + 1) * 512],
                            dt["t2_sp"][nb * 512:(nb + 1) * 512, cc * 128:(cc + 1) * 128],
                            reads=[bf_(f"dr_t2_sp{qb}")], writes=[t2Tb], transpose=True)
                x13d = split_pa(x1_sp)
                ctx.dma("sync", "qS", x1b[par][:].rearrange("p (a d) -> p a d", d=1024),
                        x13d[nb * 4:(nb + 1) * 4].rearrange("a p d -> p a d"),
                        reads=[bf_(f"dr_x1_sp{qb}")], writes=[x1B])

            def p3_z_gate(nb):
                par = nb % 3
                t2Tb = bf_(f"sb_t2T{par}")
                pfsb = bf_(f"sb_pfs{nb % 2}")
                zslots = []
                for e in range(6):
                    zp = psum[:, e * 512:(e + 1) * 512]
                    zslots.append(zp)
                    mms = []
                    for cc in range(8):
                        mms.append(lambda e_, cc=cc, e0=e, zp=zp, par=par: e_.matmul(
                            zp[0:100, :],
                            tokfm[:, cc * 608 + e0 * 100: cc * 608 + (e0 + 1) * 100],
                            t2T[par][:, cc * 512:(cc + 1) * 512],
                            start=(cc == 0), stop=False, skip_group_check=True))
                    ctx.ops("tensor", mms, reads=[bf_("sb_tokfm"), t2Tb], writes=[pbank[e]])
                LG, LG_B = HB[0]
                MT, MT_B = HB[1]
                for nch in range(4):
                    mms = []
                    for cc in range(8):
                        mms.append(lambda e_, cc=cc, nch=nch, par=par: e_.matmul(
                            LG[:, 0:6],
                            t2T[par][:, cc * 512 + nch * 128: cc * 512 + (nch + 1) * 128],
                            wg_fm[:, cc * 6:(cc + 1) * 6],
                            start=(cc == 0), stop=(cc == 7), skip_group_check=True))
                    ctx.ops("tensor", mms, reads=[t2Tb, bf_("sb_wg_fm")], writes=LG_B)
                    ctx.op("vector", lambda e_: e_.tensor_reduce(
                        mx1[:], LG[:, 0:6], axis=AX.X, op=ALU.max),
                        reads=LG_B, writes=[bf_("b_mx1")])
                    ctx.op("vector", lambda e_: e_.tensor_scalar(
                        masktok[:], LG[:, 0:6], mx1[:], None, ALU.is_ge),
                        reads=LG_B + [bf_("b_mx1")], writes=[bf_("b_masktok")])
                    ctx.op("tensor", lambda e_: e_.transpose(
                        MT[0:6, 0:128], masktok[:], IDENT_F32[:, :]),
                        reads=[bf_("b_masktok"), CF], writes=MT_B)
                    ctx.op("vector", lambda e_, nch=nch: e_.tensor_copy(
                        maskT[:, nch * 128:(nch + 1) * 128], MT[0:6, 0:128]),
                        reads=MT_B, writes=[bf_("sb_maskT")])
                for e in range(6):
                    ctx.op("scalar", lambda e_, e0=e: e_.activation(
                        pf0[0:100, e0 * 512:(e0 + 1) * 512], zslots[e0][0:100, :], AF.Exp,
                        bias=cf32[0:100, 2:3], scale=1.0 / 32.0),
                        reads=[pbank[e], CF], writes=[bf_("sb_pf0")])
                return zslots

            def p3_gate_close(nb, zslots):
                pfsb = bf_(f"sb_pfs{nb % 2}")
                LG, LG_B = HB[0]
                MT, MT_B = HB[1]
                mms = []
                for e in range(6):
                    mms.append(lambda e_, e0=e: e_.matmul(
                        MT[0:6, 0:512], eind[:, e0 * 6:(e0 + 1) * 6],
                        pf0[0:100, e0 * 512:(e0 + 1) * 512],
                        start=(e0 == 0), stop=(e0 == 5), skip_group_check=True))
                ctx.ops("tensor", mms, reads=[bf_("sb_pf0"), EI], writes=MT_B)
                ctx.op("scalar", lambda e_: e_.activation(lnS[:], MT[0:6, 0:512], AF.Ln,
                                                          bias=cf32[0:6, 2:3], scale=1.0),
                       reads=MT_B + [CF], writes=[bf_("b_lnS")])
                ctx.op("vector", lambda e_: e_.tensor_scalar(lnS[:], lnS[:], -32.0, 2560.0,
                                                             ALU.mult, op1=ALU.add),
                       reads=[bf_("b_lnS")], writes=[bf_("b_lnS")])
                ctx.op("vector", lambda e_: e_.tensor_tensor(lnG[:], lnS[:], maskT[:], ALU.mult),
                       reads=[bf_("b_lnS"), bf_("sb_maskT")], writes=[bf_("b_lnG")])
                ctx.op("vector", lambda e_: e_.tensor_scalar(lnG[:], lnG[:], -2560.0, None, ALU.add),
                       reads=[bf_("b_lnG")], writes=[bf_("b_lnG")])
                for e in range(6):
                    ctx.op("tensor", lambda e_, e0=e: e_.matmul(
                        zslots[e0][0:100, :], selm[:, e0 * 100:(e0 + 1) * 100], lnG[:],
                        start=False, stop=True, skip_group_check=True),
                        reads=[bf_("b_lnG"), SM_], writes=[pbank[e]])
                for e in range(6):
                    ctx.op("scalar", lambda e_, e0=e, pfsb=pfsb: e_.activation(
                        pfs[nb % 2][0:100, e0 * 512:(e0 + 1) * 512], zslots[e0][0:100, :], AF.Exp,
                        bias=cf32[0:100, 2:3], scale=1.0 / 32.0),
                        reads=[pbank[e], CF], writes=[pfsb])

            def p3_h(nb):
                par = nb % 3
                t2Tb = bf_(f"sb_t2T{par}")
                pfsb = bf_(f"sb_pfs{nb % 2}")
                yTb = bf_(f"sb_yT{nb % 2}")
                for dc in range(8):
                    hp, hp_b = HB[dc % 2]
                    mms = []
                    for e in range(6):
                        mms.append(lambda e_, e0=e, dc=dc, hp=hp, nb=nb: e_.matmul(
                            hp[:], val_s[e0][0:100, dc * 128:(dc + 1) * 128],
                            pfs[nb % 2][0:100, e0 * 512:(e0 + 1) * 512],
                            start=(e0 == 0), stop=(e0 == 5), skip_group_check=True))
                    ctx.ops("tensor", mms,
                            reads=[pfsb] + [bf_(f"sb_val{e}") for e in range(6)],
                            writes=hp_b)
                    ctx.op("vector", lambda e_, dc=dc, hp=hp, par=par, nb=nb: e_.tensor_tensor(
                        yT[nb % 2][:, dc * 512:(dc + 1) * 512], hp[:],
                        t2T[par][:, dc * 512:(dc + 1) * 512], ALU.add),
                        reads=hp_b + [t2Tb], writes=[yTb])

            def p3_delta(nb):
                par = nb % 3
                x1B = bf_(f"sb_x1b{par}")
                yTb = bf_(f"sb_yT{nb % 2}")
                for ncs in range(4):
                    for ch in range(2):
                        dp, dp_b = HB[(2 * ncs + ch) % 2]
                        mms = []
                        for dc in range(8):
                            mms.append(lambda e_, dc=dc, ncs=ncs, ch=ch, dp=dp, nb=nb: e_.matmul(
                                dp[:],
                                yT[nb % 2][:, dc * 512 + ncs * 128: dc * 512 + (ncs + 1) * 128],
                                wdT[:, dc * 1024 + ch * 512: dc * 1024 + (ch + 1) * 512],
                                start=(dc == 0), stop=False, skip_group_check=True))
                        mms.append(lambda e_, ch=ch, dp=dp: e_.matmul(
                            dp[:],
                            ONES_ROW_128, rows[0:1, 1024 + ch * 512: 1024 + (ch + 1) * 512],
                            start=False, stop=True, skip_group_check=True))
                        ctx.ops("tensor", mms, reads=[yTb, bf_("sb_wdT"), CB, RW], writes=dp_b)
                        c0 = ncs * 1024 + ch * 512
                        ctx.op("scalar", lambda e_, c0=c0, dp=dp: e_.activation(
                            outst[:, c0:c0 + 512], dp[:], AF.Exp,
                            bias=BIAS_Z, scale=SCALE_COL),
                            reads=dp_b + [CF], writes=[bf_("sb_outst")])
                        ctx.op("scalar", lambda e_, c0=c0: e_.activation(
                            outst[:, c0:c0 + 512], outst[:, c0:c0 + 512], AF.Ln,
                            bias=BIAS_Z, scale=1.0),
                            reads=[bf_("sb_outst"), CF], writes=[bf_("sb_outst")])
                    ctx.op("vector", lambda e_, ncs=ncs, par=par: e_.tensor_tensor(
                        outst[:, ncs * 1024:(ncs + 1) * 1024], outst[:, ncs * 1024:(ncs + 1) * 1024],
                        x1b[par][:, ncs * 1024:(ncs + 1) * 1024], ALU.add),
                        reads=[bf_("sb_outst"), x1B], writes=[bf_("sb_outst")])
                o3d = split_pa(out_d)
                ctx.dma("gpsimd", "qG", o3d[nb * 4:(nb + 1) * 4].rearrange("a p d -> p a d"),
                        outst[:].rearrange("p (a d) -> p a d", d=1024),
                        reads=[bf_("sb_outst")], writes=[bf_("dr_out")])

            p3_loads(0)
            for nb in range(NB3 + 1):
                zs = None
                if nb < NB3:
                    if nb + 1 < NB3:
                        p3_loads(nb + 1)
                    zs = p3_z_gate(nb)
                if nb >= 1:
                    p3_h(nb - 1)
                if nb < NB3:
                    p3_gate_close(nb, zs)
                if nb >= 1:
                    p3_delta(nb - 1)

        ctx.barrier_all()
        for name in taps:
            ctx.dma("gpsimd", "qG", tap_d[name], dt[name],
                    reads=[b for n, b in bufs.items() if n.startswith("dr_" + name)])
        ctx.barrier_queue("sync", "qG")
        ctx.barrier_queue("sync", "qA")
        ctx.barrier_queue("sync", "qS")

    return nc


def kernel(**inputs):
    x = np.asarray(inputs["L0_attn_t"])
    W = np.ascontiguousarray(inputs["L0_attn_W_qkv"])
    b_qkv = np.asarray(inputs["L0_attn_b_qkv"])
    A = np.ascontiguousarray(inputs["L0_ffn_A"])
    Bm = np.ascontiguousarray(inputs["L0_ffn_B"])
    wg = np.ascontiguousarray(inputs["L0_ffn_w_gate"])
    wtw = np.ascontiguousarray(inputs["L0_ffn_wt_w"])
    wtb = np.asarray(inputs["L0_ffn_wt_b"])
    wdw = np.ascontiguousarray(inputs["L0_ffn_wd_w"])
    wdb = np.asarray(inputs["L0_ffn_wd_b"])
    scale = np.asarray(inputs["L0_ffn_scale"])

    cbf, selm, eind = _host_consts()
    bcol = np.ascontiguousarray(b_qkv[0:2048].reshape(16, 128).T)
    rows_bf = np.concatenate([wtb, wdb, b_qkv[2048:3072]])[None, :].astype(bf16)
    cf32 = np.zeros((128, 132), np.float32)
    cf32[:, 0] = -C_ATTN
    cf32[:, 1] = 1e-5
    cf32[:, 2] = 0.0
    cf32[:, 3] = scale[0]
    cf32[:, 4:132] = np.eye(128, dtype=np.float32)

    nc = build()
    in_maps = []
    for c in range(8):
        b, h = divmod(c, 2)
        xs = np.ascontiguousarray(np.roll(x[b], -h * HALF, axis=0)) if h else np.ascontiguousarray(x[b])
        in_maps.append({
            "x_seq": xs, "w_qkv": W, "bcol": bcol, "a_in": A, "b_in": Bm,
            "wgate": wg, "wt_w": wtw, "wd_w": wdw, "rows_bf": rows_bf,
            "cbf": cbf, "selm": selm, "eind": eind, "cf32": cf32,
        })
    res = run_bass_kernel_spmd(nc, in_maps, core_ids=list(range(8)))
    out = np.empty((B, N, D), np.float32)
    for c in range(8):
        b, h = divmod(c, 2)
        out[b, h * HALF:(h + 1) * HALF] = res.results[c]["out"]
    return out
